# version 03
# baseline (speedup 1.0000x reference)
import math
import numpy as np
import ml_dtypes
import concourse.bass as bass
import concourse.mybir as mybir
from concourse.bass_utils import run_bass_kernel_spmd

F32 = mybir.dt.float32
BF16 = mybir.dt.bfloat16
ALU = mybir.AluOpType
AF = mybir.ActivationFunctionType
AX = mybir.AxisListType

NCORE = 8
TC = 2048
D = 1024
NCH = D // 128

import os as _os
_STOP = _os.environ.get("K_STOP") or None
_DBG = False

CONST_VALS = [1e-5, -0.5, 0.5, 1e-6, 2.0, -1.0, -15.0]


class Ctx:

    def __init__(self, nc):
        self.nc = nc
        self.c = {}
        self.sem = {}
        for name in ["T", "V", "S", "G", "Y", "DY", "DS", "DG", "CC"]:
            self.sem[name] = nc.alloc_semaphore(name)
            self.c[name] = 0

    def bump(self, name, n=1):
        self.c[name] += n
        return self.c[name]

    def w(self, eng, name, val=None):
        eng.wait_ge(self.sem[name], self.c[name] if val is None else val)

    def inc(self, instr, name, n=1):
        instr.then_inc(self.sem[name], n)
        self.bump(name, n)
        return instr

    def vb(self, ve, ins):
        self.inc(ins, "V")
        self.w(ve, "V")


def ln_emit(k, blk, XT, XN, PB, AFA, tag):
    nc = k.nc
    ONES = k.ONESbf
    ONESf = k.ONESf
    for tc in range(4):
        t0 = tc * 512

        @blk.scalar
        def _(sc, tc=tc, t0=t0):
            k.w(sc, "V")
            k.w(sc, "T")
            for fc in range(8):
                ins = sc.activation(out=AFA[:, fc * 512:(fc + 1) * 512],
                                    in_=XT[:, fc * 2048 + t0: fc * 2048 + t0 + 512],
                                    func=AF.Square)
                if fc == 7:
                    k.inc(ins, "S")

        @blk.tensor
        def _(te, tc=tc, t0=t0):
            k.w(te, "S")
            for fc in range(8):
                te.matmul(PB[0][:, :], ONESf[:, :],
                          XT[:, fc * 2048 + t0: fc * 2048 + t0 + 512],
                          start=(fc == 0), stop=(fc == 7))
            for fc in range(8):
                ins = te.matmul(PB[1][:, :], ONESf[:, :],
                                AFA[:, fc * 512:(fc + 1) * 512],
                                start=(fc == 0), stop=(fc == 7))
            k.inc(ins, "T")

        @blk.vector
        def _(ve, tc=tc, t0=t0):
            k.w(ve, "T")
            MB = AFA[:, 4096:4608]
            VR = AFA[:, 4608:5120]
            ve.tensor_scalar(MB, PB[0][:, :], 1.0 / 1024.0, None, ALU.mult)
            ve.tensor_tensor(VR, MB, MB, ALU.mult)
            ins = ve.scalar_tensor_tensor(VR, PB[1][:, :], 1.0 / 1024.0, VR,
                                          ALU.mult, ALU.subtract)
            k.inc(ins, "V")

        @blk.scalar
        def _(sc, tc=tc, t0=t0):
            k.w(sc, "V")
            VR = AFA[:, 4608:5120]
            sc.activation(out=VR, in_=VR, func=AF.Ln, bias=k.cc(1e-5), scale=1.0)
            ins = sc.activation(out=VR, in_=VR, func=AF.Exp, scale=k.cc(-0.5))
            k.inc(ins, "S")

        @blk.vector
        def _(ve, tc=tc, t0=t0):
            k.w(ve, "S")
            MB = AFA[:, 4096:4608]
            VR = AFA[:, 4608:5120]
            TMP = AFA[:, 5120:5632]
            for fc in range(8):
                sl = slice(fc * 2048 + t0, fc * 2048 + t0 + 512)
                ve.tensor_tensor(TMP, XT[:, sl], MB, ALU.subtract)
                ins = ve.tensor_tensor(XN[:, sl], TMP, VR, ALU.mult)
            k.inc(ins, "V")





def sort_network_ops():
    ops = []

    def cmp(pb, pc, a_off, b_off, dims, a_rev=False, b_rev=False):
        ops.append(("cmp", dict(pb=pb, pc=pc, a_off=a_off, b_off=b_off,
                                dims=tuple(dims), a_rev=a_rev, b_rev=b_rev)))

    def cmp2(a_pb, b_pb, pc, b_rev=False):
        ops.append(("cmp2", dict(a_pb=a_pb, b_pb=b_pb, pc=pc, a_rev=False,
                                 b_rev=b_rev)))

    def apply_perm_groups(gsrc):
        lane_map = [0] * 128
        for g in range(16):
            for c in range(8):
                lane_map[g * 8 + c] = gsrc[g] * 8 + c
        ops.append(("perm", tuple(lane_map)))

    n = 1024
    for kk in [2 ** i for i in range(10)]:
        cmp(0, 128, 0, kk, [(2 * kk, n // (2 * kk)), (1, kk)], b_rev=True)
        d = kk // 2
        while d >= 1:
            cmp(0, 128, 0, d, [(2 * d, n // (2 * d)), (1, d)])
            d //= 2
    apply_perm_groups([2 * m for m in range(8)] + [2 * m + 1 for m in range(8)])
    cmp2(0, 64, 64, b_rev=True)
    for d in [512, 256, 128, 64, 32, 16, 8, 4, 2, 1]:
        cmp(0, 128, 0, d, [(2 * d, 1024 // (2 * d)), (1, d)])
    gsrc = [0] * 16
    for m in range(4):
        for u in range(2):
            gsrc[u * 4 + m] = u * 8 + 2 * m
            gsrc[8 + u * 4 + m] = (1 - u) * 8 + 2 * m + 1
    apply_perm_groups(gsrc)
    cmp2(0, 64, 64, b_rev=True)
    cmp2(0, 64, 64)
    cmp2(0, 32, 32)
    cmp2(64, 96, 32)
    for d in [512, 256, 128, 64, 32, 16, 8, 4, 2, 1]:
        cmp(0, 128, 0, d, [(2 * d, 1024 // (2 * d)), (1, d)])
    gsrc = [0] * 16
    for a_i, r in ((0, 0), (1, 2)):
        for q in range(4):
            gsrc[q * 2 + a_i] = (q % 2) * 4 + r + (q // 2) * 8
    for a_i, r in ((0, 1), (1, 3)):
        for w in range(4):
            gsrc[8 + (3 - w) * 2 + a_i] = (w % 2) * 4 + r + (w // 2) * 8
    apply_perm_groups(gsrc)
    cmp2(0, 64, 64, b_rev=True)
    cmp2(0, 64, 64)
    cmp2(0, 32, 32)
    cmp2(64, 96, 32)
    gsrc = [0] * 16
    for a_i in range(2):
        for q in range(4):
            gsrc[(q % 2) * 4 + (q // 2) * 2 + a_i] = q * 2 + a_i
        for j in range(4):
            gsrc[8 + (j % 2) * 4 + (j // 2) * 2 + a_i] = 8 + j * 2 + a_i
    apply_perm_groups(gsrc)
    cmp2(0, 32, 32)
    cmp2(64, 96, 32)
    for d in [512, 256, 128, 64, 32, 16, 8, 4, 2, 1]:
        cmp(0, 128, 0, d, [(2 * d, 1024 // (2 * d)), (1, d)])
    gsrc = [0] * 16
    for q in range(8):
        g = ((q % 2) * 4 + ((q // 2) % 2) * 2 + 0) + (8 if q >= 4 else 0)
        gsrc[q] = g
    for w in range(8):
        g = ((w % 2) * 4 + ((w // 2) % 2) * 2 + 1) + (8 if w >= 4 else 0)
        gsrc[8 + (7 - w)] = g
    apply_perm_groups(gsrc)
    cmp2(0, 64, 64, b_rev=True)
    cmp2(0, 64, 64)
    cmp2(0, 32, 32)
    cmp2(64, 96, 32)
    gsrc = [0] * 16
    for q in range(16):
        gsrc[(q % 4) * 4 + q // 4] = q
    apply_perm_groups(gsrc)
    cmp2(0, 64, 64)
    cmp2(0, 32, 32)
    cmp2(64, 96, 32)
    for d in [512, 256, 128, 64, 32, 16, 8, 4, 2, 1]:
        cmp(0, 128, 0, d, [(2 * d, 1024 // (2 * d)), (1, d)])
    fmap = [(q % 4) * 4 + q // 4 for q in range(16)]
    return ops, fmap


def emit_sort(k, blk, SA, SB_, SCR):
    ops, fmap = sort_network_ops()
    cur, oth = SA, SB_

    def vsec(f):
        blk.vector(f)

    def ysec(f):
        blk.sync(f)

    i = 0
    while i < len(ops):
        op = ops[i]
        if op[0] == "perm":
            lane_map = list(op[1])
            _c, _o = cur, oth

            def yf(sy, lane_map=lane_map, _c=_c, _o=_o):
                k.w(sy, "V")
                for g in range(16):
                    s0 = lane_map[g * 8]
                    ins = sy.dma_start(out=_o[g * 8:(g + 1) * 8, :],
                                       in_=_c[s0:s0 + 8, :])
                    k.bump("DY", 16)
                    ins.then_inc(k.sem["DY"], 16)
            ysec(yf)

            def vf(ve):
                k.w(ve, "DY")
                ins = ve.nop()
                k.inc(ins, "V")
            vsec(vf)
            cur, oth = oth, cur
            i += 1
            continue
        if op[0] == "cmp":
            d = op[1]
            _c, _o = cur, oth

            def vf(ve, d=d, _c=_c, _o=_o):
                (so, co), (si, ci) = d["dims"]
                va = _c.rearrange("p (b x) -> p b x", x=so)
                vo = _o.rearrange("p (b x) -> p b x", x=so)

                def sl(v, off, rev, ci=ci):
                    if rev:
                        if off == 0:
                            return v[:, :, ci - 1::-1]
                        return v[:, :, off + ci - 1:off - 1:-1]
                    return v[:, :, off:off + ci]
                a_off, b_off, brev = d["a_off"], d["b_off"], d["b_rev"]
                ve.tensor_tensor(sl(vo, a_off, False), sl(va, a_off, False),
                                 sl(va, b_off, brev), ALU.min)
                ins = ve.tensor_tensor(sl(vo, b_off, False), sl(va, a_off, brev),
                                       sl(va, b_off, False), ALU.max)
                k.inc(ins, "V")
            vsec(vf)
            cur, oth = oth, cur
            i += 1
            continue
        d = op[1]
        group = [d]
        if d["pc"] == 32:
            d2 = ops[i + 1][1]
            assert ops[i + 1][0] == "cmp2" and d2["pc"] == 32
            group.append(d2)
            i += 2
        else:
            i += 1
        _c, _o = cur, oth

        def vf(ve, group=group, _c=_c, _o=_o):
            for d in group:
                apb, bpb, pc, brev = d["a_pb"], d["b_pb"], d["pc"], d["b_rev"]
                scr = SCR[apb:apb + pc, :]
                if brev:
                    ve.tensor_copy(scr, _c[bpb:bpb + pc, 1023::-1])
                else:
                    ve.tensor_copy(scr, _c[bpb:bpb + pc, :])
                ve.tensor_tensor(_o[apb:apb + pc, :], _c[apb:apb + pc, :],
                                 scr, ALU.min)
                if brev:
                    ins = ve.tensor_tensor(_o[bpb:bpb + pc, :],
                                           _c[apb:apb + pc, 1023::-1],
                                           SCR[apb:apb + pc, 1023::-1], ALU.max)
                else:
                    ins = ve.tensor_tensor(_o[bpb:bpb + pc, :],
                                           _c[apb:apb + pc, :], scr, ALU.max)
            k.inc(ins, "V")
        vsec(vf)
        cur, oth = oth, cur
    return cur, fmap



def l0ffn_emit(k, blk, ci):
    nc = k.nc
    XT, XN, AB, AFA = k.XT, k.XN, k.AB, k.AFA
    PB = k.PB
    ln_emit(k, blk, XT, XN, PB, AFA, f"l0ffn{ci}_ln")

    STOKf = AFA[:, 2048:3072]
    for tt in range(16):
        def tf(te, tt=tt):
            k.w(te, "V")
            for fc in range(8):
                ins = te.matmul(PB[0][:, 0:64], XN[:, fc * 2048 + tt * 128: fc * 2048 + (tt + 1) * 128],
                                k.WR[:, fc * 64:(fc + 1) * 64],
                                start=(fc == 0), stop=(fc == 7))
            k.inc(ins, "T")
        blk.tensor(tf)

        def vf(ve, tt=tt):
            k.w(ve, "T")
            SM = AFA[:, 5120:5184]
            NM = AFA[:, 5184:5185]
            ve.tensor_tensor(SM, PB[0][:, 0:64], k.BRB[:, :], ALU.add)
            ins = ve.tensor_reduce(NM, SM, AX.X, ALU.max, negate=True)
            k.inc(ins, "V")
        blk.vector(vf)

        def sf(sc, tt=tt):
            k.w(sc, "V")
            SM = AFA[:, 5120:5184]
            SE = AFA[:, 5185:5249]
            NM = AFA[:, 5184:5185]
            SSUM = AFA[:, 5249:5250]
            ins = sc.activation(out=SE, in_=SM, func=AF.Exp, bias=NM,
                                scale=1.0)
            k.inc(ins, "S")
        blk.scalar(sf)

        def vf2(ve, tt=tt):
            k.w(ve, "S")
            SE = AFA[:, 5185:5249]
            SSUM = AFA[:, 5249:5250]
            RC = AFA[:, 5250:5251]
            k.vb(ve, ve.tensor_reduce(SSUM, SE, AX.X, ALU.add))
            k.vb(ve, ve.reciprocal(RC, SSUM))
            ins = ve.tensor_tensor(STOKf[:, tt * 64:(tt + 1) * 64], SE,
                                   RC.to_broadcast([128, 64]), ALU.mult)
            k.inc(ins, "V")
        blk.vector(vf2)

        def tf2(te, tt=tt):
            k.w(te, "V")
            k.w(te, "DY")
            ins = te.transpose(PB[1][0:64, 0:128],
                               STOKf[:, tt * 64:(tt + 1) * 64], k.IDfs[:, :])
            k.inc(ins, "T")
        blk.tensor(tf2)

        def vf3(ve, tt=tt):
            k.w(ve, "T")
            ins = ve.tensor_copy(AFA[0:64, 5251:5379], PB[1][0:64, 0:128])
            k.inc(ins, "V")
        blk.vector(vf3)

        def yf(sy, tt=tt):
            k.w(sy, "V")
            ins = sy.dma_start(out=k.sc_in.ap()[:, tt * 128:(tt + 1) * 128],
                               in_=AFA[0:64, 5251:5379])
            k.inc(ins, "DY", 16)
        blk.sync(yf)

    if _STOP == "router":
        return
    def gf(gp):
        k.w(gp, "DY")
        ins = gp.collective_compute(
            "AllGather", ALU.bypass, replica_groups=[list(range(8))],
            ins=[k.sc_in.ap().opt()], outs=[k.sc_out.ap().opt()])
        k.inc(ins, "CC")
    blk.gpsimd(gf)

    def yf2(sy):
        k.w(sy, "CC")
        k.w(sy, "V")
        for jr in range(2):
            src3 = k.sc_out.ap()[:, jr * 1024:(jr + 1) * 1024].rearrange(
                "(j2 e) c -> j2 e c", j2=8)[:, bass.ds(k.pid * 8, 8)]
            dst3 = k.sc_my.ap().rearrange("(j2 r) c -> j2 r c",
                                          r=16)[:, jr * 8:(jr + 1) * 8, :]
            ins = sy.dma_start(out=dst3, in_=src3)
            k.inc(ins, "DY", 16)
        sy.wait_ge(k.sem["DY"], k.c["DY"])
        ins = sy.dma_start(out=AFA[:, 0:1024], in_=k.sc_my.ap())
        k.inc(ins, "DY", 16)
    blk.sync(yf2)

    def vf4(ve):
        k.w(ve, "DY")
        ins = ve.nop()
        k.inc(ins, "V")
    blk.vector(vf4)

    SRT, fmap = emit_sort(k, blk, AFA[:, 0:1024], AFA[:, 1024:2048],
                          AFA[:, 5120:6144])

    if _STOP == "sort":
        return
    def vf5(ve):
        seg = SRT.rearrange("p (s x) -> p s x", x=64)
        ve.tensor_reduce(k.GSseg[:, :], seg, AX.X, ALU.add)
        ins = ve.tensor_reduce(k.GMseg[:, :], seg, AX.X, ALU.min)
        k.inc(ins, "V")
    blk.vector(vf5)

    def yf3(sy):
        k.w(sy, "V")
        for off, seg_t in ((0, k.GSseg), (256, k.GMseg)):
            for gh in range(4):
                o4 = k.gg_in.ap()[:, off:off + 256].rearrange(
                    "c (gl g2 s) -> gl g2 c s", gl=4, g2=4)[:, gh]
                ins = sy.dma_start(out=o4, in_=seg_t[gh * 32:(gh + 1) * 32, :])
                k.inc(ins, "DY", 16)
    blk.sync(yf3)

    def gf2(gp):
        k.w(gp, "DY")
        ins = gp.collective_compute(
            "AllGather", ALU.bypass, replica_groups=[list(range(8))],
            ins=[k.gg_in.ap().opt()], outs=[k.gg_out.ap().opt()])
        k.inc(ins, "CC")
    blk.gpsimd(gf2)

    def yf4(sy):
        k.w(sy, "CC")
        ins = sy.dma_start(out=k.GSA[:, :], in_=k.gg_out.ap()[:, 0:256])
        k.inc(ins, "DY", 16)
        ins = sy.dma_start(out=k.GMA[:, :], in_=k.gg_out.ap()[:, 256:512])
        k.inc(ins, "DY", 16)
    blk.sync(yf4)

    if _STOP == "gs":
        return
    RS = k.RS
    NR = 44

    def vf6(ve):
        k.w(ve, "DY")
        ve.memset(RS[:, 0:1], 1e-38)
        ins = ve.memset(RS[:, 1:2], 64.0)
        k.inc(ins, "V")
    blk.vector(vf6)

    for r in range(NR):
        def vf7(ve, r=r):
            k.w(ve, "T")
            k.vb(ve, ve.tensor_tensor(RS[:, 2:3], RS[:, 0:1],
                                       k.CKL[:, r:r + 1], ALU.mult))
            ve.tensor_tensor(AFA[0:64, 5120:5376], k.GSA[:, :],
                             RS[:, 2:3].to_broadcast([64, 256]), ALU.is_gt)
            ins = ve.tensor_reduce(RS[:, 16:17], AFA[0:64, 5120:5376],
                                   AX.X, ALU.add)
            k.inc(ins, "V")
        blk.vector(vf7)

        def tf3(te, r=r):
            k.w(te, "V")
            ins = te.matmul(PB[0][0:64, 0:1], k.ONESf[0:64, 0:64],
                            RS[:, 16:17], start=True, stop=True)
            k.inc(ins, "T")
        blk.tensor(tf3)

        def vf8(ve, r=r):
            k.w(ve, "T")
            k.vb(ve, ve.tensor_scalar(RS[:, 3:4], PB[0][0:64, 0:1], 4096.0,
                                       None, ALU.is_ge))
            k.vb(ve, ve.tensor_tensor(RS[:, 4:5], RS[:, 2:3], RS[:, 0:1], ALU.subtract))
            k.vb(ve, ve.tensor_tensor(RS[:, 4:5], RS[:, 4:5], RS[:, 3:4], ALU.mult))
            k.vb(ve, ve.tensor_tensor(RS[:, 0:1], RS[:, 0:1], RS[:, 4:5], ALU.add))
            k.vb(ve, ve.tensor_tensor(RS[:, 4:5], RS[:, 1:2], RS[:, 2:3], ALU.subtract))
            k.vb(ve, ve.tensor_tensor(RS[:, 4:5], RS[:, 4:5], RS[:, 3:4], ALU.mult))
            ins = ve.tensor_tensor(RS[:, 1:2], RS[:, 2:3], RS[:, 4:5], ALU.add)
            k.inc(ins, "V")
        blk.vector(vf8)

    def vf9(ve):
        k.w(ve, "V")
        TMP = AFA[0:64, 5120:5376]
        ve.tensor_tensor(TMP, k.GSA[:, :], RS[:, 1:2].to_broadcast([64, 256]),
                         ALU.is_le)
        ve.tensor_tensor(TMP, TMP, k.GSA[:, :], ALU.mult)
        ins = ve.tensor_reduce(RS[:, 15:16], TMP, AX.X, ALU.max)
        k.inc(ins, "V")
    blk.vector(vf9)

    def tf4(te):
        k.w(te, "V")
        ins = te.transpose(PB[0][0:1, 0:64], RS[:, 15:16], k.IDfs[0:64, 0:64])
        k.inc(ins, "T")
    blk.tensor(tf4)

    def vf10(ve):
        k.w(ve, "T")
        ins = ve.tensor_reduce(AFA[0:1, 5120:5121], PB[0][0:1, 0:64], AX.X, ALU.max)
        k.inc(ins, "V")
    blk.vector(vf10)

    def tf5(te):
        k.w(te, "V")
        ins = te.matmul(PB[1][0:64, 0:1], k.ONESf[0:1, 0:64],
                        AFA[0:1, 5120:5121], start=True, stop=True)
        k.inc(ins, "T")
    blk.tensor(tf5)

    def vf11(ve):
        k.w(ve, "T")
        TMP = AFA[0:64, 5120:5376]
        ins = ve.tensor_copy(RS[:, 5:6], PB[1][0:64, 0:1])
        k.inc(ins, "V")
        k.w(ve, "V")
        ve.tensor_tensor(TMP, k.GSA[:, :], RS[:, 5:6].to_broadcast([64, 256]),
                         ALU.is_gt)
        ve.tensor_reduce(RS[:, 6:7], TMP, AX.X, ALU.add)
        ve.tensor_tensor(TMP, k.GSA[:, :], RS[:, 5:6].to_broadcast([64, 256]),
                         ALU.is_equal)
        ins = ve.tensor_reduce(RS[:, 7:8], TMP, AX.X, ALU.add)
        k.inc(ins, "V")
    blk.vector(vf11)

    def tf6(te):
        k.w(te, "V")
        ins = te.matmul(PB[0][0:64, 0:1], k.ONESf[0:64, 0:64],
                        RS[:, 6:7], start=True, stop=True)
        k.inc(ins, "T")
    blk.tensor(tf6)

    def vf12(ve):
        k.w(ve, "T")
        k.w(ve, "V")
        k.vb(ve, ve.tensor_scalar(RS[:, 8:9], PB[0][0:64, 0:1], -1.0, 4096.0,
                                  ALU.mult, ALU.add))
        k.vb(ve, ve.tensor_scalar(RS[:, 9:10], RS[:, 6:7], 64.0, None, ALU.mult))
        k.vb(ve, ve.tensor_tensor(RS[:, 9:10], RS[:, 9:10], k.EIDX[:, :], ALU.add))
        ins = ve.tensor_scalar(RS[:, 10:11], RS[:, 7:8], 1.0, None, ALU.is_ge)
        k.inc(ins, "V")
    blk.vector(vf12)

    def tf7(te):
        k.w(te, "V")
        te.transpose(PB[0][0:1, 0:64], RS[:, 9:10], k.IDfs[0:64, 0:64])
        ins = te.transpose(PB[0][0:1, 64:128], RS[:, 10:11], k.IDfs[0:64, 0:64])
        k.inc(ins, "T")
    blk.tensor(tf7)

    def vf13(ve):
        k.w(ve, "T")
        ins = ve.tensor_copy(AFA[0:1, 5120:5248], PB[0][0:1, 0:128])
        k.inc(ins, "V")
    blk.vector(vf13)

    def tf8(te):
        k.w(te, "V")
        ins = te.matmul(PB[1][0:64, 0:128], k.ONESf[0:1, 0:64],
                        AFA[0:1, 5120:5248], start=True, stop=True)
        k.inc(ins, "T")
    blk.tensor(tf8)

    def vf14(ve):
        k.w(ve, "T")
        k.w(ve, "V")
        CMP = AFA[0:64, 5376:5440]
        ve.tensor_tensor(CMP, PB[1][0:64, 0:64], RS[:, 9:10].to_broadcast([64, 64]),
                         ALU.is_lt)
        ve.tensor_tensor(CMP, CMP, PB[1][0:64, 64:128], ALU.mult)
        k.vb(ve, ve.tensor_reduce(RS[:, 11:12], CMP, AX.X, ALU.add))
        k.vb(ve, ve.tensor_tensor(RS[:, 12:13], RS[:, 11:12], RS[:, 8:9], ALU.is_lt))
        k.vb(ve, ve.tensor_tensor(RS[:, 12:13], RS[:, 12:13], RS[:, 10:11], ALU.mult))
        k.vb(ve, ve.tensor_tensor(RS[:, 13:14], RS[:, 6:7], RS[:, 12:13], ALU.add))
        k.vb(ve, ve.tensor_scalar(RS[:, 14:15], RS[:, 13:14], -1.0, 256.0,
                                  ALU.mult, ALU.add))
        MSK = AFA[0:64, 5376:5632]
        ve.tensor_tensor(MSK, k.IOTA256[:, :], RS[:, 14:15].to_broadcast([64, 256]),
                         ALU.is_lt)
        ve.tensor_scalar(MSK, MSK, 1e30, None, ALU.mult)
        ve.tensor_tensor(MSK, MSK, k.GMA[:, :], ALU.add)
        ins = ve.tensor_reduce(RS[:, 5:6], MSK, AX.X, ALU.min)
        k.inc(ins, "V")
    blk.vector(vf14)

    def tf9(te):
        k.w(te, "V")
        ins = te.transpose(PB[0][0:1, 0:64], RS[:, 5:6], k.IDfs[0:64, 0:64])
        k.inc(ins, "T")
    blk.tensor(tf9)

    def vf15(ve):
        k.w(ve, "T")
        ins = ve.tensor_copy(AFA[0:1, 5120:5184], PB[0][0:1, 0:64])
        k.inc(ins, "V")
    blk.vector(vf15)

    def tf10(te):
        k.w(te, "V")
        ins = te.matmul(PB[1][:, 0:64], k.ONESf[0:1, :],
                        AFA[0:1, 5120:5184], start=True, stop=True)
        k.inc(ins, "T")
    blk.tensor(tf10)

    def vf16(ve):
        k.w(ve, "T")
        ins = ve.tensor_copy(k.THB[:, :], PB[1][:, 0:64])
        k.inc(ins, "V")
    blk.vector(vf16)

    FSTf = AFA[:, 3072:4096]
    for tt in range(16):
        def vf17(ve, tt=tt):
            if tt == 0:
                k.w(ve, "V")
            MS = AFA[:, 5120:5184]
            SUM = AFA[:, 5184:5185]
            RC = AFA[:, 5185:5186]
            sl = slice(tt * 64, (tt + 1) * 64)
            ve.tensor_tensor(MS, STOKf[:, sl], k.THB[:, :], ALU.is_ge)
            ve.tensor_tensor(FSTf[:, sl], MS, STOKf[:, sl], ALU.mult)
            k.vb(ve, ve.tensor_reduce(SUM, FSTf[:, sl], AX.X, ALU.add))
            k.vb(ve, ve.tensor_scalar(SUM, SUM, 1e-6, None, ALU.add))
            k.vb(ve, ve.reciprocal(RC, SUM))
            ins = ve.tensor_tensor(FSTf[:, sl], FSTf[:, sl],
                                   RC.to_broadcast([128, 64]), ALU.mult)
            k.inc(ins, "V")
        blk.vector(vf17)

        def tf11(te, tt=tt):
            k.w(te, "V")
            ins = te.transpose(PB[1][0:64, 0:128], FSTf[:, tt * 64:(tt + 1) * 64],
                               k.IDfs[:, :])
            k.inc(ins, "T")
        blk.tensor(tf11)

        def vf18(ve, tt=tt):
            k.w(ve, "T")
            ins = ve.tensor_copy(k.FSTT[:, tt * 128:(tt + 1) * 128],
                                 PB[1][0:64, 0:128])
            k.inc(ins, "V")
        blk.vector(vf18)

    if _STOP == "fs":
        return
    moe_emit(k, blk, ci)


def moe_emit(k, blk, ci):
    nc = k.nc
    XT, XN, AB, AFA = k.XT, k.XN, k.AB, k.AFA
    PB = k.PB
    WUP = [AB[:, 0:2048], AB[:, 2048:4096]]
    WGT = [AB[:, 4096:6144], AB[:, 6144:8192]]
    WDN = [AB[:, 8192:10240], AB[:, 10240:12288]]
    SU = AFA[:, 0:256]
    GF = AFA[:, 256:512]
    HP = [AB[:, 14848:15360], AB[:, 15360:15872]]

    def wload(gp, it):
        e = it % 64
        b = it % 2
        ins = gp.dma_start(out=WUP[b].rearrange("p (kc h) -> p kc h", h=256),
                           in_=k.wup_d.ap()[e].rearrange("(kc p) h -> p kc h", p=128))
        k.inc(ins, "DG", 16)
        ins = gp.dma_start(out=WGT[b].rearrange("p (kc h) -> p kc h", h=256),
                           in_=k.wgt_d.ap()[e].rearrange("(kc p) h -> p kc h", p=128))
        k.inc(ins, "DG", 16)
        ins = gp.dma_start(out=WDN[b].rearrange("p (c d) -> p c d", d=1024),
                           in_=k.wdn_d.ap()[e].rearrange("(c p) d -> p c d", p=128))
        k.inc(ins, "DG", 16)

    def grep(gp):
        ins = gp.dma_start(out=k.REP, in_=k.REP_d.ap())
        k.inc(ins, "DG", 16)
    blk.gpsimd(grep)

    dg0 = k.c["DG"]
    NP = int(_os.environ.get("K_NP", "8"))
    EN = int(_os.environ.get("K_NE", "64"))
    if EN == 0:
        return

    def gpre(gp):
        wload(gp, 0)
        wload(gp, 1)
    blk.gpsimd(gpre)

    if EN < 0:
        def gw(gp):
            pass
        blk.gpsimd(gw)

        def vw(ve):
            ve.wait_ge(k.sem["DG"], dg0 + 96)
            ins = ve.nop()
            k.inc(ins, "V")
        blk.vector(vw)
        return

    for P in range(NP):
        t0 = P * 256
        for e in range(EN):
            it = P * EN + e
            b = it % 2
            T0, S0, V0 = k.c["T"], k.c["S"], k.c["V"]
            fsb_vwait = k.c["V"] if e == 0 else V0 - 1
            dg_wait = dg0 + 48 * (it + 1)

            def tf(te, P=P, e=e, b=b, t0=t0, T0=T0, S0=S0, V0=V0,
                   fsb_vwait=fsb_vwait, dg_wait=dg_wait):
                te.wait_ge(k.sem["DG"], dg_wait)
                te.wait_ge(k.sem["V"], fsb_vwait)
                te.matmul(PB[2][:, 0:256],
                          k.REP[:, e * 128:(e + 1) * 128],
                          k.FSTT[:, t0:t0 + 256], start=True, stop=True)
                te.wait_ge(k.sem["S"], S0)
                for kc in range(8):
                    ins = te.matmul(PB[0][:, 0:256],
                              WUP[b][:, kc * 256: kc * 256 + 128],
                              XN[:, kc * 2048 + t0: kc * 2048 + t0 + 256],
                              start=(kc == 0), stop=(kc == 7))
                k.inc(ins, "T")
                te.wait_ge(k.sem["V"], V0)
                for kc in range(8):
                    ins = te.matmul(PB[1][:, 0:256],
                              WGT[b][:, kc * 256: kc * 256 + 128],
                              XN[:, kc * 2048 + t0: kc * 2048 + t0 + 256],
                              start=(kc == 0), stop=(kc == 7))
                k.inc(ins, "T")
                te.wait_ge(k.sem["S"], S0 + 1)
                for kc in range(8):
                    ins = te.matmul(PB[0][:, 0:256],
                              WUP[b][:, kc * 256 + 128: kc * 256 + 256],
                              XN[:, kc * 2048 + t0: kc * 2048 + t0 + 256],
                              start=(kc == 0), stop=(kc == 7))
                k.inc(ins, "T")
                te.wait_ge(k.sem["V"], V0 + 2)
                for kc in range(8):
                    ins = te.matmul(PB[1][:, 0:256],
                              WGT[b][:, kc * 256 + 128: kc * 256 + 256],
                              XN[:, kc * 2048 + t0: kc * 2048 + t0 + 256],
                              start=(kc == 0), stop=(kc == 7))
                k.inc(ins, "T")
                if e > 0:
                    ep = it - 1
                    hb, wb = HP[ep % 2], WDN[ep % 2]
                    for ehc in range(2):
                        for ds in range(8):
                            ins = te.matmul(PB[3 + ds // 2][:, (ds % 2) * 256:(ds % 2) * 256 + 256],
                                      wb[:, ehc * 1024 + ds * 128: ehc * 1024 + (ds + 1) * 128],
                                      hb[:, ehc * 256:(ehc + 1) * 256],
                                      start=False, stop=False)
                else:
                    for ds in range(8):
                        ins = te.matmul(PB[3 + ds // 2][:, (ds % 2) * 256:(ds % 2) * 256 + 256],
                                  k.BDN[:, ds * 128:(ds + 1) * 128],
                                  k.FSTT[:, t0:t0 + 256],
                                  start=(ds % 2 == 0), stop=False)
                k.inc(ins, "T")
            blk.tensor(tf)

            def sf(sc, e=e, T0=T0, V0=V0):
                sc.wait_ge(k.sem["T"], T0 + 1)
                if e > 0:
                    sc.wait_ge(k.sem["V"], V0 - 1)
                ins = sc.activation(out=SU[:, :], in_=PB[0][:, 0:256],
                                    func=AF.Silu, bias=k.BUP[:, e:e + 1],
                                    scale=1.0)
                k.inc(ins, "S")
                sc.wait_ge(k.sem["T"], T0 + 3)
                sc.wait_ge(k.sem["V"], V0 + 1)
                ins = sc.activation(out=SU[:, :], in_=PB[0][:, 0:256],
                                    func=AF.Silu, bias=k.BUP[:, 64 + e:64 + e + 1],
                                    scale=1.0)
                k.inc(ins, "S")
            blk.scalar(sf)

            def vf(ve, e=e, it=it, T0=T0, S0=S0, V0=V0):
                ve.wait_ge(k.sem["S"], S0 + 1)
                ve.wait_ge(k.sem["T"], T0 + 1)
                ins = ve.tensor_tensor(GF[:, :], SU[:, :], PB[2][:, 0:256], ALU.mult)
                k.inc(ins, "V")
                ve.wait_ge(k.sem["T"], T0 + 2)
                ins = ve.scalar_tensor_tensor(HP[it % 2][:, 0:256], PB[1][:, 0:256],
                                              k.BGT[:, e:e + 1], GF[:, :],
                                              ALU.add, ALU.mult)
                k.inc(ins, "V")
                ve.wait_ge(k.sem["S"], S0 + 2)
                ins = ve.tensor_tensor(GF[:, :], SU[:, :], PB[2][:, 0:256], ALU.mult)
                k.inc(ins, "V")
                ve.wait_ge(k.sem["T"], T0 + 4)
                ins = ve.scalar_tensor_tensor(HP[it % 2][:, 256:512], PB[1][:, 0:256],
                                              k.BGT[:, 64 + e:64 + e + 1], GF[:, :],
                                              ALU.add, ALU.mult)
                k.inc(ins, "V")
            blk.vector(vf)

            if it + 2 < NP * EN:
                nit = it + 2
                dwait = T0 + 10 if e < EN - 1 else T0 + 6
                def gf3(gp, nit=nit, dwait=dwait):
                    gp.wait_ge(k.sem["T"], dwait)
                    wload(gp, nit)
                blk.gpsimd(gf3)

        def tf12(te, P=P, t0=t0, itl=P * EN + EN - 1):
            k.w(te, "V")
            hb, wb = HP[itl % 2], WDN[itl % 2]
            for ehc in range(2):
                for ds in range(8):
                    ins = te.matmul(PB[3 + ds // 2][:, (ds % 2) * 256:(ds % 2) * 256 + 256],
                              wb[:, ehc * 1024 + ds * 128: ehc * 1024 + (ds + 1) * 128],
                              hb[:, ehc * 256:(ehc + 1) * 256],
                              start=False, stop=(ehc == 1 and ds % 2 == 1))
            k.inc(ins, "T")
        blk.tensor(tf12)

        def vfa(ve, P=P, t0=t0):
            k.w(ve, "T")
            for ds in range(8):
                xsl = slice(ds * 2048 + t0, ds * 2048 + t0 + 256)
                ins = ve.scalar_tensor_tensor(
                    XT[:, xsl], PB[3 + ds // 2][:, (ds % 2) * 256:(ds % 2) * 256 + 256],
                    0.5, XT[:, xsl], ALU.mult, ALU.add)
            k.inc(ins, "V")
        blk.vector(vfa)



def attn_emit(k, blk, layer):
    nc = k.nc
    XT, XN, AB, AFA, PB = k.XT, k.XN, k.AB, k.AFA, k.PB
    ln_emit(k, blk, XT, XN, PB, AFA, f"l{layer}a_ln")

    XOT = AB[:, 0:4096]
    LATF = AB[:, 0:8192]
    KP = AB[:, 8192:12288]
    LATO = AB[:, 8192:12288]
    VT = AB[:, 12288:16448]
    QP = AB[:, 16448:18496]
    EB = AB[:, 18496:19520]
    CTX = AB[:, 19520:19776]
    CTSTG = AB[0:64, 19776:20032]
    WQP = AB[:, 20032:21056]
    WOP = AB[:, 21056:22080]
    WKP = AB[:, 22080:23104]
    WVP = AB[:, 23104:24128]
    WLP = AB[:, 24128:26176]
    DCOPY = AFA[0:1, 0:256]
    RD = AFA[0:1, 256:512]
    MTMP = AFA[:, 512:768]
    BCOLS = AFA[:, 768:800]
    CCOL = AFA[:, 800:801]
    RDB = AFA[0:64, 1024:1280]
    NKT = 32
    SL8 = [0.7071067811865476 ** (h + 1) / 8.0 for h in range(16)]
    nkv = 2 if layer == 0 else 8

    if layer == 0:
        def gw0(gp):
            ins = gp.dma_start(out=WLP.rearrange("p (kc m) -> p kc m", m=256),
                               in_=k.l0a_wl.ap().rearrange("(kc p) m -> p kc m", p=128))
            k.inc(ins, "DG", 16)
        blk.gpsimd(gw0)
        for lc in range(2):
            for tb in range(4):
                def tf(te, lc=lc, tb=tb):
                    k.w(te, "DG")
                    k.w(te, "V")
                    k.w(te, "S")
                    for kc in range(8):
                        ins = te.matmul(
                            PB[4][:, :],
                            WLP[:, kc * 256 + lc * 128: kc * 256 + lc * 128 + 128],
                            XN[:, kc * 2048 + tb * 512: kc * 2048 + tb * 512 + 512],
                            start=(kc == 0), stop=(kc == 7))
                    k.inc(ins, "T")
                blk.tensor(tf)

                def vf(ve, lc=lc, tb=tb):
                    k.w(ve, "T")
                    o = lc * 2048 + tb * 512
                    ins = ve.tensor_scalar(LATO[:, o:o + 512], PB[4][:, :],
                                           k.BL0[:, lc:lc + 1], None, ALU.add)
                    k.inc(ins, "V")
                blk.vector(vf)

        def yl(sy):
            k.w(sy, "V")
            ins = sy.dma_start(out=k.lat_in.ap(), in_=LATO)
            k.inc(ins, "DY", 16)
        blk.sync(yl)

        def gl(gp):
            k.w(gp, "DY")
            ins = gp.collective_compute(
                "AllGather", ALU.bypass,
                replica_groups=[[0, 1], [2, 3], [4, 5], [6, 7]],
                ins=[k.lat_in.ap().opt()], outs=[k.lat_out.ap().opt()])
            k.inc(ins, "CC")
        blk.gpsimd(gl)

        def yl2(sy):
            k.w(sy, "CC")
            for lc in range(2):
                for h in range(2):
                    ins = sy.dma_start(
                        out=LATF[:, lc * 4096 + h * 2048: lc * 4096 + (h + 1) * 2048],
                        in_=k.lat_out.ap()[h * 128:(h + 1) * 128,
                                           lc * 2048:(lc + 1) * 2048])
                    k.inc(ins, "DY", 16)
        blk.sync(yl2)

        def vsig(ve):
            k.w(ve, "DY")
            k.inc(ve.nop(), "V")
        blk.vector(vsig)
    else:
        def yx(sy):
            k.w(sy, "V")
            ins = sy.dma_start(out=k.xn_in.ap(), in_=XN[:, :])
            k.inc(ins, "DY", 16)
        blk.sync(yx)

        def gx(gp):
            k.w(gp, "DY")
            ins = gp.collective_compute(
                "AllGather", ALU.bypass,
                replica_groups=[[0, 1], [2, 3], [4, 5], [6, 7]],
                ins=[k.xn_in.ap().opt()], outs=[k.xn_out.ap().opt()])
            k.inc(ins, "CC")
        blk.gpsimd(gx)

        def vsig2(ve):
            ve.wait_ge(k.sem["CC"], k.c["CC"])
            k.inc(ve.nop(), "V")
        blk.vector(vsig2)

    def vones(ve):
        for t in range(NKT):
            ve.memset(VT[:, t * 130 + 64: t * 130 + 65], 1.0)
            ins = ve.memset(VT[:, t * 130 + 129: t * 130 + 130], 1.0)
        k.inc(ins, "V")
    blk.vector(vones)

    for hp in range(8):
        wq_d = k.l0a_wq if layer == 0 else k.l1a_wq
        wk_d = k.l0a_wk if layer == 0 else k.l1a_wk
        wv_d = k.l0a_wv if layer == 0 else k.l1a_wv
        wo_d = k.l0a_wo if layer == 0 else k.l1a_wo

        def gwp(gp, hp=hp, wq_d=wq_d, wk_d=wk_d, wv_d=wv_d, wo_d=wo_d):
            k.w(gp, "T")
            ins = gp.dma_start(
                out=WQP.rearrange("p (kc m) -> p kc m", m=128),
                in_=wq_d.ap()[:, hp * 128:(hp + 1) * 128].rearrange(
                    "(kc p) m -> p kc m", p=128))
            k.inc(ins, "DG", 16)
            ins = gp.dma_start(
                out=WKP[:, 0:nkv * 128].rearrange("p (kc m) -> p kc m", m=128),
                in_=wk_d.ap()[:, hp * 128:(hp + 1) * 128].rearrange(
                    "(kc p) m -> p kc m", p=128))
            k.inc(ins, "DG", 16)
            ins = gp.dma_start(
                out=WVP[:, 0:nkv * 128].rearrange("p (kc m) -> p kc m", m=128),
                in_=wv_d.ap()[:, hp * 128:(hp + 1) * 128].rearrange(
                    "(kc p) m -> p kc m", p=128))
            k.inc(ins, "DG", 16)
            ins = gp.dma_start(out=WOP,
                               in_=wo_d.ap()[hp * 128:(hp + 1) * 128, :])
            k.inc(ins, "DG", 16)
        blk.gpsimd(gwp)

        for tb in range(4):
            def tq(te, hp=hp, tb=tb):
                k.w(te, "DG")
                k.w(te, "V")
                pb = PB[4 + tb % 2]
                for kc in range(8):
                    ins = te.matmul(
                        pb[:, :], WQP[:, kc * 128:(kc + 1) * 128],
                        XN[:, kc * 2048 + tb * 512: kc * 2048 + tb * 512 + 512],
                        start=(kc == 0), stop=(kc == 7))
                k.inc(ins, "T")
            blk.tensor(tq)

            def vq(ve, hp=hp, tb=tb, layer=layer):
                k.w(ve, "T")
                pb = PB[4 + tb % 2]
                bias = k.BQ0[:, hp:hp + 1] if layer == 0 else 0.0
                ins = ve.tensor_scalar(QP[:, tb * 512:(tb + 1) * 512],
                                       pb[:, :], bias, None, ALU.add)
                k.inc(ins, "V")
            blk.vector(vq)

        for kb in range(8):
            if layer == 1:
                def yxo(sy, kb=kb):
                    k.w(sy, "T")
                    k.w(sy, "CC")
                    h = kb // 4
                    ltk = (kb % 4) * 512
                    src = k.xn_out.ap()[h * 128:(h + 1) * 128, :].rearrange(
                        "p (c n) -> p c n", c=8)[:, :, ltk:ltk + 512]
                    ins = sy.dma_start(
                        out=XOT.rearrange("p (c n) -> p c n", n=512), in_=src)
                    k.inc(ins, "DY", 16)
                blk.sync(yxo)

            def tk(te, hp=hp, kb=kb, layer=layer):
                if layer == 1:
                    k.w(te, "DY")
                k.w(te, "V")
                pb = PB[4 + kb % 2]
                for kc in range(nkv):
                    if layer == 0:
                        rhs = LATF[:, kc * 4096 + kb * 512: kc * 4096 + kb * 512 + 512]
                    else:
                        rhs = XOT[:, kc * 512:(kc + 1) * 512]
                    ins = te.matmul(pb[:, :], WKP[:, kc * 128:(kc + 1) * 128],
                                    rhs, start=(kc == 0), stop=(kc == nkv - 1))
                k.inc(ins, "T")
            blk.tensor(tk)

            def vk(ve, hp=hp, kb=kb, layer=layer):
                k.w(ve, "T")
                pb = PB[4 + kb % 2]
                bias = k.BK0[:, hp:hp + 1] if layer == 0 else 0.0
                ins = ve.tensor_scalar(KP[:, kb * 512:(kb + 1) * 512],
                                       pb[:, :], bias, None, ALU.add)
                k.inc(ins, "V")
            blk.vector(vk)

            for ti in range(4):
                t = kb * 4 + ti
                def tv(te, hp=hp, kb=kb, ti=ti, t=t, layer=layer):
                    k.w(te, "V")
                    pb = PB[6][:, (ti % 2) * 128:(ti % 2) * 128 + 128]
                    for kc in range(nkv):
                        if layer == 0:
                            lhsT = LATF[:, kc * 4096 + t * 128: kc * 4096 + t * 128 + 128]
                        else:
                            lhsT = XOT[:, kc * 512 + ti * 128: kc * 512 + ti * 128 + 128]
                        ins = te.matmul(pb, lhsT, WVP[:, kc * 128:(kc + 1) * 128],
                                        start=(kc == 0), stop=(kc == nkv - 1))
                    k.inc(ins, "T")
                blk.tensor(tv)

                def vv(ve, t=t, ti=ti):
                    k.w(ve, "T")
                    pb = PB[6][:, (ti % 2) * 128:(ti % 2) * 128 + 128]
                    ve.tensor_copy(VT[:, t * 130: t * 130 + 64], pb[:, 0:64])
                    ins = ve.tensor_copy(VT[:, t * 130 + 65: t * 130 + 129],
                                         pb[:, 64:128])
                    k.inc(ins, "V")
                blk.vector(vv)

        k.vt_ready = k.c["V"]

        for qb in range(8):
            for head in range(2):
                gh = 2 * hp + head
                diag = sorted({2 * qb, 2 * qb + 1, 16 + 2 * qb, 17 + 2 * qb}
                              ) if layer == 1 else []
                vbias = [0] * NKT
                if layer == 1:
                    V0b = k.c["V"]
                    for t in range(NKT):
                        vbias[t] = V0b + 2 * (t + 1)
                T0, S0 = k.c["T"], k.c["S"]
                tsc = [0] * NKT
                tctx = [0] * NKT
                sexp = [S0 + t + 1 for t in range(NKT)]
                tsc[0] = T0 + 1
                for t in range(1, NKT):
                    tsc[t] = T0 + 2 * t
                    tctx[t - 1] = T0 + 2 * t + 1
                tctx[NKT - 1] = T0 + 2 * NKT

                if layer == 1:
                    def vb1(ve, qb=qb, gh=gh):
                        sl8 = SL8[gh]
                        k.w(ve, "S")
                        for t in range(NKT):
                            pen = 10000.0 if 128 * t > 256 * qb + 255 else 0.0
                            pen1 = 10000.0 if 128 * t > 2048 + 256 * qb + 255 else 0.0
                            a = sl8 * (128 * t - 256 * qb) - 25.0 - pen
                            b = -sl8 * 2048.0 + pen - pen1
                            ins = ve.tensor_scalar(BCOLS[:, t:t + 1],
                                             k.PIOTA[:, :], sl8, a,
                                             ALU.mult, ALU.add)
                            k.vb(ve, ins)
                            ins = ve.scalar_tensor_tensor(
                                BCOLS[:, t:t + 1], k.HFB[:, :], b,
                                BCOLS[:, t:t + 1], ALU.mult, ALU.add)
                            k.inc(ins, "V")
                    blk.vector(vb1)
                    assert k.c["V"] == vbias[NKT - 1]

                vmask = {}
                if layer == 1:
                    V0m = k.c["V"]
                    for i, t in enumerate(diag):
                        vmask[t] = V0m + 2 * (i + 1)

                    def vmk(ve, qb=qb, diag=diag, sexp=sexp):
                        for t in diag:
                            ve.wait_ge(k.sem["S"], sexp[t])
                            ins = ve.tensor_scalar(
                                CCOL, k.HFB[:, :], 2048.0,
                                float(256 * qb - 128 * t),
                                ALU.mult, ALU.add)
                            k.inc(ins, "V")
                            k.w(ve, "V")
                            ve.tensor_scalar(MTMP, k.DIF[:, :], CCOL, None,
                                             ALU.is_le)
                            ins = ve.tensor_tensor(
                                EB[:, (t % 4) * 256:(t % 4 + 1) * 256],
                                EB[:, (t % 4) * 256:(t % 4 + 1) * 256],
                                MTMP, ALU.mult)
                            k.inc(ins, "V")
                    blk.vector(vmk)
                    assert k.c["V"] == V0m + 2 * len(diag)

                V_vt = k.vt_ready

                def tat(te, qb=qb, head=head, layer=layer, vbias=vbias,
                        vmask=vmask, sexp=sexp, diag=diag, S0=S0, V_vt=V_vt):
                    te.wait_ge(k.sem["V"], V_vt)

                    def score(t):
                        if layer == 1:
                            te.wait_ge(k.sem["V"], vbias[t])
                        if t >= 2:
                            te.wait_ge(k.sem["S"], sexp[t - 2])
                        else:
                            te.wait_ge(k.sem["S"], S0)
                        ins = te.matmul(PB[t % 2][:, 0:256],
                                        KP[head * 64:(head + 1) * 64,
                                           t * 128:(t + 1) * 128],
                                        QP[head * 64:(head + 1) * 64,
                                           qb * 256:(qb + 1) * 256],
                                        start=True, stop=True)
                        k.inc(ins, "T")

                    def ctx(t):
                        if layer == 1 and t in diag:
                            te.wait_ge(k.sem["V"], vmask[t])
                        else:
                            te.wait_ge(k.sem["S"], sexp[t])
                        ins = te.matmul(
                            PB[2][0:65, head * 256:(head + 1) * 256],
                            VT[:, t * 130 + head * 65: t * 130 + head * 65 + 65],
                            EB[:, (t % 4) * 256:(t % 4 + 1) * 256],
                            start=(t == 0 and head == 0),
                            stop=(t == NKT - 1 and head == 1))
                        k.inc(ins, "T")
                    score(0)
                    for t in range(1, NKT):
                        score(t)
                        ctx(t - 1)
                    ctx(NKT - 1)
                blk.tensor(tat)
                assert k.c["T"] == tctx[NKT - 1]

                def ssf(sc, qb=qb, head=head, layer=layer, vbias=vbias,
                        tsc=tsc, tctx=tctx):
                    for t in range(NKT):
                        sc.wait_ge(k.sem["T"], tsc[t])
                        if layer == 1:
                            sc.wait_ge(k.sem["V"], vbias[t])
                        if t >= 4:
                            sc.wait_ge(k.sem["T"], tctx[t - 4])
                        bias = (BCOLS[:, t:t + 1] if layer == 1
                                else k.cc(-15.0))
                        ins = sc.activation(
                            out=EB[:, (t % 4) * 256:(t % 4 + 1) * 256],
                            in_=PB[t % 2][:, 0:256], func=AF.Exp,
                            bias=bias, scale=0.125)
                        k.inc(ins, "S")
                blk.scalar(ssf)
                assert k.c["S"] == sexp[NKT - 1]

                def vdiv(ve, head=head, tc31=tctx[NKT - 1]):
                    ve.wait_ge(k.sem["T"], tc31)
                    ins = ve.tensor_copy(
                        DCOPY, PB[2][64:65, head * 256:(head + 1) * 256])
                    k.inc(ins, "V")
                    k.w(ve, "V")
                    ins = ve.reciprocal(RD, DCOPY)
                    k.inc(ins, "V")
                blk.vector(vdiv)

                def trd(te, head=head):
                    k.w(te, "V")
                    ins = te.matmul(PB[3][0:64, head * 256:(head + 1) * 256],
                                    k.ONESf[0:1, 0:64], RD,
                                    start=True, stop=True)
                    k.inc(ins, "T")
                blk.tensor(trd)

                def vev(ve, hp=hp, head=head, layer=layer):
                    k.w(ve, "T")
                    hr = slice(head * 256, (head + 1) * 256)
                    dst = CTX[0:64, :] if head == 0 else CTSTG
                    ve.tensor_copy(RDB, PB[3][0:64, hr])
                    ins = ve.tensor_tensor(dst, PB[2][0:64, hr],
                                           RDB, ALU.mult)
                    k.inc(ins, "V")
                    if layer == 0:
                        c = 2 * hp + head
                        ins = ve.tensor_scalar(dst, dst,
                                               k.BVH0[0:64, c:c + 1], None,
                                               ALU.add)
                        k.inc(ins, "V")
                    if head == 1:
                        ins = ve.tensor_copy(CTX[64:128, :], CTSTG)
                        k.inc(ins, "V")
                blk.vector(vev)

            def top(te, hp=hp, qb=qb):
                k.w(te, "V")
                for dch in range(8):
                    ins = te.matmul(
                        PB[3 + dch // 2][:, (dch % 2) * 256:(dch % 2) * 256 + 256],
                        WOP[:, dch * 128:(dch + 1) * 128], CTX,
                        start=(dch % 2 == 0), stop=(dch % 2 == 1))
                k.inc(ins, "T")
            blk.tensor(top)

            def vop(ve, qb=qb):
                k.w(ve, "T")
                for dch in range(8):
                    sl = slice(dch * 2048 + qb * 256, dch * 2048 + qb * 256 + 256)
                    ins = ve.tensor_tensor(
                        XT[:, sl], XT[:, sl],
                        PB[3 + dch // 2][:, (dch % 2) * 256:(dch % 2) * 256 + 256],
                        ALU.add)
                k.inc(ins, "V")
            blk.vector(vop)

    def vbo(ve, layer=layer):
        BO = k.BO0 if layer == 0 else k.BO1
        for dch in range(8):
            ins = ve.tensor_scalar(XT[:, dch * 2048:(dch + 1) * 2048],
                                   XT[:, dch * 2048:(dch + 1) * 2048],
                                   BO[:, dch:dch + 1], None, ALU.add)
        k.inc(ins, "V")
    blk.vector(vbo)



FP4_MID = [0.25, 0.75, 1.25, 1.75, 2.5, 3.5, 5.0]
FP4_D = [0.5, 0.5, 0.5, 0.5, 1.0, 1.0, 2.0]


def quant_sweep(k, blk, X, ABSX, T, Q, SCL, MJ, QF32, N, tag):
    nc = k.nc
    NB = N // 16

    def sab(sc):
        k.w(sc, "V")
        ins = sc.activation(out=ABSX, in_=X, func=AF.Abs)
        k.inc(ins, "S")
    blk.scalar(sab)

    def vsw(ve):
        k.w(ve, "S")
        va = ABSX.rearrange("p (b s) -> p b s", s=16)
        ve.tensor_reduce(SCL, va, AX.X, ALU.max)
        ve.tensor_scalar(SCL, SCL, 1.0 / 6.0, None, ALU.mult)
        ve.tensor_scalar(SCL, SCL, 1e-12, None, ALU.max)
        first = True
        for j in range(7):
            ins = ve.tensor_scalar(MJ, SCL, FP4_MID[j], None, ALU.mult)
            k.vb(ve, ins)
            mjb = MJ.rearrange("p (b u) -> p b u", u=1).to_broadcast([128, NB, 16])
            ve.tensor_tensor(T.rearrange("p (b s) -> p b s", s=16), va, mjb,
                             ALU.is_gt)
            if first:
                ve.tensor_scalar(Q, T, FP4_D[j], None, ALU.mult)
                first = False
            else:
                ve.scalar_tensor_tensor(Q, T, FP4_D[j], Q, ALU.mult, ALU.add)
        ve.tensor_scalar(T, X, 0.0, None, ALU.is_gt)
        ins = ve.tensor_scalar(T, T, 2.0, -1.0, ALU.mult, ALU.add)
        k.vb(ve, ins)
        sclb = SCL.rearrange("p (b u) -> p b u", u=1).to_broadcast([128, NB, 16])
        ve.tensor_tensor(QF32.rearrange("p (b s) -> p b s", s=16),
                         Q.rearrange("p (b s) -> p b s", s=16), sclb, ALU.mult)
        ins = ve.tensor_tensor(QF32, QF32, T, ALU.mult)
        k.inc(ins, "V")
    blk.vector(vsw)


def l1w_quant_emit(k, blk):
    nc = k.nc
    AB, AFA, PB = k.AB, k.AFA, k.PB
    SCL = AFA[:, 5632:5820]
    MJ = AFA[:, 5820:6008]

    def wtile(src_d, dst_d, r, ncols):
        X = AFA[:, 0:ncols]
        nh = 1408 if ncols > 2048 else ncols
        STG = AB[:, 0:ncols]

        def yld(sy, src_d=src_d, r=r, ncols=ncols):
            k.w(sy, "V")
            k.w(sy, "T")
            ins = sy.dma_start(out=X, in_=src_d.ap()[r * 128:(r + 1) * 128, :])
            k.inc(ins, "DY", 16)
        blk.sync(yld)

        def vw(ve):
            k.w(ve, "DY")
            k.inc(ve.nop(), "V")
        blk.vector(vw)

        for h0 in range(0, ncols, nh):
            quant_sweep(k, blk, AFA[:, h0:h0 + nh],
                        AFA[:, 2816:2816 + nh],
                        AFA[:, 4224:4224 + nh],
                        AB[:, 4096:4096 + nh],
                        SCL[:, 0:nh // 16], MJ[:, 0:nh // 16],
                        AFA[:, h0:h0 + nh],
                        nh, "wq")

        nj = ncols // 128
        for jc in range(nj):
            def tt_(te, jc=jc, ncols=ncols):
                if jc == 0:
                    k.w(te, "V")
                else:
                    k.w(te, "V")
                ins = te.transpose(PB[4 + jc % 2][:, 0:128],
                                   AFA[:, jc * 128:(jc + 1) * 128],
                                   k.IDfs[:, :])
                k.inc(ins, "T")
            blk.tensor(tt_)

            def vc_(ve, jc=jc, ncols=ncols):
                k.w(ve, "T")
                if jc == 0:
                    k.w(ve, "DY")
                ins = ve.tensor_copy(STG[:, jc * 128:(jc + 1) * 128],
                                     PB[4 + jc % 2][:, 0:128])
                k.inc(ins, "V")
            blk.vector(vc_)

        def yst(sy, dst_d=dst_d, r=r, ncols=ncols, nj=nj):
            k.w(sy, "V")
            for jc in range(nj):
                ins = sy.dma_start(
                    out=dst_d.ap()[jc * 128:(jc + 1) * 128,
                                   r * 128:(r + 1) * 128],
                    in_=STG[:, jc * 128:(jc + 1) * 128])
                k.inc(ins, "DY", 16)
        blk.sync(yst)

    for r in range(22):
        wtile(k.l1f_wg, k.wqg_t, r, 1024)
        wtile(k.l1f_wu, k.wqu_t, r, 1024)
    for r in range(8):
        wtile(k.l1f_wd, k.wqd_t, r, 2816)

    def vfin(ve):
        k.w(ve, "DY")
        k.inc(ve.nop(), "V")
    blk.vector(vfin)


def l1ffn_emit(k, blk, ci):
    nc = k.nc
    AB, AFA, PB, XT = k.AB, k.AFA, k.PB, k.XT
    if not getattr(k, "l1w_done", False):
        l1w_quant_emit(k, blk)
        k.l1w_done = True
    XQF = AB[:, 0:2048]
    HT = AB[:, 2048:7680]
    TB_ = AB[:, 7680:10496]
    QB_ = AB[:, 10496:13312]
    HQF = AB[:, 13312:18944]
    WS = [AB[:, 18944:21504], AB[:, 21504:24064]]
    ABSB = AB[:, 24064:26880]
    SCL = AFA[:, 5632:5820]
    MJ = AFA[:, 5820:6008]
    MU = AFA[:, 6008:6009]
    VS = AFA[:, 6010:6011]
    RS_ = AFA[:, 6012:6013]

    for blkI in range(8):
        t0 = blkI * 256
        for tt in range(2):
            def txp(te, t0=t0, tt=tt):
                k.w(te, "V")
                k.w(te, "S")
                for fc in range(8):
                    ins = te.transpose(
                        PB[4 + fc // 4][:, (fc % 4) * 128:(fc % 4) * 128 + 128],
                        XT[:, fc * 2048 + t0 + tt * 128:
                           fc * 2048 + t0 + tt * 128 + 128],
                        k.IDfs[:, :])
                k.inc(ins, "T")
            blk.tensor(txp)

            def vcp(ve):
                k.w(ve, "T")
                ve.tensor_copy(AFA[:, 0:512], PB[4][:, :])
                ins = ve.tensor_copy(AFA[:, 512:1024], PB[5][:, :])
                k.inc(ins, "V")
            blk.vector(vcp)

            def vln(ve):
                XTM = AFA[:, 0:1024]
                XC = AFA[:, 1024:2048]
                k.w(ve, "V")
                k.vb(ve, ve.tensor_reduce(MU, XTM, AX.X, ALU.add))
                k.vb(ve, ve.tensor_scalar(MU, MU, 1.0 / 1024.0, None,
                                          ALU.mult))
                ins = ve.tensor_scalar(XC, XTM, MU, None, ALU.subtract)
                k.inc(ins, "V")
            blk.vector(vln)

            def sln(sc):
                XC = AFA[:, 1024:2048]
                k.w(sc, "S")
                k.w(sc, "V")
                ins = sc.activation(out=AFA[:, 2048:3072], in_=XC,
                                    func=AF.Square, accum_out=VS)
                k.inc(ins, "S")
                k.w(sc, "S")
                ins = sc.activation(out=RS_, in_=VS, func=AF.Ln,
                                    bias=k.cc(1e-5), scale=1.0 / 1024.0)
                k.inc(ins, "S")
                k.w(sc, "S")
                ins = sc.activation(out=RS_, in_=RS_, func=AF.Exp,
                                    scale=k.cc(-0.5))
                k.inc(ins, "S")
            blk.scalar(sln)

            def vn2(ve):
                XTM = AFA[:, 0:1024]
                XC = AFA[:, 1024:2048]
                k.w(ve, "S")
                ins = ve.tensor_scalar(XTM, XC, RS_, None, ALU.mult)
                k.inc(ins, "V")
            blk.vector(vn2)

            quant_sweep(k, blk, AFA[:, 0:1024], AFA[:, 2048:3072],
                        AFA[:, 3072:4096], QB_[:, 0:1024],
                        SCL[:, 0:64], MJ[:, 0:64], AFA[:, 0:1024], 1024, "xq")

            def txq(te, tt=tt):
                k.w(te, "V")
                for fc in range(8):
                    ins = te.transpose(
                        PB[4 + fc // 4][:, (fc % 4) * 128:(fc % 4) * 128 + 128],
                        AFA[:, fc * 128:(fc + 1) * 128], k.IDfs[:, :])
                k.inc(ins, "T")
            blk.tensor(txq)

            def vxq(ve, tt=tt):
                k.w(ve, "T")
                for fc in range(8):
                    ins = ve.tensor_copy(
                        XQF[:, fc * 256 + tt * 128: fc * 256 + tt * 128 + 128],
                        PB[4 + fc // 4][:, (fc % 4) * 128:(fc % 4) * 128 + 128])
                k.inc(ins, "V")
            blk.vector(vxq)

        te_done = {}
        for ncI in range(22):
            def ywg(sy, ncI=ncI, te_done=te_done):
                if ncI >= 2:
                    sy.wait_ge(k.sem["T"], te_done[ncI - 2])
                else:
                    k.w(sy, "V")
                b = ncI % 2
                ins = sy.dma_start(
                    out=WS[b][:, 0:1024].rearrange("p (kc m) -> p kc m", m=128),
                    in_=k.wqg_t.ap()[:, ncI * 128:(ncI + 1) * 128].rearrange(
                        "(kc p) m -> p kc m", p=128))
                k.inc(ins, "DY", 16)
                ins = sy.dma_start(
                    out=WS[b][:, 1024:2048].rearrange("p (kc m) -> p kc m", m=128),
                    in_=k.wqu_t.ap()[:, ncI * 128:(ncI + 1) * 128].rearrange(
                        "(kc p) m -> p kc m", p=128))
                k.inc(ins, "DY", 16)
            blk.sync(ywg)
            dyw = k.c["DY"]

            def tgu(te, ncI=ncI, dyw=dyw):
                te.wait_ge(k.sem["DY"], dyw)
                k.w(te, "V")
                k.w(te, "S")
                b = ncI % 2
                for kc in range(8):
                    te.matmul(PB[0][:, 0:256], WS[b][:, kc * 128:(kc + 1) * 128],
                              XQF[:, kc * 256:(kc + 1) * 256],
                              start=(kc == 0), stop=(kc == 7))
                for kc in range(8):
                    ins = te.matmul(PB[1][:, 0:256],
                                    WS[b][:, 1024 + kc * 128:1024 + (kc + 1) * 128],
                                    XQF[:, kc * 256:(kc + 1) * 256],
                                    start=(kc == 0), stop=(kc == 7))
                k.inc(ins, "T")
            blk.tensor(tgu)

            def sgu(sc):
                k.w(sc, "T")
                ins = sc.activation(out=AFA[:, 0:256], in_=PB[0][:, 0:256],
                                    func=AF.Silu)
                k.inc(ins, "S")
            blk.scalar(sgu)

            def vh(ve):
                k.w(ve, "S")
                ins = ve.tensor_tensor(AFA[:, 256:512], AFA[:, 0:256],
                                       PB[1][:, 0:256], ALU.mult)
                k.inc(ins, "V")
            blk.vector(vh)

            def tht(te, ncI=ncI, te_done=te_done):
                k.w(te, "V")
                for tt in range(2):
                    ins = te.transpose(PB[2][:, tt * 128:(tt + 1) * 128],
                                       AFA[:, 256 + tt * 128:256 + (tt + 1) * 128],
                                       k.IDfs[:, :])
                k.inc(ins, "T")
                te_done[ncI] = k.c["T"]
            blk.tensor(tht)

            def vht(ve, ncI=ncI):
                k.w(ve, "T")
                for tt in range(2):
                    ins = ve.tensor_copy(
                        HT[:, tt * 2816 + ncI * 128: tt * 2816 + (ncI + 1) * 128],
                        PB[2][:, tt * 128:(tt + 1) * 128])
                k.inc(ins, "V")
            blk.vector(vht)

        for tt in range(2):
            quant_sweep(k, blk, HT[:, tt * 2816:(tt + 1) * 2816], ABSB,
                        TB_, QB_, SCL[:, 0:176], MJ[:, 0:176],
                        AFA[:, 0:2816], 2816, "hq")
            for jc in range(22):
                def thq(te, jc=jc):
                    k.w(te, "V")
                    ins = te.transpose(PB[4 + jc % 2][:, 0:128],
                                       AFA[:, jc * 128:(jc + 1) * 128],
                                       k.IDfs[:, :])
                    k.inc(ins, "T")
                blk.tensor(thq)

                def vhq(ve, jc=jc, tt=tt):
                    k.w(ve, "T")
                    ins = ve.tensor_copy(
                        HQF[:, jc * 256 + tt * 128: jc * 256 + tt * 128 + 128],
                        PB[4 + jc % 2][:, 0:128])
                    k.inc(ins, "V")
                blk.vector(vhq)

        td_done = {}
        for kc in range(22):
            def ywd(sy, kc=kc, td_done=td_done):
                if kc >= 2:
                    sy.wait_ge(k.sem["T"], td_done[kc - 2])
                else:
                    k.w(sy, "V")
                b = kc % 2
                ins = sy.dma_start(out=WS[b][:, 0:1024],
                                   in_=k.wqd_t.ap()[kc * 128:(kc + 1) * 128, :])
                k.inc(ins, "DY", 16)
            blk.sync(ywd)
            dyw = k.c["DY"]

            def tdn(te, kc=kc, dyw=dyw, td_done=td_done):
                te.wait_ge(k.sem["DY"], dyw)
                if kc == 0:
                    k.w(te, "V")
                b = kc % 2
                for dch in range(8):
                    ins = te.matmul(
                        PB[3 + dch // 2][:, (dch % 2) * 256:(dch % 2) * 256 + 256],
                        WS[b][:, dch * 128:(dch + 1) * 128],
                        HQF[:, kc * 256:(kc + 1) * 256],
                        start=(kc == 0 and dch % 2 == 0),
                        stop=(kc == 21 and dch % 2 == 1))
                k.inc(ins, "T")
                td_done[kc] = k.c["T"]
            blk.tensor(tdn)

        def vdn(ve, t0=t0):
            k.w(ve, "T")
            for dch in range(8):
                xsl = slice(dch * 2048 + t0, dch * 2048 + t0 + 256)
                ins = ve.scalar_tensor_tensor(
                    XT[:, xsl],
                    PB[3 + dch // 2][:, (dch % 2) * 256:(dch % 2) * 256 + 256],
                    0.5, XT[:, xsl], ALU.mult, ALU.add)
            k.inc(ins, "V")
        blk.vector(vdn)


def build(core_id):
    nc = bass.Bass("TRN2", target_bir_lowering=False, debug=False)
    k = Ctx(nc)
    k.pid = nc.partition_id()

    x_in = nc.dram_tensor("x", [TC, D], F32, kind="ExternalInput").ap()
    out = nc.dram_tensor("out", [TC, D], F32, kind="ExternalOutput").ap()
    EI = "ExternalInput"
    k.wr_d = nc.dram_tensor("l0f_wr", [D, 64], F32, kind=EI)
    k.br_d = nc.dram_tensor("l0f_br", [1, 64], F32, kind=EI)
    k.wup_d = nc.dram_tensor("l0f_wup", [64, D, 256], F32, kind=EI)
    k.bup_d = nc.dram_tensor("l0f_bup", [64, 256], F32, kind=EI)
    k.wgt_d = nc.dram_tensor("l0f_wgt", [64, D, 256], F32, kind=EI)
    k.bgt_d = nc.dram_tensor("l0f_bgt", [64, 256], F32, kind=EI)
    k.wdn_d = nc.dram_tensor("l0f_wdn", [64, 256, D], F32, kind=EI)
    k.bdn_d = nc.dram_tensor("l0f_bdn", [64, D], F32, kind=EI)
    k.l0a_wq = nc.dram_tensor("l0a_wq", [D, D], F32, kind=EI)
    k.l0a_bq = nc.dram_tensor("l0a_bq", [8, 128], F32, kind=EI)
    k.l0a_wl = nc.dram_tensor("l0a_wl", [D, 256], F32, kind=EI)
    k.l0a_bl = nc.dram_tensor("l0a_bl", [2, 128], F32, kind=EI)
    k.l0a_wk = nc.dram_tensor("l0a_wk", [256, D], F32, kind=EI)
    k.l0a_bk = nc.dram_tensor("l0a_bk", [8, 128], F32, kind=EI)
    k.l0a_wv = nc.dram_tensor("l0a_wv", [256, D], F32, kind=EI)
    k.l0a_bv = nc.dram_tensor("l0a_bv", [16, 64], F32, kind=EI)
    k.l0a_wo = nc.dram_tensor("l0a_wo", [D, D], F32, kind=EI)
    k.l0a_bo = nc.dram_tensor("l0a_bo", [8, 128], F32, kind=EI)
    k.l1a_wq = nc.dram_tensor("l1a_wq", [D, D], F32, kind=EI)
    k.l1a_wk = nc.dram_tensor("l1a_wk", [D, D], F32, kind=EI)
    k.l1a_wv = nc.dram_tensor("l1a_wv", [D, D], F32, kind=EI)
    k.l1a_wo = nc.dram_tensor("l1a_wo", [D, D], F32, kind=EI)
    k.l1a_bo = nc.dram_tensor("l1a_bo", [8, 128], F32, kind=EI)
    k.cinfo = nc.dram_tensor("cinfo", [1, 1], F32, kind=EI)
    k.l1f_wg = nc.dram_tensor("l1f_wg", [2816, D], F32, kind=EI)
    k.l1f_wu = nc.dram_tensor("l1f_wu", [2816, D], F32, kind=EI)
    k.l1f_wd = nc.dram_tensor("l1f_wd", [D, 2816], F32, kind=EI)
    k.wqg_t = nc.dram_tensor("wqg_t", [D, 2816], BF16)
    k.wqu_t = nc.dram_tensor("wqu_t", [D, 2816], BF16)
    k.wqd_t = nc.dram_tensor("wqd_t", [2816, D], BF16)
    k.lat_in = nc.dram_tensor("lat_in", [128, 4096], BF16)
    k.lat_out = nc.dram_tensor("lat_out", [256, 4096], BF16)
    k.xn_in = nc.dram_tensor("xn_in", [128, NCH * TC], BF16)
    k.xn_out = nc.dram_tensor("xn_out", [256, NCH * TC], BF16)
    k.sc_in = nc.dram_tensor("sc_in", [64, TC], F32)
    k.sc_my = nc.dram_tensor("sc_my", [128, 1024], F32)
    k.sc_out = nc.dram_tensor("sc_out", [512, TC], F32)
    k.gg_in = nc.dram_tensor("gg_in", [8, 512], F32)
    k.gg_out = nc.dram_tensor("gg_out", [64, 512], F32)
    if _DBG:
        k.dbg_d = nc.dram_tensor("dbg", [128, 8192], F32, kind="ExternalOutput")
        k.dbg2_d = nc.dram_tensor("dbg2", [128, NCH * TC], mybir.dt.bfloat16,
                                  kind="ExternalOutput")

    k.IDbf = nc.inline_tensor(np.eye(128, dtype=ml_dtypes.bfloat16), "idbf")
    k.IDf = nc.inline_tensor(np.eye(128, dtype=np.float32), "idf")
    onesf = np.ones((128, 128), dtype=np.float32)
    k.ONESf_d = nc.inline_tensor(onesf, "onesf")
    k.ONESbf_d = nc.inline_tensor(onesf.astype(ml_dtypes.bfloat16), "onesbf")
    conarr = np.tile(np.asarray(CONST_VALS, np.float32)[None, :], (128, 1))
    k.CON_d = nc.inline_tensor(np.ascontiguousarray(conarr), "conal")

    XT = nc.alloc_sbuf_tensor("XT", [128, NCH * TC], F32)
    XN = nc.alloc_sbuf_tensor("XN", [128, NCH * TC], BF16)
    AB = nc.alloc_sbuf_tensor("AB", [128, 28672], BF16)
    AFA = nc.alloc_sbuf_tensor("AFA", [128, 6144], F32)
    k.XT, k.XN, k.AB, k.AFA = XT, XN, AB, AFA
    k.REP = AB[0:64, 15872:24064]
    IDbf = nc.alloc_sbuf_tensor("IDbf_s", [128, 128], BF16)
    IDf = nc.alloc_sbuf_tensor("IDf_s", [128, 128], F32)
    ONESf = nc.alloc_sbuf_tensor("ONESf_s", [128, 128], F32)
    ONESbf = nc.alloc_sbuf_tensor("ONESbf_s", [128, 128], BF16)
    k.ONESf, k.ONESbf, k.IDfs, k.IDbfs = ONESf, ONESbf, IDf, IDbf
    CONS = nc.alloc_sbuf_tensor("CONS", [128, len(CONST_VALS)], F32)
    k.CONS = CONS
    k.cc = lambda v: CONS[:, CONST_VALS.index(v):CONST_VALS.index(v) + 1]
    k.WR = nc.alloc_sbuf_tensor("WRs", [128, 512], BF16)
    k.BRB = nc.alloc_sbuf_tensor("BRBs", [128, 64], F32)
    k.BR1 = nc.alloc_sbuf_tensor("BR1s", [1, 64], F32)
    k.BUP = nc.alloc_sbuf_tensor("BUPs", [128, 128], F32)
    k.BGT = nc.alloc_sbuf_tensor("BGTs", [128, 128], F32)
    k.BUPR = nc.alloc_sbuf_tensor("BUPRs", [64, 256], F32)
    k.BGTR = nc.alloc_sbuf_tensor("BGTRs", [64, 256], F32)
    k.BDN = nc.alloc_sbuf_tensor("BDNs", [64, 1024], BF16)
    k.GSseg = nc.alloc_sbuf_tensor("GSsegs", [128, 16], F32)
    k.GMseg = nc.alloc_sbuf_tensor("GMsegs", [128, 16], F32)
    k.GSA = nc.alloc_sbuf_tensor("GSAs", [64, 256], F32)
    k.GMA = nc.alloc_sbuf_tensor("GMAs", [64, 256], F32)
    k.RS = nc.alloc_sbuf_tensor("RSs", [64, 20], F32)
    k.THB = nc.alloc_sbuf_tensor("THBs", [128, 64], F32)
    k.FSTT = nc.alloc_sbuf_tensor("FSTTs", [64, 2048], BF16)
    NRND = 44
    ckl = np.zeros((64, NRND), np.float32)
    lr = math.log(64.0) - math.log(1e-38)
    for r in range(NRND):
        ckl[:, r] = math.exp(lr * (2.0 ** -(r + 1)))
    k.CKL_d = nc.inline_tensor(np.ascontiguousarray(ckl), "ckl")
    k.CKL = nc.alloc_sbuf_tensor("CKLs", [64, NRND], F32)
    k.EIDX_d = nc.inline_tensor(
        np.ascontiguousarray(np.arange(64, dtype=np.float32)[:, None]), "eidx")
    k.EIDX = nc.alloc_sbuf_tensor("EIDXs", [64, 1], F32)
    k.IOTA_d = nc.inline_tensor(np.ascontiguousarray(
        np.tile(np.arange(256, dtype=np.float32)[None, :], (64, 1))), "iota256")
    k.IOTA256 = nc.alloc_sbuf_tensor("IOTAs", [64, 256], F32)
    k.EYE64_d = nc.inline_tensor(np.eye(64, dtype=ml_dtypes.bfloat16), "eye64")
    k.EYE64 = nc.alloc_sbuf_tensor("EYE64s", [64, 64], BF16)
    rep = np.repeat(np.eye(64, dtype=ml_dtypes.bfloat16), 128, axis=1)
    k.REP_d = nc.inline_tensor(np.ascontiguousarray(rep), "rep64")
    k.PIO_d = nc.inline_tensor(
        np.ascontiguousarray(np.arange(128, dtype=np.float32)[:, None]), "pio")
    k.DIF_d = nc.inline_tensor(np.ascontiguousarray(
        (np.arange(128, dtype=np.float32)[:, None]
         - np.arange(256, dtype=np.float32)[None, :])), "dif")
    k.PIOTA = nc.alloc_sbuf_tensor("PIOTAs", [128, 1], F32)
    k.DIF = nc.alloc_sbuf_tensor("DIFs", [128, 256], F32)
    k.BQ0 = nc.alloc_sbuf_tensor("BQ0s", [128, 8], F32)
    k.BK0 = nc.alloc_sbuf_tensor("BK0s", [128, 8], F32)
    k.BVH0 = nc.alloc_sbuf_tensor("BVH0s", [64, 16], F32)
    k.BO0 = nc.alloc_sbuf_tensor("BO0s", [128, 8], F32)
    k.BL0 = nc.alloc_sbuf_tensor("BL0s", [128, 2], F32)
    k.BO1 = nc.alloc_sbuf_tensor("BO1s", [128, 8], F32)
    k.HFS = nc.alloc_sbuf_tensor("HFSs", [1, 1], F32)
    k.HFB = nc.alloc_sbuf_tensor("HFBs", [128, 1], F32)

    PB = [nc.alloc_psum_tensor(f"pb{i}", [128, 512], F32) for i in range(7)]
    PH = nc.alloc_psum_tensor("ph0", [128, 1024], BF16)
    k.PB, k.PH = PB, PH

    with nc.Block() as blk:
        @blk.sync
        def _(sy):
            ins = sy.dma_start(out=IDf[:, :], in_=k.IDf.ap())
            k.inc(ins, "DY", 16)
            ins = sy.dma_start(out=IDbf[:, :], in_=k.IDbf.ap())
            k.inc(ins, "DY", 16)
            ins = sy.dma_start(out=ONESf[:, :], in_=k.ONESf_d.ap())
            k.inc(ins, "DY", 16)
            ins = sy.dma_start(out=ONESbf[:, :], in_=k.ONESbf_d.ap())
            k.inc(ins, "DY", 16)
            ins = sy.dma_start(out=CONS[:, :], in_=k.CON_d.ap())
            k.inc(ins, "DY", 16)
            ins = sy.dma_start(out=k.CKL[:, :], in_=k.CKL_d.ap())
            k.inc(ins, "DY", 16)
            ins = sy.dma_start(out=k.EIDX[:, :], in_=k.EIDX_d.ap())
            k.inc(ins, "DY", 16)
            ins = sy.dma_start(out=k.IOTA256[:, :], in_=k.IOTA_d.ap())
            k.inc(ins, "DY", 16)
            ins = sy.dma_start(out=k.EYE64[:, :], in_=k.EYE64_d.ap())
            k.inc(ins, "DY", 16)
            ins = sy.dma_start(out=k.BR1[:, :], in_=k.br_d.ap())
            k.inc(ins, "DY", 16)
            ins = sy.dma_start(out=k.BUPR[:, :], in_=k.bup_d.ap())
            k.inc(ins, "DY", 16)
            ins = sy.dma_start(out=k.BGTR[:, :], in_=k.bgt_d.ap())
            k.inc(ins, "DY", 16)
            ins = sy.dma_start(out=k.PIOTA[:, :], in_=k.PIO_d.ap())
            k.inc(ins, "DY", 16)
            ins = sy.dma_start(out=k.DIF[:, :], in_=k.DIF_d.ap())
            k.inc(ins, "DY", 16)
            for dst, srcd in ((k.BQ0, k.l0a_bq), (k.BK0, k.l0a_bk),
                              (k.BO0, k.l0a_bo), (k.BO1, k.l1a_bo),
                              (k.BVH0, k.l0a_bv), (k.BL0, k.l0a_bl)):
                ins = sy.dma_start(out=dst[:, :],
                                   in_=srcd.ap().rearrange("c p -> p c"),
                                   allow_slow_non_contiguous=True)
                k.inc(ins, "DY", 16)
            ins = sy.dma_start(out=k.HFS[:, :], in_=k.cinfo.ap())
            k.inc(ins, "DY", 16)

        @blk.gpsimd
        def _(gp):
            ins = gp.dma_start(out=k.WR[:, :].rearrange("p (kc e) -> p kc e", e=64),
                               in_=k.wr_d.ap().rearrange("(kc p) e -> p kc e", p=128))
            k.inc(ins, "DG", 16)
            ins = gp.dma_start(out=k.BDN[:, :], in_=k.bdn_d.ap())
            k.inc(ins, "DG", 16)

        @blk.tensor
        def _(te):
            k.w(te, "DY")
            te.matmul(PB[5][:, 0:64], ONESf[0:1, :], k.BR1[:, :],
                      start=True, stop=True)
            te.matmul(PB[3][:, 0:1], ONESf[0:1, :], k.HFS[:, :],
                      start=True, stop=True)
            te.transpose(PB[6][:, 0:64], k.BUPR[:, 0:128], IDf[0:64, 0:64])
            te.transpose(PB[6][:, 64:128], k.BUPR[:, 128:256], IDf[0:64, 0:64])
            te.transpose(PB[4][:, 0:64], k.BGTR[:, 0:128], IDf[0:64, 0:64])
            ins = te.transpose(PB[4][:, 64:128], k.BGTR[:, 128:256], IDf[0:64, 0:64])
            k.inc(ins, "T")

        @blk.vector
        def _(ve):
            k.w(ve, "T")
            ve.tensor_copy(k.HFB[:, :], PB[3][:, 0:1])
            ve.tensor_copy(k.BRB[:, :], PB[5][:, 0:64])
            ve.tensor_copy(k.BUP[:, :], PB[6][:, 0:128])
            ins = ve.tensor_copy(k.BGT[:, :], PB[4][:, 0:128])
            k.inc(ins, "V")
        for tt in range(16):
            @blk.sync
            def _(sy, tt=tt):
                k.w(sy, "V")
                ins = sy.dma_start(out=AFA[:, 0:1024],
                                   in_=x_in[tt * 128:(tt + 1) * 128, :])
                k.inc(ins, "DY", 16)

            for half in range(2):
                @blk.tensor
                def _(te, tt=tt, half=half):
                    if half == 0:
                        k.w(te, "DY")
                    else:
                        k.w(te, "V")
                    for q in range(4):
                        fc = half * 4 + q
                        ins = te.transpose(PB[q][:, 0:128],
                                           AFA[:, fc * 128:(fc + 1) * 128],
                                           IDf[:, :])
                    k.inc(ins, "T")

                @blk.vector
                def _(ve, tt=tt, half=half):
                    k.w(ve, "T")
                    for q in range(4):
                        fc = half * 4 + q
                        ins = ve.tensor_copy(
                            XT[:, fc * 2048 + tt * 128: fc * 2048 + (tt + 1) * 128],
                            PB[q][:, 0:128])
                    k.inc(ins, "V")

        subs = [s for s in (_os.environ.get("K_SUBS") or
                            "f0,a0,f0,F1,A1,F1").split(",") if s]
        if _STOP == "load":
            subs = []
        nf0 = 0
        for s in subs:
            if s == "f0":
                l0ffn_emit(k, blk, nf0)
                nf0 += 1
            elif s == "a0":
                attn_emit(k, blk, 0)
            elif s == "A1":
                attn_emit(k, blk, 1)
            elif s == "F1":
                l1ffn_emit(k, blk, 0)
            else:
                raise ValueError(f"unknown sublayer {s}")
        if _DBG:
            def dbgf(sy):
                k.w(sy, "V")
                k.w(sy, "T")
                k.w(sy, "S")
                dd = k.dbg_d.ap()
                ins = sy.dma_start(out=dd[0:64, 0:256], in_=k.GSA[:, :])
                k.inc(ins, "DY", 16)
                ins = sy.dma_start(out=dd[0:64, 256:512], in_=k.GMA[:, :])
                k.inc(ins, "DY", 16)
                ins = sy.dma_start(out=dd[0:64, 512:532], in_=k.RS[:, :])
                k.inc(ins, "DY", 16)
                ins = sy.dma_start(out=dd[0:128, 1024:2048], in_=AFA[:, 3072:4096])
                k.inc(ins, "DY", 16)
                ins = sy.dma_start(out=dd[0:128, 2048:3072], in_=AFA[:, 2048:3072])
                k.inc(ins, "DY", 16)
                ins = sy.dma_start(out=dd[0:128, 3072:3136], in_=k.THB[:, :])
                k.inc(ins, "DY", 16)
                ins = sy.dma_start(out=k.dbg2_d.ap(), in_=k.AB[:, 0:16384])
                k.inc(ins, "DY", 16)
                ins = sy.dma_start(out=dd[0:128, 4096:4227], in_=AFA[:, 5120:5251])
                k.inc(ins, "DY", 16)
            blk.sync(dbgf)

        for tt in range(16):
            for half in range(2):
                @blk.tensor
                def _(te, tt=tt, half=half):
                    k.w(te, "V")
                    for q in range(4):
                        fc = half * 4 + q
                        ins = te.transpose(PB[q][:, 0:128],
                                           XT[:, fc * 2048 + tt * 128: fc * 2048 + (tt + 1) * 128],
                                           IDf[:, :])
                    k.inc(ins, "T")

                @blk.vector
                def _(ve, tt=tt, half=half):
                    k.w(ve, "T")
                    k.w(ve, "DY")
                    for q in range(4):
                        fc = half * 4 + q
                        ins = ve.tensor_copy(AFA[:, fc * 128:(fc + 1) * 128],
                                             PB[q][:, 0:128])
                    k.inc(ins, "V")

            @blk.sync
            def _(sy, tt=tt):
                k.w(sy, "V")
                ins = sy.dma_start(out=out[tt * 128:(tt + 1) * 128, :],
                                   in_=AFA[:, 0:1024])
                k.inc(ins, "DY", 16)

        @blk.sync
        def _(sy):
            k.w(sy, "DY")

    return nc


def _f32c(a):
    return np.ascontiguousarray(np.asarray(a, dtype=np.float32))


def kernel(**inputs):
    x = np.asarray(inputs["L0_attn_x"], dtype=np.float32)
    x_shards = np.ascontiguousarray(x.reshape(NCORE, TC, D))

    def gi(name, shape):
        a = inputs.get(name)
        if a is None:
            return np.zeros(shape, np.float32)
        return _f32c(a).reshape(shape)

    nc = build(0)
    shared = {
        "l0f_wr": gi("L0_ffn_Wr", (D, 64)),
        "l0f_br": gi("L0_ffn_br", (1, 64)),
        "l0f_wup": gi("L0_ffn_Wup", (64, D, 256)),
        "l0f_bup": gi("L0_ffn_bup", (64, 256)),
        "l0f_wgt": gi("L0_ffn_Wgate", (64, D, 256)),
        "l0f_bgt": gi("L0_ffn_bgate", (64, 256)),
        "l0f_wdn": gi("L0_ffn_Wdown", (64, 256, D)),
        "l0f_bdn": gi("L0_ffn_bdown", (64, D)),
        "l0a_wq": gi("L0_attn_Wq", (D, D)),
        "l0a_bq": gi("L0_attn_bq", (8, 128)),
        "l0a_wl": gi("L0_attn_Wl", (D, 256)),
        "l0a_bl": gi("L0_attn_bl", (2, 128)),
        "l0a_wk": gi("L0_attn_Wk", (256, D)),
        "l0a_bk": gi("L0_attn_bk", (8, 128)),
        "l0a_wv": gi("L0_attn_Wv", (256, D)),
        "l0a_bv": gi("L0_attn_bv", (16, 64)),
        "l0a_wo": gi("L0_attn_Wo", (D, D)),
        "l0a_bo": gi("L0_attn_bo", (8, 128)),
        "l1a_wq": gi("L1_attn_Wq", (D, D)),
        "l1a_wk": gi("L1_attn_Wk", (D, D)),
        "l1a_wv": gi("L1_attn_Wv", (D, D)),
        "l1a_wo": gi("L1_attn_Wo", (D, D)),
        "l1a_bo": gi("L1_attn_bo", (8, 128)),
        "l1f_wg": gi("L1_ffn_w_gate", (2816, D)),
        "l1f_wu": gi("L1_ffn_w_up", (2816, D)),
        "l1f_wd": gi("L1_ffn_w_down", (D, 2816)),
    }
    in_maps = []
    for c in range(NCORE):
        m = dict(shared)
        m["x"] = x_shards[c]
        m["cinfo"] = np.asarray([[float(c % 2)]], np.float32)
        in_maps.append(m)
    res = run_bass_kernel_spmd(nc, in_maps, core_ids=list(range(NCORE)))
    out = np.concatenate([r["out"] for r in res.results], axis=0)
    if _DBG:
        kernel._dbg = [r["dbg"] for r in res.results]
        kernel._dbg2 = [r["dbg2"] for r in res.results]
    return out.reshape(4, 4096, 1024)


if __name__ == "__main__":
    rng = np.random.default_rng(0)
    x = rng.standard_normal((4, 4096, 1024)).astype(np.float32)
    y = kernel(L0_attn_x=x)
    print("identity check:", np.abs(y - x).max())


# version 07
# speedup vs baseline: 1.1638x; 1.1638x over previous
import math
import numpy as np
import ml_dtypes
import concourse.bass as bass
import concourse.mybir as mybir
from concourse.bass_utils import run_bass_kernel_spmd

F32 = mybir.dt.float32
BF16 = mybir.dt.bfloat16
ALU = mybir.AluOpType
AF = mybir.ActivationFunctionType
AX = mybir.AxisListType

NCORE = 8
TC = 2048
D = 1024
NCH = D // 128

import os as _os
_STOP = _os.environ.get("K_STOP") or None
_DBG = False

CONST_VALS = [1e-5, -0.5, 0.5, 1e-6, 2.0, -1.0, -15.0]


class Ctx:

    def __init__(self, nc):
        self.nc = nc
        self.c = {}
        self.sem = {}
        for name in ["T", "V", "S", "G", "Y", "DY", "DS", "DG", "CC"]:
            self.sem[name] = nc.alloc_semaphore(name)
            self.c[name] = 0

    def bump(self, name, n=1):
        self.c[name] += n
        return self.c[name]

    def w(self, eng, name, val=None):
        eng.wait_ge(self.sem[name], self.c[name] if val is None else val)

    def inc(self, instr, name, n=1):
        instr.then_inc(self.sem[name], n)
        self.bump(name, n)
        return instr

    def vb(self, ve, ins):
        self.inc(ins, "V")
        self.w(ve, "V")


def ln_emit(k, blk, XT, XN, PB, AFA, tag):
    nc = k.nc
    ONES = k.ONESbf
    ONESf = k.ONESf
    for tc in range(4):
        t0 = tc * 512

        @blk.scalar
        def _(sc, tc=tc, t0=t0):
            k.w(sc, "V")
            k.w(sc, "T")
            for fc in range(8):
                ins = sc.activation(out=AFA[:, fc * 512:(fc + 1) * 512],
                                    in_=XT[:, fc * 2048 + t0: fc * 2048 + t0 + 512],
                                    func=AF.Square)
                if fc == 7:
                    k.inc(ins, "S")

        @blk.tensor
        def _(te, tc=tc, t0=t0):
            k.w(te, "S")
            for fc in range(8):
                te.matmul(PB[0][:, :], ONESf[:, :],
                          XT[:, fc * 2048 + t0: fc * 2048 + t0 + 512],
                          start=(fc == 0), stop=(fc == 7))
            for fc in range(8):
                ins = te.matmul(PB[1][:, :], ONESf[:, :],
                                AFA[:, fc * 512:(fc + 1) * 512],
                                start=(fc == 0), stop=(fc == 7))
            k.inc(ins, "T")

        @blk.vector
        def _(ve, tc=tc, t0=t0):
            k.w(ve, "T")
            MB = AFA[:, 4096:4608]
            VR = AFA[:, 4608:5120]
            ve.tensor_scalar(MB, PB[0][:, :], 1.0 / 1024.0, None, ALU.mult)
            ve.tensor_tensor(VR, MB, MB, ALU.mult)
            ins = ve.scalar_tensor_tensor(VR, PB[1][:, :], 1.0 / 1024.0, VR,
                                          ALU.mult, ALU.subtract)
            k.inc(ins, "V")

        @blk.scalar
        def _(sc, tc=tc, t0=t0):
            k.w(sc, "V")
            VR = AFA[:, 4608:5120]
            sc.activation(out=VR, in_=VR, func=AF.Ln, bias=k.cc(1e-5), scale=1.0)
            ins = sc.activation(out=VR, in_=VR, func=AF.Exp, scale=k.cc(-0.5))
            k.inc(ins, "S")

        @blk.vector
        def _(ve, tc=tc, t0=t0):
            k.w(ve, "S")
            MB = AFA[:, 4096:4608]
            VR = AFA[:, 4608:5120]
            TMP = AFA[:, 5120:5632]
            for fc in range(8):
                sl = slice(fc * 2048 + t0, fc * 2048 + t0 + 512)
                ve.tensor_tensor(TMP, XT[:, sl], MB, ALU.subtract)
                ins = ve.tensor_tensor(XN[:, sl], TMP, VR, ALU.mult)
            k.inc(ins, "V")





def sort_network_ops():
    ops = []

    def cmp(pb, pc, a_off, b_off, dims, a_rev=False, b_rev=False):
        ops.append(("cmp", dict(pb=pb, pc=pc, a_off=a_off, b_off=b_off,
                                dims=tuple(dims), a_rev=a_rev, b_rev=b_rev)))

    def cmp2(a_pb, b_pb, pc, b_rev=False):
        ops.append(("cmp2", dict(a_pb=a_pb, b_pb=b_pb, pc=pc, a_rev=False,
                                 b_rev=b_rev)))

    def apply_perm_groups(gsrc):
        lane_map = [0] * 128
        for g in range(16):
            for c in range(8):
                lane_map[g * 8 + c] = gsrc[g] * 8 + c
        ops.append(("perm", tuple(lane_map)))

    n = 1024
    for kk in [2 ** i for i in range(10)]:
        cmp(0, 128, 0, kk, [(2 * kk, n // (2 * kk)), (1, kk)], b_rev=True)
        d = kk // 2
        while d >= 1:
            cmp(0, 128, 0, d, [(2 * d, n // (2 * d)), (1, d)])
            d //= 2
    apply_perm_groups([2 * m for m in range(8)] + [2 * m + 1 for m in range(8)])
    cmp2(0, 64, 64, b_rev=True)
    for d in [512, 256, 128, 64, 32, 16, 8, 4, 2, 1]:
        cmp(0, 128, 0, d, [(2 * d, 1024 // (2 * d)), (1, d)])
    gsrc = [0] * 16
    for m in range(4):
        for u in range(2):
            gsrc[u * 4 + m] = u * 8 + 2 * m
            gsrc[8 + u * 4 + m] = (1 - u) * 8 + 2 * m + 1
    apply_perm_groups(gsrc)
    cmp2(0, 64, 64, b_rev=True)
    cmp2(0, 64, 64)
    cmp2(0, 32, 32)
    cmp2(64, 96, 32)
    for d in [512, 256, 128, 64, 32, 16, 8, 4, 2, 1]:
        cmp(0, 128, 0, d, [(2 * d, 1024 // (2 * d)), (1, d)])
    gsrc = [0] * 16
    for a_i, r in ((0, 0), (1, 2)):
        for q in range(4):
            gsrc[q * 2 + a_i] = (q % 2) * 4 + r + (q // 2) * 8
    for a_i, r in ((0, 1), (1, 3)):
        for w in range(4):
            gsrc[8 + (3 - w) * 2 + a_i] = (w % 2) * 4 + r + (w // 2) * 8
    apply_perm_groups(gsrc)
    cmp2(0, 64, 64, b_rev=True)
    cmp2(0, 64, 64)
    cmp2(0, 32, 32)
    cmp2(64, 96, 32)
    gsrc = [0] * 16
    for a_i in range(2):
        for q in range(4):
            gsrc[(q % 2) * 4 + (q // 2) * 2 + a_i] = q * 2 + a_i
        for j in range(4):
            gsrc[8 + (j % 2) * 4 + (j // 2) * 2 + a_i] = 8 + j * 2 + a_i
    apply_perm_groups(gsrc)
    cmp2(0, 32, 32)
    cmp2(64, 96, 32)
    for d in [512, 256, 128, 64, 32, 16, 8, 4, 2, 1]:
        cmp(0, 128, 0, d, [(2 * d, 1024 // (2 * d)), (1, d)])
    gsrc = [0] * 16
    for q in range(8):
        g = ((q % 2) * 4 + ((q // 2) % 2) * 2 + 0) + (8 if q >= 4 else 0)
        gsrc[q] = g
    for w in range(8):
        g = ((w % 2) * 4 + ((w // 2) % 2) * 2 + 1) + (8 if w >= 4 else 0)
        gsrc[8 + (7 - w)] = g
    apply_perm_groups(gsrc)
    cmp2(0, 64, 64, b_rev=True)
    cmp2(0, 64, 64)
    cmp2(0, 32, 32)
    cmp2(64, 96, 32)
    gsrc = [0] * 16
    for q in range(16):
        gsrc[(q % 4) * 4 + q // 4] = q
    apply_perm_groups(gsrc)
    cmp2(0, 64, 64)
    cmp2(0, 32, 32)
    cmp2(64, 96, 32)
    for d in [512, 256, 128, 64, 32, 16, 8, 4, 2, 1]:
        cmp(0, 128, 0, d, [(2 * d, 1024 // (2 * d)), (1, d)])
    fmap = [(q % 4) * 4 + q // 4 for q in range(16)]
    return ops, fmap


def emit_sort(k, blk, SA, SB_, SCR):
    ops, fmap = sort_network_ops()
    cur, oth = SA, SB_

    def vsec(f):
        blk.vector(f)

    def ysec(f):
        blk.sync(f)

    i = 0
    while i < len(ops):
        op = ops[i]
        if op[0] == "perm":
            lane_map = list(op[1])
            _c, _o = cur, oth

            def yf(sy, lane_map=lane_map, _c=_c, _o=_o):
                k.w(sy, "V")
                for g in range(16):
                    s0 = lane_map[g * 8]
                    ins = sy.dma_start(out=_o[g * 8:(g + 1) * 8, :],
                                       in_=_c[s0:s0 + 8, :])
                    k.bump("DY", 16)
                    ins.then_inc(k.sem["DY"], 16)
            ysec(yf)

            def vf(ve):
                k.w(ve, "DY")
                ins = ve.nop()
                k.inc(ins, "V")
            vsec(vf)
            cur, oth = oth, cur
            i += 1
            continue
        if op[0] == "cmp":
            d = op[1]
            _c, _o = cur, oth

            def vf(ve, d=d, _c=_c, _o=_o):
                (so, co), (si, ci) = d["dims"]
                va = _c.rearrange("p (b x) -> p b x", x=so)
                vo = _o.rearrange("p (b x) -> p b x", x=so)

                def sl(v, off, rev, ci=ci):
                    if rev:
                        if off == 0:
                            return v[:, :, ci - 1::-1]
                        return v[:, :, off + ci - 1:off - 1:-1]
                    return v[:, :, off:off + ci]
                a_off, b_off, brev = d["a_off"], d["b_off"], d["b_rev"]
                ve.tensor_tensor(sl(vo, a_off, False), sl(va, a_off, False),
                                 sl(va, b_off, brev), ALU.min)
                ins = ve.tensor_tensor(sl(vo, b_off, False), sl(va, a_off, brev),
                                       sl(va, b_off, False), ALU.max)
                k.inc(ins, "V")
            vsec(vf)
            cur, oth = oth, cur
            i += 1
            continue
        d = op[1]
        group = [d]
        if d["pc"] == 32:
            d2 = ops[i + 1][1]
            assert ops[i + 1][0] == "cmp2" and d2["pc"] == 32
            group.append(d2)
            i += 2
        else:
            i += 1
        _c, _o = cur, oth

        def vf(ve, group=group, _c=_c, _o=_o):
            for d in group:
                apb, bpb, pc, brev = d["a_pb"], d["b_pb"], d["pc"], d["b_rev"]
                scr = SCR[apb:apb + pc, :]
                if brev:
                    ve.tensor_copy(scr, _c[bpb:bpb + pc, 1023::-1])
                else:
                    ve.tensor_copy(scr, _c[bpb:bpb + pc, :])
                ve.tensor_tensor(_o[apb:apb + pc, :], _c[apb:apb + pc, :],
                                 scr, ALU.min)
                if brev:
                    ins = ve.tensor_tensor(_o[bpb:bpb + pc, :],
                                           _c[apb:apb + pc, 1023::-1],
                                           SCR[apb:apb + pc, 1023::-1], ALU.max)
                else:
                    ins = ve.tensor_tensor(_o[bpb:bpb + pc, :],
                                           _c[apb:apb + pc, :], scr, ALU.max)
            k.inc(ins, "V")
        vsec(vf)
        cur, oth = oth, cur
    return cur, fmap



def l0ffn_emit(k, blk, ci):
    nc = k.nc
    XT, XN, AB, AFA = k.XT, k.XN, k.AB, k.AFA
    PB = k.PB
    ln_emit(k, blk, XT, XN, PB, AFA, f"l0ffn{ci}_ln")

    STOKf = AFA[:, 2048:3072]
    for tt in range(16):
        def tf(te, tt=tt):
            k.w(te, "V")
            for fc in range(8):
                ins = te.matmul(PB[0][:, 0:64], XN[:, fc * 2048 + tt * 128: fc * 2048 + (tt + 1) * 128],
                                k.WR[:, fc * 64:(fc + 1) * 64],
                                start=(fc == 0), stop=(fc == 7))
            k.inc(ins, "T")
        blk.tensor(tf)

        def vf(ve, tt=tt):
            k.w(ve, "T")
            SM = AFA[:, 5120:5184]
            NM = AFA[:, 5184:5185]
            ve.tensor_tensor(SM, PB[0][:, 0:64], k.BRB[:, :], ALU.add)
            ins = ve.tensor_reduce(NM, SM, AX.X, ALU.max, negate=True)
            k.inc(ins, "V")
        blk.vector(vf)

        def sf(sc, tt=tt):
            k.w(sc, "V")
            SM = AFA[:, 5120:5184]
            SE = AFA[:, 5185:5249]
            NM = AFA[:, 5184:5185]
            SSUM = AFA[:, 5249:5250]
            ins = sc.activation(out=SE, in_=SM, func=AF.Exp, bias=NM,
                                scale=1.0)
            k.inc(ins, "S")
        blk.scalar(sf)

        def vf2(ve, tt=tt):
            k.w(ve, "S")
            SE = AFA[:, 5185:5249]
            SSUM = AFA[:, 5249:5250]
            RC = AFA[:, 5250:5251]
            k.vb(ve, ve.tensor_reduce(SSUM, SE, AX.X, ALU.add))
            k.vb(ve, ve.reciprocal(RC, SSUM))
            ins = ve.tensor_tensor(STOKf[:, tt * 64:(tt + 1) * 64], SE,
                                   RC.to_broadcast([128, 64]), ALU.mult)
            k.inc(ins, "V")
        blk.vector(vf2)

        def tf2(te, tt=tt):
            k.w(te, "V")
            k.w(te, "DY")
            ins = te.transpose(PB[1][0:64, 0:128],
                               STOKf[:, tt * 64:(tt + 1) * 64], k.IDfs[:, :])
            k.inc(ins, "T")
        blk.tensor(tf2)

        def vf3(ve, tt=tt):
            k.w(ve, "T")
            ins = ve.tensor_copy(AFA[0:64, 5251:5379], PB[1][0:64, 0:128])
            k.inc(ins, "V")
        blk.vector(vf3)

        def yf(sy, tt=tt):
            k.w(sy, "V")
            ins = sy.dma_start(out=k.sc_in.ap()[:, tt * 128:(tt + 1) * 128],
                               in_=AFA[0:64, 5251:5379])
            k.inc(ins, "DY", 16)
        blk.sync(yf)

    if _STOP == "router":
        return
    def gf(gp):
        k.w(gp, "DY")
        ins = gp.collective_compute(
            "AllGather", ALU.bypass, replica_groups=[list(range(8))],
            ins=[k.sc_in.ap().opt()], outs=[k.sc_out.ap().opt()])
        k.inc(ins, "CC")
    blk.gpsimd(gf)

    def yf2(sy):
        k.w(sy, "CC")
        k.w(sy, "V")
        for jr in range(2):
            src3 = k.sc_out.ap()[:, jr * 1024:(jr + 1) * 1024].rearrange(
                "(j2 e) c -> j2 e c", j2=8)[:, bass.ds(k.pid * 8, 8)]
            dst3 = k.sc_my.ap().rearrange("(j2 r) c -> j2 r c",
                                          r=16)[:, jr * 8:(jr + 1) * 8, :]
            ins = sy.dma_start(out=dst3, in_=src3)
            k.inc(ins, "DY", 16)
        sy.wait_ge(k.sem["DY"], k.c["DY"])
        ins = sy.dma_start(out=AFA[:, 0:1024], in_=k.sc_my.ap())
        k.inc(ins, "DY", 16)
    blk.sync(yf2)

    def vf4(ve):
        k.w(ve, "DY")
        ins = ve.nop()
        k.inc(ins, "V")
    blk.vector(vf4)

    SRT, fmap = emit_sort(k, blk, AFA[:, 0:1024], AFA[:, 1024:2048],
                          AFA[:, 5120:6144])

    if _STOP == "sort":
        return
    def vf5(ve):
        seg = SRT.rearrange("p (s x) -> p s x", x=64)
        ve.tensor_reduce(k.GSseg[:, :], seg, AX.X, ALU.add)
        ins = ve.tensor_reduce(k.GMseg[:, :], seg, AX.X, ALU.min)
        k.inc(ins, "V")
    blk.vector(vf5)

    def yf3(sy):
        k.w(sy, "V")
        for off, seg_t in ((0, k.GSseg), (256, k.GMseg)):
            for gh in range(4):
                o4 = k.gg_in.ap()[:, off:off + 256].rearrange(
                    "c (gl g2 s) -> gl g2 c s", gl=4, g2=4)[:, gh]
                ins = sy.dma_start(out=o4, in_=seg_t[gh * 32:(gh + 1) * 32, :])
                k.inc(ins, "DY", 16)
    blk.sync(yf3)

    def gf2(gp):
        k.w(gp, "DY")
        ins = gp.collective_compute(
            "AllGather", ALU.bypass, replica_groups=[list(range(8))],
            ins=[k.gg_in.ap().opt()], outs=[k.gg_out.ap().opt()])
        k.inc(ins, "CC")
    blk.gpsimd(gf2)

    def yf4(sy):
        k.w(sy, "CC")
        ins = sy.dma_start(out=k.GSA[:, :], in_=k.gg_out.ap()[:, 0:256])
        k.inc(ins, "DY", 16)
        ins = sy.dma_start(out=k.GMA[:, :], in_=k.gg_out.ap()[:, 256:512])
        k.inc(ins, "DY", 16)
    blk.sync(yf4)

    if _STOP == "gs":
        return
    RS = k.RS
    NR = 44

    def vf6(ve):
        k.w(ve, "DY")
        ve.memset(RS[:, 0:1], 1e-38)
        ins = ve.memset(RS[:, 1:2], 64.0)
        k.inc(ins, "V")
    blk.vector(vf6)

    for r in range(NR):
        def vf7(ve, r=r):
            k.w(ve, "T")
            k.vb(ve, ve.tensor_tensor(RS[:, 2:3], RS[:, 0:1],
                                       k.CKL[:, r:r + 1], ALU.mult))
            ve.tensor_tensor(AFA[0:64, 5120:5376], k.GSA[:, :],
                             RS[:, 2:3].to_broadcast([64, 256]), ALU.is_gt)
            ins = ve.tensor_reduce(RS[:, 16:17], AFA[0:64, 5120:5376],
                                   AX.X, ALU.add)
            k.inc(ins, "V")
        blk.vector(vf7)

        def tf3(te, r=r):
            k.w(te, "V")
            ins = te.matmul(PB[0][0:64, 0:1], k.ONESf[0:64, 0:64],
                            RS[:, 16:17], start=True, stop=True)
            k.inc(ins, "T")
        blk.tensor(tf3)

        def vf8(ve, r=r):
            k.w(ve, "T")
            k.vb(ve, ve.tensor_scalar(RS[:, 3:4], PB[0][0:64, 0:1], 4096.0,
                                       None, ALU.is_ge))
            k.vb(ve, ve.tensor_tensor(RS[:, 4:5], RS[:, 2:3], RS[:, 0:1], ALU.subtract))
            k.vb(ve, ve.tensor_tensor(RS[:, 4:5], RS[:, 4:5], RS[:, 3:4], ALU.mult))
            k.vb(ve, ve.tensor_tensor(RS[:, 0:1], RS[:, 0:1], RS[:, 4:5], ALU.add))
            k.vb(ve, ve.tensor_tensor(RS[:, 4:5], RS[:, 1:2], RS[:, 2:3], ALU.subtract))
            k.vb(ve, ve.tensor_tensor(RS[:, 4:5], RS[:, 4:5], RS[:, 3:4], ALU.mult))
            ins = ve.tensor_tensor(RS[:, 1:2], RS[:, 2:3], RS[:, 4:5], ALU.add)
            k.inc(ins, "V")
        blk.vector(vf8)

    def vf9(ve):
        k.w(ve, "V")
        TMP = AFA[0:64, 5120:5376]
        ve.tensor_tensor(TMP, k.GSA[:, :], RS[:, 1:2].to_broadcast([64, 256]),
                         ALU.is_le)
        ve.tensor_tensor(TMP, TMP, k.GSA[:, :], ALU.mult)
        ins = ve.tensor_reduce(RS[:, 15:16], TMP, AX.X, ALU.max)
        k.inc(ins, "V")
    blk.vector(vf9)

    def tf4(te):
        k.w(te, "V")
        ins = te.transpose(PB[0][0:1, 0:64], RS[:, 15:16], k.IDfs[0:64, 0:64])
        k.inc(ins, "T")
    blk.tensor(tf4)

    def vf10(ve):
        k.w(ve, "T")
        ins = ve.tensor_reduce(AFA[0:1, 5120:5121], PB[0][0:1, 0:64], AX.X, ALU.max)
        k.inc(ins, "V")
    blk.vector(vf10)

    def tf5(te):
        k.w(te, "V")
        ins = te.matmul(PB[1][0:64, 0:1], k.ONESf[0:1, 0:64],
                        AFA[0:1, 5120:5121], start=True, stop=True)
        k.inc(ins, "T")
    blk.tensor(tf5)

    def vf11(ve):
        k.w(ve, "T")
        TMP = AFA[0:64, 5120:5376]
        ins = ve.tensor_copy(RS[:, 5:6], PB[1][0:64, 0:1])
        k.inc(ins, "V")
        k.w(ve, "V")
        ve.tensor_tensor(TMP, k.GSA[:, :], RS[:, 5:6].to_broadcast([64, 256]),
                         ALU.is_gt)
        ve.tensor_reduce(RS[:, 6:7], TMP, AX.X, ALU.add)
        ve.tensor_tensor(TMP, k.GSA[:, :], RS[:, 5:6].to_broadcast([64, 256]),
                         ALU.is_equal)
        ins = ve.tensor_reduce(RS[:, 7:8], TMP, AX.X, ALU.add)
        k.inc(ins, "V")
    blk.vector(vf11)

    def tf6(te):
        k.w(te, "V")
        ins = te.matmul(PB[0][0:64, 0:1], k.ONESf[0:64, 0:64],
                        RS[:, 6:7], start=True, stop=True)
        k.inc(ins, "T")
    blk.tensor(tf6)

    def vf12(ve):
        k.w(ve, "T")
        k.w(ve, "V")
        k.vb(ve, ve.tensor_scalar(RS[:, 8:9], PB[0][0:64, 0:1], -1.0, 4096.0,
                                  ALU.mult, ALU.add))
        k.vb(ve, ve.tensor_scalar(RS[:, 9:10], RS[:, 6:7], 64.0, None, ALU.mult))
        k.vb(ve, ve.tensor_tensor(RS[:, 9:10], RS[:, 9:10], k.EIDX[:, :], ALU.add))
        ins = ve.tensor_scalar(RS[:, 10:11], RS[:, 7:8], 1.0, None, ALU.is_ge)
        k.inc(ins, "V")
    blk.vector(vf12)

    def tf7(te):
        k.w(te, "V")
        te.transpose(PB[0][0:1, 0:64], RS[:, 9:10], k.IDfs[0:64, 0:64])
        ins = te.transpose(PB[0][0:1, 64:128], RS[:, 10:11], k.IDfs[0:64, 0:64])
        k.inc(ins, "T")
    blk.tensor(tf7)

    def vf13(ve):
        k.w(ve, "T")
        ins = ve.tensor_copy(AFA[0:1, 5120:5248], PB[0][0:1, 0:128])
        k.inc(ins, "V")
    blk.vector(vf13)

    def tf8(te):
        k.w(te, "V")
        ins = te.matmul(PB[1][0:64, 0:128], k.ONESf[0:1, 0:64],
                        AFA[0:1, 5120:5248], start=True, stop=True)
        k.inc(ins, "T")
    blk.tensor(tf8)

    def vf14(ve):
        k.w(ve, "T")
        k.w(ve, "V")
        CMP = AFA[0:64, 5376:5440]
        ve.tensor_tensor(CMP, PB[1][0:64, 0:64], RS[:, 9:10].to_broadcast([64, 64]),
                         ALU.is_lt)
        ve.tensor_tensor(CMP, CMP, PB[1][0:64, 64:128], ALU.mult)
        k.vb(ve, ve.tensor_reduce(RS[:, 11:12], CMP, AX.X, ALU.add))
        k.vb(ve, ve.tensor_tensor(RS[:, 12:13], RS[:, 11:12], RS[:, 8:9], ALU.is_lt))
        k.vb(ve, ve.tensor_tensor(RS[:, 12:13], RS[:, 12:13], RS[:, 10:11], ALU.mult))
        k.vb(ve, ve.tensor_tensor(RS[:, 13:14], RS[:, 6:7], RS[:, 12:13], ALU.add))
        k.vb(ve, ve.tensor_scalar(RS[:, 14:15], RS[:, 13:14], -1.0, 256.0,
                                  ALU.mult, ALU.add))
        MSK = AFA[0:64, 5376:5632]
        ve.tensor_tensor(MSK, k.IOTA256[:, :], RS[:, 14:15].to_broadcast([64, 256]),
                         ALU.is_lt)
        ve.tensor_scalar(MSK, MSK, 1e30, None, ALU.mult)
        ve.tensor_tensor(MSK, MSK, k.GMA[:, :], ALU.add)
        ins = ve.tensor_reduce(RS[:, 5:6], MSK, AX.X, ALU.min)
        k.inc(ins, "V")
    blk.vector(vf14)

    def tf9(te):
        k.w(te, "V")
        ins = te.transpose(PB[0][0:1, 0:64], RS[:, 5:6], k.IDfs[0:64, 0:64])
        k.inc(ins, "T")
    blk.tensor(tf9)

    def vf15(ve):
        k.w(ve, "T")
        ins = ve.tensor_copy(AFA[0:1, 5120:5184], PB[0][0:1, 0:64])
        k.inc(ins, "V")
    blk.vector(vf15)

    def tf10(te):
        k.w(te, "V")
        ins = te.matmul(PB[1][:, 0:64], k.ONESf[0:1, :],
                        AFA[0:1, 5120:5184], start=True, stop=True)
        k.inc(ins, "T")
    blk.tensor(tf10)

    def vf16(ve):
        k.w(ve, "T")
        ins = ve.tensor_copy(k.THB[:, :], PB[1][:, 0:64])
        k.inc(ins, "V")
    blk.vector(vf16)

    FSTf = AFA[:, 3072:4096]
    for tt in range(16):
        def vf17(ve, tt=tt):
            if tt == 0:
                k.w(ve, "V")
            MS = AFA[:, 5120:5184]
            SUM = AFA[:, 5184:5185]
            RC = AFA[:, 5185:5186]
            sl = slice(tt * 64, (tt + 1) * 64)
            ve.tensor_tensor(MS, STOKf[:, sl], k.THB[:, :], ALU.is_ge)
            ve.tensor_tensor(FSTf[:, sl], MS, STOKf[:, sl], ALU.mult)
            k.vb(ve, ve.tensor_reduce(SUM, FSTf[:, sl], AX.X, ALU.add))
            k.vb(ve, ve.tensor_scalar(SUM, SUM, 1e-6, None, ALU.add))
            k.vb(ve, ve.reciprocal(RC, SUM))
            ins = ve.tensor_tensor(FSTf[:, sl], FSTf[:, sl],
                                   RC.to_broadcast([128, 64]), ALU.mult)
            k.inc(ins, "V")
        blk.vector(vf17)

        def tf11(te, tt=tt):
            k.w(te, "V")
            ins = te.transpose(PB[1][0:64, 0:128], FSTf[:, tt * 64:(tt + 1) * 64],
                               k.IDfs[:, :])
            k.inc(ins, "T")
        blk.tensor(tf11)

        def vf18(ve, tt=tt):
            k.w(ve, "T")
            ins = ve.tensor_copy(k.FSTT[:, tt * 128:(tt + 1) * 128],
                                 PB[1][0:64, 0:128])
            k.inc(ins, "V")
        blk.vector(vf18)

    if _STOP == "fs":
        return
    moe_emit(k, blk, ci)


def precast_emit(k, blk):
    def gpc(gp):
        for e in range(64):
            ins = gp.dma_start(
                out=k.wup_b.ap()[e].rearrange("p (kc h) -> p kc h", h=256),
                in_=k.wup_d.ap()[e].rearrange("(kc p) h -> p kc h", p=128))
            k.inc(ins, "G", 16)
            ins = gp.dma_start(
                out=k.wgt_b.ap()[e].rearrange("p (kc h) -> p kc h", h=256),
                in_=k.wgt_d.ap()[e].rearrange("(kc p) h -> p kc h", p=128))
            k.inc(ins, "G", 16)
            ins = gp.dma_start(
                out=k.wdn_b.ap()[e].rearrange("p (c d) -> p c d", d=1024),
                in_=k.wdn_d.ap()[e].rearrange("(c p) d -> p c d", p=128))
            k.inc(ins, "G", 16)
    blk.gpsimd(gpc)


def moe_emit(k, blk, ci):
    nc = k.nc
    XT, XN, AB, AFA = k.XT, k.XN, k.AB, k.AFA
    PB = k.PB
    WUP = [AB[:, 0:2048], AB[:, 2048:4096]]
    WGT = [AB[:, 4096:6144], AB[:, 6144:8192]]
    WDN = [AB[:, 8192:10240], AB[:, 10240:12288]]
    SU0 = AFA[:, 0:256]
    SU1 = AFA[:, 512:768]
    GF0 = AFA[:, 256:512]
    GF1 = AFA[:, 768:1024]
    HP = [AB[:, 14848:15360], AB[:, 15360:15872]]

    def wload(sy, it, e):
        b = it % 2
        sy.wait_ge(k.sem["G"], min(e + 1, 64) * 48)
        for buf, srcd in ((WUP[b], k.wup_b), (WGT[b], k.wgt_b),
                          (WDN[b], k.wdn_b)):
            ins = sy.dma_start(out=buf, in_=srcd.ap()[e])
            k.inc(ins, "DS", 16)

    def grep(gp):
        ins = gp.dma_start(out=k.REP, in_=k.REP_d.ap())
        k.inc(ins, "DG", 16)
    blk.gpsimd(grep)

    dg0 = k.c["DS"]
    rep_dg = k.c["DG"]
    NP = int(_os.environ.get("K_NP", "8"))
    EN = int(_os.environ.get("K_NE", "64"))
    if EN == 0:
        return

    def gpre(sy):
        wload(sy, 0, 0)
        wload(sy, 1, 1)
    blk.sync(gpre)

    if EN < 0:
        def vw(ve):
            ve.wait_ge(k.sem["DS"], dg0 + 96)
            ins = ve.nop()
            k.inc(ins, "V")
        blk.vector(vw)
        return

    for P in range(NP):
        t0 = P * 256
        for e in range(EN):
            it = P * EN + e
            b = it % 2
            T0, S0, V0 = k.c["T"], k.c["S"], k.c["V"]
            fsb_vwait = k.c["V"] if e == 0 else V0 - 2
            dg_wait = dg0 + 48 * (it + 1)

            def tf(te, P=P, e=e, b=b, t0=t0, T0=T0, S0=S0, V0=V0,
                   fsb_vwait=fsb_vwait, dg_wait=dg_wait):
                te.wait_ge(k.sem["DS"], dg_wait)
                te.wait_ge(k.sem["DG"], rep_dg)
                te.wait_ge(k.sem["V"], fsb_vwait)
                te.matmul(PB[2][:, 0:256],
                          k.REP[:, e * 128:(e + 1) * 128],
                          k.FSTT[:, t0:t0 + 256], start=True, stop=True)
                te.wait_ge(k.sem["S"], S0)
                for kc in range(8):
                    ins = te.matmul(PB[0][:, 0:256],
                              WUP[b][:, kc * 256: kc * 256 + 128],
                              XN[:, kc * 2048 + t0: kc * 2048 + t0 + 256],
                              start=(kc == 0), stop=(kc == 7))
                k.inc(ins, "T")
                for kc in range(8):
                    ins = te.matmul(PB[0][:, 256:512],
                              WUP[b][:, kc * 256 + 128: kc * 256 + 256],
                              XN[:, kc * 2048 + t0: kc * 2048 + t0 + 256],
                              start=(kc == 0), stop=(kc == 7))
                k.inc(ins, "T")
                te.wait_ge(k.sem["V"], V0)
                for kc in range(8):
                    ins = te.matmul(PB[1][:, 0:256],
                              WGT[b][:, kc * 256: kc * 256 + 128],
                              XN[:, kc * 2048 + t0: kc * 2048 + t0 + 256],
                              start=(kc == 0), stop=(kc == 7))
                k.inc(ins, "T")
                for kc in range(8):
                    ins = te.matmul(PB[1][:, 256:512],
                              WGT[b][:, kc * 256 + 128: kc * 256 + 256],
                              XN[:, kc * 2048 + t0: kc * 2048 + t0 + 256],
                              start=(kc == 0), stop=(kc == 7))
                k.inc(ins, "T")
                if e > 0:
                    ep = it - 1
                    hb, wb = HP[ep % 2], WDN[ep % 2]
                    for ehc in range(2):
                        for ds in range(8):
                            ins = te.matmul(PB[3 + ds // 2][:, (ds % 2) * 256:(ds % 2) * 256 + 256],
                                      wb[:, ehc * 1024 + ds * 128: ehc * 1024 + (ds + 1) * 128],
                                      hb[:, ehc * 256:(ehc + 1) * 256],
                                      start=False, stop=False)
                else:
                    for ds in range(8):
                        ins = te.matmul(PB[3 + ds // 2][:, (ds % 2) * 256:(ds % 2) * 256 + 256],
                                  k.BDN[:, ds * 128:(ds + 1) * 128],
                                  k.FSTT[:, t0:t0 + 256],
                                  start=(ds % 2 == 0), stop=False)
                k.inc(ins, "T")
            blk.tensor(tf)

            def sf(sc, e=e, T0=T0, V0=V0):
                sc.wait_ge(k.sem["T"], T0 + 2)
                sc.wait_ge(k.sem["V"], max(V0 - 3, 0))
                ins = sc.activation(out=SU0, in_=PB[0][:, 0:256],
                                    func=AF.Silu, bias=k.BUP[:, e:e + 1],
                                    scale=1.0)
                k.inc(ins, "S")
                sc.wait_ge(k.sem["V"], max(V0 - 2, 0))
                ins = sc.activation(out=SU1, in_=PB[0][:, 256:512],
                                    func=AF.Silu, bias=k.BUP[:, 64 + e:64 + e + 1],
                                    scale=1.0)
                k.inc(ins, "S")
            blk.scalar(sf)

            def vf(ve, e=e, it=it, T0=T0, S0=S0, V0=V0):
                ve.wait_ge(k.sem["S"], S0 + 1)
                ins = ve.tensor_tensor(GF0, SU0, PB[2][:, 0:256], ALU.mult)
                k.inc(ins, "V")
                ve.wait_ge(k.sem["S"], S0 + 2)
                ins = ve.tensor_tensor(GF1, SU1, PB[2][:, 0:256], ALU.mult)
                k.inc(ins, "V")
                ve.wait_ge(k.sem["T"], T0 + 4)
                ins = ve.scalar_tensor_tensor(HP[it % 2][:, 0:256], PB[1][:, 0:256],
                                              k.BGT[:, e:e + 1], GF0,
                                              ALU.add, ALU.mult)
                k.inc(ins, "V")
                ins = ve.scalar_tensor_tensor(HP[it % 2][:, 256:512], PB[1][:, 256:512],
                                              k.BGT[:, 64 + e:64 + e + 1], GF1,
                                              ALU.add, ALU.mult)
                k.inc(ins, "V")
            blk.vector(vf)

            if it + 2 < NP * EN:
                nit = it + 2
                dwait = T0 + 10 if e < EN - 1 else T0 + 6
                def gf3(sy, nit=nit, dwait=dwait):
                    sy.wait_ge(k.sem["T"], dwait)
                    wload(sy, nit, nit % EN)
                blk.sync(gf3)

        def tf12(te, P=P, t0=t0, itl=P * EN + EN - 1):
            k.w(te, "V")
            hb, wb = HP[itl % 2], WDN[itl % 2]
            for ehc in range(2):
                for ds in range(8):
                    ins = te.matmul(PB[3 + ds // 2][:, (ds % 2) * 256:(ds % 2) * 256 + 256],
                              wb[:, ehc * 1024 + ds * 128: ehc * 1024 + (ds + 1) * 128],
                              hb[:, ehc * 256:(ehc + 1) * 256],
                              start=False, stop=(ehc == 1 and ds % 2 == 1))
            k.inc(ins, "T")
        blk.tensor(tf12)

        def vfa(ve, P=P, t0=t0):
            k.w(ve, "T")
            for ds in range(8):
                xsl = slice(ds * 2048 + t0, ds * 2048 + t0 + 256)
                ins = ve.scalar_tensor_tensor(
                    XT[:, xsl], PB[3 + ds // 2][:, (ds % 2) * 256:(ds % 2) * 256 + 256],
                    0.5, XT[:, xsl], ALU.mult, ALU.add)
            k.inc(ins, "V")
        blk.vector(vfa)



def attn_emit(k, blk, layer):
    nc = k.nc
    XT, XN, AB, AFA, PB = k.XT, k.XN, k.AB, k.AFA, k.PB
    ln_emit(k, blk, XT, XN, PB, AFA, f"l{layer}a_ln")

    XOT = AB[:, 0:4096]
    LATF = AB[:, 0:8192]
    KP = AB[:, 8192:12288]
    LATO = AB[:, 8192:12288]
    VT = AB[:, 12288:16448]
    QP = AB[:, 16448:18496]
    EB = AB[:, 18496:19520]
    CTX = AB[:, 19520:19776]
    CTSTG = AB[0:64, 19776:20032]
    WQP = AB[:, 20032:21056]
    WOP = AB[:, 21056:22080]
    WKP = AB[:, 22080:23104]
    WVP = AB[:, 23104:24128]
    WLP = AB[:, 24128:26176]
    DCOPY = AFA[0:1, 0:256]
    RD = AFA[0:1, 256:512]
    RDB = AFA[0:64, 1024:1280]
    TBL = AFA[:, 1280:1372]
    NKT = 32
    SL8 = [0.7071067811865476 ** (h + 1) / 8.0 for h in range(16)]
    nkv = 2 if layer == 0 else 8

    if layer == 0:
        def gw0(gp):
            ins = gp.dma_start(out=WLP.rearrange("p (kc m) -> p kc m", m=256),
                               in_=k.l0a_wl.ap().rearrange("(kc p) m -> p kc m", p=128))
            k.inc(ins, "DG", 16)
        blk.gpsimd(gw0)
        for lc in range(2):
            for tb in range(4):
                def tf(te, lc=lc, tb=tb):
                    k.w(te, "DG")
                    k.w(te, "V")
                    k.w(te, "S")
                    for kc in range(8):
                        ins = te.matmul(
                            PB[4][:, :],
                            WLP[:, kc * 256 + lc * 128: kc * 256 + lc * 128 + 128],
                            XN[:, kc * 2048 + tb * 512: kc * 2048 + tb * 512 + 512],
                            start=(kc == 0), stop=(kc == 7))
                    k.inc(ins, "T")
                blk.tensor(tf)

                def vf(ve, lc=lc, tb=tb):
                    k.w(ve, "T")
                    o = lc * 2048 + tb * 512
                    ins = ve.tensor_scalar(LATO[:, o:o + 512], PB[4][:, :],
                                           k.BL0[:, lc:lc + 1], None, ALU.add)
                    k.inc(ins, "V")
                blk.vector(vf)

        def yl(sy):
            k.w(sy, "V")
            ins = sy.dma_start(out=k.lat_in.ap(), in_=LATO)
            k.inc(ins, "DY", 16)
        blk.sync(yl)

        def gl(gp):
            k.w(gp, "DY")
            ins = gp.collective_compute(
                "AllGather", ALU.bypass,
                replica_groups=[[0, 1], [2, 3], [4, 5], [6, 7]],
                ins=[k.lat_in.ap().opt()], outs=[k.lat_out.ap().opt()])
            k.inc(ins, "CC")
        blk.gpsimd(gl)

        def yl2(sy):
            k.w(sy, "CC")
            for lc in range(2):
                for h in range(2):
                    ins = sy.dma_start(
                        out=LATF[:, lc * 4096 + h * 2048: lc * 4096 + (h + 1) * 2048],
                        in_=k.lat_out.ap()[h * 128:(h + 1) * 128,
                                           lc * 2048:(lc + 1) * 2048])
                    k.inc(ins, "DY", 16)
        blk.sync(yl2)

        def vsig(ve):
            k.w(ve, "DY")
            k.inc(ve.nop(), "V")
        blk.vector(vsig)
    else:
        def yx(sy):
            k.w(sy, "V")
            ins = sy.dma_start(out=k.xn_in.ap(), in_=XN[:, :])
            k.inc(ins, "DY", 16)
        blk.sync(yx)

        def gx(gp):
            k.w(gp, "DY")
            ins = gp.collective_compute(
                "AllGather", ALU.bypass,
                replica_groups=[[0, 1], [2, 3], [4, 5], [6, 7]],
                ins=[k.xn_in.ap().opt()], outs=[k.xn_out.ap().opt()])
            k.inc(ins, "CC")
        blk.gpsimd(gx)

        def vsig2(ve):
            ve.wait_ge(k.sem["CC"], k.c["CC"])
            k.inc(ve.nop(), "V")
        blk.vector(vsig2)

    def vones(ve):
        for t in range(NKT):
            ve.memset(VT[:, t * 130 + 64: t * 130 + 65], 1.0)
            ins = ve.memset(VT[:, t * 130 + 129: t * 130 + 130], 1.0)
        k.inc(ins, "V")
        if layer == 1:
            DIFH = AFA[:, 512:768]
            ins = ve.tensor_scalar(k.MC0[:, :], k.HFB[:, :], 2048.0, None,
                                   ALU.mult)
            k.vb(ve, ins)
            ve.tensor_scalar(DIFH, k.DIF[:, :], k.MC0[:, :], None,
                             ALU.subtract)
            for mt, off in ((k.M0, 0.0), (k.M1, -128.0), (k.M2, -2048.0),
                            (k.M3, -2176.0)):
                ins = ve.tensor_scalar(mt[:, :], DIFH, off, None, ALU.is_le)
            k.inc(ins, "V")
            ins = ve.tensor_scalar(k.HFROW[:, :], k.ONESf[0:1, :],
                                   k.HFS[0:1, 0:1], None, ALU.mult)
            k.inc(ins, "V")
    blk.vector(vones)

    for hp in range(8):
        wq_d = k.l0a_wq if layer == 0 else k.l1a_wq
        wk_d = k.l0a_wk if layer == 0 else k.l1a_wk
        wv_d = k.l0a_wv if layer == 0 else k.l1a_wv
        wo_d = k.l0a_wo if layer == 0 else k.l1a_wo

        def gwp(gp, hp=hp, wq_d=wq_d, wk_d=wk_d, wv_d=wv_d, wo_d=wo_d):
            k.w(gp, "T")
            ins = gp.dma_start(
                out=WQP.rearrange("p (kc m) -> p kc m", m=128),
                in_=wq_d.ap()[:, hp * 128:(hp + 1) * 128].rearrange(
                    "(kc p) m -> p kc m", p=128))
            k.inc(ins, "DG", 16)
            ins = gp.dma_start(
                out=WKP[:, 0:nkv * 128].rearrange("p (kc m) -> p kc m", m=128),
                in_=wk_d.ap()[:, hp * 128:(hp + 1) * 128].rearrange(
                    "(kc p) m -> p kc m", p=128))
            k.inc(ins, "DG", 16)
            ins = gp.dma_start(
                out=WVP[:, 0:nkv * 128].rearrange("p (kc m) -> p kc m", m=128),
                in_=wv_d.ap()[:, hp * 128:(hp + 1) * 128].rearrange(
                    "(kc p) m -> p kc m", p=128))
            k.inc(ins, "DG", 16)
            ins = gp.dma_start(out=WOP,
                               in_=wo_d.ap()[hp * 128:(hp + 1) * 128, :])
            k.inc(ins, "DG", 16)
        blk.gpsimd(gwp)

        for tb in range(4):
            def tq(te, hp=hp, tb=tb):
                k.w(te, "DG")
                k.w(te, "V")
                pb = PB[4 + tb % 2]
                for kc in range(8):
                    ins = te.matmul(
                        pb[:, :], WQP[:, kc * 128:(kc + 1) * 128],
                        XN[:, kc * 2048 + tb * 512: kc * 2048 + tb * 512 + 512],
                        start=(kc == 0), stop=(kc == 7))
                k.inc(ins, "T")
            blk.tensor(tq)

            def vq(ve, hp=hp, tb=tb, layer=layer):
                k.w(ve, "T")
                pb = PB[4 + tb % 2]
                bias = k.BQ0[:, hp:hp + 1] if layer == 0 else 0.0
                ins = ve.tensor_scalar(QP[:, tb * 512:(tb + 1) * 512],
                                       pb[:, :], bias, None, ALU.add)
                k.inc(ins, "V")
            blk.vector(vq)

        for kb in range(8):
            if layer == 1:
                def yxo(sy, kb=kb):
                    k.w(sy, "T")
                    k.w(sy, "CC")
                    h = kb // 4
                    ltk = (kb % 4) * 512
                    src = k.xn_out.ap()[h * 128:(h + 1) * 128, :].rearrange(
                        "p (c n) -> p c n", c=8)[:, :, ltk:ltk + 512]
                    ins = sy.dma_start(
                        out=XOT.rearrange("p (c n) -> p c n", n=512), in_=src)
                    k.inc(ins, "DY", 16)
                blk.sync(yxo)

            def tk(te, hp=hp, kb=kb, layer=layer):
                if layer == 1:
                    k.w(te, "DY")
                k.w(te, "V")
                pb = PB[4 + kb % 2]
                for kc in range(nkv):
                    if layer == 0:
                        rhs = LATF[:, kc * 4096 + kb * 512: kc * 4096 + kb * 512 + 512]
                    else:
                        rhs = XOT[:, kc * 512:(kc + 1) * 512]
                    ins = te.matmul(pb[:, :], WKP[:, kc * 128:(kc + 1) * 128],
                                    rhs, start=(kc == 0), stop=(kc == nkv - 1))
                k.inc(ins, "T")
            blk.tensor(tk)

            def vk(ve, hp=hp, kb=kb, layer=layer):
                k.w(ve, "T")
                pb = PB[4 + kb % 2]
                bias = k.BK0[:, hp:hp + 1] if layer == 0 else 0.0
                ins = ve.tensor_scalar(KP[:, kb * 512:(kb + 1) * 512],
                                       pb[:, :], bias, None, ALU.add)
                k.inc(ins, "V")
            blk.vector(vk)

            for ti in range(4):
                t = kb * 4 + ti
                def tv(te, hp=hp, kb=kb, ti=ti, t=t, layer=layer):
                    k.w(te, "V")
                    pb = PB[6][:, (ti % 2) * 128:(ti % 2) * 128 + 128]
                    for kc in range(nkv):
                        if layer == 0:
                            lhsT = LATF[:, kc * 4096 + t * 128: kc * 4096 + t * 128 + 128]
                        else:
                            lhsT = XOT[:, kc * 512 + ti * 128: kc * 512 + ti * 128 + 128]
                        ins = te.matmul(pb, lhsT, WVP[:, kc * 128:(kc + 1) * 128],
                                        start=(kc == 0), stop=(kc == nkv - 1))
                    k.inc(ins, "T")
                blk.tensor(tv)

                def vv(ve, t=t, ti=ti):
                    k.w(ve, "T")
                    pb = PB[6][:, (ti % 2) * 128:(ti % 2) * 128 + 128]
                    ve.tensor_copy(VT[:, t * 130: t * 130 + 64], pb[:, 0:64])
                    ins = ve.tensor_copy(VT[:, t * 130 + 65: t * 130 + 129],
                                         pb[:, 64:128])
                    k.inc(ins, "V")
                blk.vector(vv)

        k.vt_ready = k.c["V"]

        if layer == 1:
            def ttb(te, hp=hp):
                k.w(te, "V")
                te.matmul(PB[5][:, 0:92], k.ONESf[0:1, :],
                          k.AROWS[0:1, hp * 92:(hp + 1) * 92],
                          start=True, stop=False)
                ins = te.matmul(PB[5][:, 0:92], k.HFROW[:, :],
                                k.BROWS[0:1, hp * 92:(hp + 1) * 92],
                                start=False, stop=True)
                k.inc(ins, "T")
            blk.tensor(ttb)

            def vtb(ve, hp=hp):
                k.w(ve, "T")
                PCOL = AFA[:, 801:802]
                for head in range(2):
                    sl8 = SL8[2 * hp + head]
                    ins = ve.tensor_scalar(PCOL, k.PIOTA[:, :], sl8, None,
                                           ALU.mult)
                    k.vb(ve, ins)
                    ins = ve.tensor_scalar(
                        TBL[:, head * 46:(head + 1) * 46],
                        PB[5][:, head * 46:(head + 1) * 46], PCOL, None,
                        ALU.add)
                k.inc(ins, "V")
            blk.vector(vtb)
        tbl_done = k.c["V"]

        for qb in range(8):
            for head in range(2):
                gh = 2 * hp + head
                diag = sorted({2 * qb, 2 * qb + 1, 16 + 2 * qb, 17 + 2 * qb}
                              ) if layer == 1 else []
                T0, S0 = k.c["T"], k.c["S"]
                LAG = int(_os.environ.get("K_LAG", "2"))
                NR3 = [0, 1, 3 - head] if LAG >= 2 else [0, 1]
                NRN = len(NR3)
                V_pre = k.c["V"]
                sexp = [S0 + t + 1 for t in range(NKT)]
                tsc = [0] * NKT
                tctx = [0] * NKT
                for t in range(LAG):
                    tsc[t] = T0 + t + 1
                for t in range(LAG, NKT):
                    tsc[t] = T0 + 2 * t - LAG + 1
                for t in range(NKT - LAG):
                    tctx[t] = T0 + 2 * t + LAG + 2
                for i, t in enumerate(range(NKT - LAG, NKT)):
                    tctx[t] = T0 + 2 * (NKT - LAG) + LAG + i + 1

                vmask = {}
                if layer == 1:
                    V0m = k.c["V"]
                    for i, t in enumerate(diag):
                        vmask[t] = V0m + i + 1

                    def vmk(ve, qb=qb, diag=diag, sexp=sexp):
                        for t in diag:
                            ve.wait_ge(k.sem["S"], sexp[t])
                            u = (t - 2 * qb) % 2
                            lo = t < 16
                            mm = [[k.M2, k.M3], [k.M0, k.M1]][lo][u][:, :]
                            ins = ve.tensor_tensor(
                                EB[:, (t % 4) * 256:(t % 4 + 1) * 256],
                                EB[:, (t % 4) * 256:(t % 4 + 1) * 256],
                                mm, ALU.mult)
                            k.inc(ins, "V")
                    blk.vector(vmk)

                V_vt = k.vt_ready

                def tat(te, qb=qb, head=head, layer=layer, NR3=NR3, NRN=NRN,
                        vmask=vmask, sexp=sexp, diag=diag, S0=S0, V_vt=V_vt,
                        V_pre=V_pre):
                    te.wait_ge(k.sem["V"], max(V_vt, V_pre))

                    def score(t):
                        if t >= NRN:
                            te.wait_ge(k.sem["S"], sexp[t - NRN])
                        else:
                            te.wait_ge(k.sem["S"], S0)
                        ins = te.matmul(PB[NR3[t % NRN]][:, 0:256],
                                        KP[head * 64:(head + 1) * 64,
                                           t * 128:(t + 1) * 128],
                                        QP[head * 64:(head + 1) * 64,
                                           qb * 256:(qb + 1) * 256],
                                        start=True, stop=True)
                        k.inc(ins, "T")

                    def ctx(t):
                        if layer == 1 and t in diag:
                            te.wait_ge(k.sem["V"], vmask[t])
                        else:
                            te.wait_ge(k.sem["S"], sexp[t])
                        ins = te.matmul(
                            PB[2 + head][0:65, 0:256],
                            VT[:, t * 130 + head * 65: t * 130 + head * 65 + 65],
                            EB[:, (t % 4) * 256:(t % 4 + 1) * 256],
                            start=(t == 0), stop=(t == NKT - 1))
                        k.inc(ins, "T")
                    LAG = int(_os.environ.get("K_LAG", "3"))
                    for t in range(LAG):
                        score(t)
                    for t in range(LAG, NKT):
                        score(t)
                        ctx(t - LAG)
                    for t in range(NKT - LAG, NKT):
                        ctx(t)
                blk.tensor(tat)
                assert k.c["T"] == tctx[NKT - 1]

                def ssf(sc, qb=qb, head=head, layer=layer, NR3=NR3, NRN=NRN,
                        tsc=tsc, tctx=tctx, tbl_done=tbl_done):
                    if layer == 1:
                        sc.wait_ge(k.sem["V"], tbl_done)
                    for t in range(NKT):
                        sc.wait_ge(k.sem["T"], tsc[t])
                        if t >= 4:
                            sc.wait_ge(k.sem["T"], tctx[t - 4])
                        if layer == 1:
                            m = t - 2 * qb + 14
                            bias = TBL[:, head * 46 + m: head * 46 + m + 1]
                        else:
                            bias = k.cc(-15.0)
                        ins = sc.activation(
                            out=EB[:, (t % 4) * 256:(t % 4 + 1) * 256],
                            in_=PB[NR3[t % NRN]][:, 0:256], func=AF.Exp,
                            bias=bias, scale=0.125)
                        k.inc(ins, "S")
                blk.scalar(ssf)
                assert k.c["S"] == sexp[NKT - 1]

                def vdiv(ve, head=head, tc31=tctx[NKT - 1]):
                    ve.wait_ge(k.sem["T"], tc31)
                    ins = ve.tensor_copy(
                        DCOPY, PB[2 + head][64:65, 0:256])
                    k.inc(ins, "V")
                    k.w(ve, "V")
                    ins = ve.reciprocal(RD, DCOPY)
                    k.inc(ins, "V")
                blk.vector(vdiv)

                def trd(te, head=head):
                    k.w(te, "V")
                    ins = te.matmul(PB[4 + head][0:64, 0:256],
                                    k.ONESf[0:1, 0:64], RD,
                                    start=True, stop=True)
                    k.inc(ins, "T")
                blk.tensor(trd)

                def vev(ve, hp=hp, head=head, layer=layer):
                    k.w(ve, "T")
                    dst = CTX[0:64, :] if head == 0 else CTSTG
                    ve.tensor_copy(RDB, PB[4 + head][0:64, 0:256])
                    ins = ve.tensor_tensor(dst, PB[2 + head][0:64, 0:256],
                                           RDB, ALU.mult)
                    k.inc(ins, "V")
                    if layer == 0:
                        c = 2 * hp + head
                        ins = ve.tensor_scalar(dst, dst,
                                               k.BVH0[0:64, c:c + 1], None,
                                               ALU.add)
                        k.inc(ins, "V")
                    if head == 1:
                        ins = ve.tensor_copy(CTX[64:128, :], CTSTG)
                        k.inc(ins, "V")
                blk.vector(vev)

            def top(te, hp=hp, qb=qb):
                k.w(te, "V")
                for dch in range(8):
                    ins = te.matmul(
                        PB[3 + dch // 2][:, (dch % 2) * 256:(dch % 2) * 256 + 256],
                        WOP[:, dch * 128:(dch + 1) * 128], CTX,
                        start=(dch % 2 == 0), stop=(dch % 2 == 1))
                k.inc(ins, "T")
            blk.tensor(top)

            def vop(ve, qb=qb):
                k.w(ve, "T")
                for dch in range(8):
                    sl = slice(dch * 2048 + qb * 256, dch * 2048 + qb * 256 + 256)
                    ins = ve.tensor_tensor(
                        XT[:, sl], XT[:, sl],
                        PB[3 + dch // 2][:, (dch % 2) * 256:(dch % 2) * 256 + 256],
                        ALU.add)
                k.inc(ins, "V")
            blk.vector(vop)

    def vbo(ve, layer=layer):
        BO = k.BO0 if layer == 0 else k.BO1
        for dch in range(8):
            ins = ve.tensor_scalar(XT[:, dch * 2048:(dch + 1) * 2048],
                                   XT[:, dch * 2048:(dch + 1) * 2048],
                                   BO[:, dch:dch + 1], None, ALU.add)
        k.inc(ins, "V")
    blk.vector(vbo)



FP4_MID = [0.25, 0.75, 1.25, 1.75, 2.5, 3.5, 5.0]
FP4_D = [0.5, 0.5, 0.5, 0.5, 1.0, 1.0, 2.0]


def quant_sweep(k, blk, X, ABSX, T, Q, SCL, MJ, QF32, N, tag):
    nc = k.nc
    NB = N // 16

    def sab(sc):
        k.w(sc, "V")
        ins = sc.activation(out=ABSX, in_=X, func=AF.Abs)
        k.inc(ins, "S")
    blk.scalar(sab)

    def vsw(ve):
        k.w(ve, "S")
        va = ABSX.rearrange("p (b s) -> p b s", s=16)
        ve.tensor_reduce(SCL, va, AX.X, ALU.max)
        ve.tensor_scalar(SCL, SCL, 1.0 / 6.0, None, ALU.mult)
        ve.tensor_scalar(SCL, SCL, 1e-12, None, ALU.max)
        first = True
        for j in range(7):
            ins = ve.tensor_scalar(MJ, SCL, FP4_MID[j], None, ALU.mult)
            k.vb(ve, ins)
            mjb = MJ.rearrange("p (b u) -> p b u", u=1).to_broadcast([128, NB, 16])
            ve.tensor_tensor(T.rearrange("p (b s) -> p b s", s=16), va, mjb,
                             ALU.is_gt)
            if first:
                ve.tensor_scalar(Q, T, FP4_D[j], None, ALU.mult)
                first = False
            else:
                ve.scalar_tensor_tensor(Q, T, FP4_D[j], Q, ALU.mult, ALU.add)
        ve.tensor_scalar(T, X, 0.0, None, ALU.is_gt)
        ins = ve.tensor_scalar(T, T, 2.0, -1.0, ALU.mult, ALU.add)
        k.vb(ve, ins)
        sclb = SCL.rearrange("p (b u) -> p b u", u=1).to_broadcast([128, NB, 16])
        ve.tensor_tensor(QF32.rearrange("p (b s) -> p b s", s=16),
                         Q.rearrange("p (b s) -> p b s", s=16), sclb, ALU.mult)
        ins = ve.tensor_tensor(QF32, QF32, T, ALU.mult)
        k.inc(ins, "V")
    blk.vector(vsw)


def l1w_quant_emit(k, blk):
    nc = k.nc
    AB, AFA, PB = k.AB, k.AFA, k.PB
    SCL = AFA[:, 5632:5820]
    MJ = AFA[:, 5820:6008]

    def wtile(src_d, dst_d, r, ncols):
        X = AFA[:, 0:ncols]
        nh = 1408 if ncols > 2048 else ncols
        STG = AB[:, 0:ncols]

        def yld(sy, src_d=src_d, r=r, ncols=ncols):
            k.w(sy, "V")
            k.w(sy, "T")
            ins = sy.dma_start(out=X, in_=src_d.ap()[r * 128:(r + 1) * 128, :])
            k.inc(ins, "DY", 16)
        blk.sync(yld)

        def vw(ve):
            k.w(ve, "DY")
            k.inc(ve.nop(), "V")
        blk.vector(vw)

        for h0 in range(0, ncols, nh):
            quant_sweep(k, blk, AFA[:, h0:h0 + nh],
                        AFA[:, 2816:2816 + nh],
                        AFA[:, 4224:4224 + nh],
                        AB[:, 4096:4096 + nh],
                        SCL[:, 0:nh // 16], MJ[:, 0:nh // 16],
                        AFA[:, h0:h0 + nh],
                        nh, "wq")

        nj = ncols // 128
        for jc in range(nj):
            def tt_(te, jc=jc, ncols=ncols):
                if jc == 0:
                    k.w(te, "V")
                else:
                    k.w(te, "V")
                ins = te.transpose(PB[4 + jc % 2][:, 0:128],
                                   AFA[:, jc * 128:(jc + 1) * 128],
                                   k.IDfs[:, :])
                k.inc(ins, "T")
            blk.tensor(tt_)

            def vc_(ve, jc=jc, ncols=ncols):
                k.w(ve, "T")
                if jc == 0:
                    k.w(ve, "DY")
                ins = ve.tensor_copy(STG[:, jc * 128:(jc + 1) * 128],
                                     PB[4 + jc % 2][:, 0:128])
                k.inc(ins, "V")
            blk.vector(vc_)

        def yst(sy, dst_d=dst_d, r=r, ncols=ncols, nj=nj):
            k.w(sy, "V")
            for jc in range(nj):
                ins = sy.dma_start(
                    out=dst_d.ap()[jc * 128:(jc + 1) * 128,
                                   r * 128:(r + 1) * 128],
                    in_=STG[:, jc * 128:(jc + 1) * 128])
                k.inc(ins, "DY", 16)
        blk.sync(yst)

    for r in range(22):
        wtile(k.l1f_wg, k.wqg_t, r, 1024)
        wtile(k.l1f_wu, k.wqu_t, r, 1024)
    for r in range(8):
        wtile(k.l1f_wd, k.wqd_t, r, 2816)

    def vfin(ve):
        k.w(ve, "DY")
        k.inc(ve.nop(), "V")
    blk.vector(vfin)


def l1ffn_emit(k, blk, ci):
    nc = k.nc
    AB, AFA, PB, XT = k.AB, k.AFA, k.PB, k.XT
    if not getattr(k, "l1w_done", False):
        l1w_quant_emit(k, blk)
        k.l1w_done = True
    XQF = AB[:, 0:2048]
    HT = AB[:, 2048:7680]
    TB_ = AB[:, 7680:10496]
    QB_ = AB[:, 10496:13312]
    HQF = AB[:, 13312:18944]
    WS = [AB[:, 18944:21504], AB[:, 21504:24064]]
    ABSB = AB[:, 24064:26880]
    SCL = AFA[:, 5632:5820]
    MJ = AFA[:, 5820:6008]
    MU = AFA[:, 6008:6009]
    VS = AFA[:, 6010:6011]
    RS_ = AFA[:, 6012:6013]

    for blkI in range(8):
        t0 = blkI * 256
        for tt in range(2):
            def txp(te, t0=t0, tt=tt):
                k.w(te, "V")
                k.w(te, "S")
                for fc in range(8):
                    ins = te.transpose(
                        PB[4 + fc // 4][:, (fc % 4) * 128:(fc % 4) * 128 + 128],
                        XT[:, fc * 2048 + t0 + tt * 128:
                           fc * 2048 + t0 + tt * 128 + 128],
                        k.IDfs[:, :])
                k.inc(ins, "T")
            blk.tensor(txp)

            def vcp(ve):
                k.w(ve, "T")
                ve.tensor_copy(AFA[:, 0:512], PB[4][:, :])
                ins = ve.tensor_copy(AFA[:, 512:1024], PB[5][:, :])
                k.inc(ins, "V")
            blk.vector(vcp)

            def vln(ve):
                XTM = AFA[:, 0:1024]
                XC = AFA[:, 1024:2048]
                k.w(ve, "V")
                k.vb(ve, ve.tensor_reduce(MU, XTM, AX.X, ALU.add))
                k.vb(ve, ve.tensor_scalar(MU, MU, 1.0 / 1024.0, None,
                                          ALU.mult))
                ins = ve.tensor_scalar(XC, XTM, MU, None, ALU.subtract)
                k.inc(ins, "V")
            blk.vector(vln)

            def sln(sc):
                XC = AFA[:, 1024:2048]
                k.w(sc, "S")
                k.w(sc, "V")
                ins = sc.activation(out=AFA[:, 2048:3072], in_=XC,
                                    func=AF.Square, accum_out=VS)
                k.inc(ins, "S")
                k.w(sc, "S")
                ins = sc.activation(out=RS_, in_=VS, func=AF.Ln,
                                    bias=k.cc(1e-5), scale=1.0 / 1024.0)
                k.inc(ins, "S")
                k.w(sc, "S")
                ins = sc.activation(out=RS_, in_=RS_, func=AF.Exp,
                                    scale=k.cc(-0.5))
                k.inc(ins, "S")
            blk.scalar(sln)

            def vn2(ve):
                XTM = AFA[:, 0:1024]
                XC = AFA[:, 1024:2048]
                k.w(ve, "S")
                ins = ve.tensor_scalar(XTM, XC, RS_, None, ALU.mult)
                k.inc(ins, "V")
            blk.vector(vn2)

            quant_sweep(k, blk, AFA[:, 0:1024], AFA[:, 2048:3072],
                        AFA[:, 3072:4096], QB_[:, 0:1024],
                        SCL[:, 0:64], MJ[:, 0:64], AFA[:, 0:1024], 1024, "xq")

            def txq(te, tt=tt):
                k.w(te, "V")
                for fc in range(8):
                    ins = te.transpose(
                        PB[4 + fc // 4][:, (fc % 4) * 128:(fc % 4) * 128 + 128],
                        AFA[:, fc * 128:(fc + 1) * 128], k.IDfs[:, :])
                k.inc(ins, "T")
            blk.tensor(txq)

            def vxq(ve, tt=tt):
                k.w(ve, "T")
                for fc in range(8):
                    ins = ve.tensor_copy(
                        XQF[:, fc * 256 + tt * 128: fc * 256 + tt * 128 + 128],
                        PB[4 + fc // 4][:, (fc % 4) * 128:(fc % 4) * 128 + 128])
                k.inc(ins, "V")
            blk.vector(vxq)

        te_done = {}
        for ncI in range(22):
            def ywg(sy, ncI=ncI, te_done=te_done):
                if ncI >= 2:
                    sy.wait_ge(k.sem["T"], te_done[ncI - 2])
                else:
                    k.w(sy, "V")
                b = ncI % 2
                ins = sy.dma_start(
                    out=WS[b][:, 0:1024].rearrange("p (kc m) -> p kc m", m=128),
                    in_=k.wqg_t.ap()[:, ncI * 128:(ncI + 1) * 128].rearrange(
                        "(kc p) m -> p kc m", p=128))
                k.inc(ins, "DY", 16)
                ins = sy.dma_start(
                    out=WS[b][:, 1024:2048].rearrange("p (kc m) -> p kc m", m=128),
                    in_=k.wqu_t.ap()[:, ncI * 128:(ncI + 1) * 128].rearrange(
                        "(kc p) m -> p kc m", p=128))
                k.inc(ins, "DY", 16)
            blk.sync(ywg)
            dyw = k.c["DY"]

            def tgu(te, ncI=ncI, dyw=dyw):
                te.wait_ge(k.sem["DY"], dyw)
                k.w(te, "V")
                k.w(te, "S")
                b = ncI % 2
                for kc in range(8):
                    te.matmul(PB[0][:, 0:256], WS[b][:, kc * 128:(kc + 1) * 128],
                              XQF[:, kc * 256:(kc + 1) * 256],
                              start=(kc == 0), stop=(kc == 7))
                for kc in range(8):
                    ins = te.matmul(PB[1][:, 0:256],
                                    WS[b][:, 1024 + kc * 128:1024 + (kc + 1) * 128],
                                    XQF[:, kc * 256:(kc + 1) * 256],
                                    start=(kc == 0), stop=(kc == 7))
                k.inc(ins, "T")
            blk.tensor(tgu)

            def sgu(sc):
                k.w(sc, "T")
                ins = sc.activation(out=AFA[:, 0:256], in_=PB[0][:, 0:256],
                                    func=AF.Silu)
                k.inc(ins, "S")
            blk.scalar(sgu)

            def vh(ve):
                k.w(ve, "S")
                ins = ve.tensor_tensor(AFA[:, 256:512], AFA[:, 0:256],
                                       PB[1][:, 0:256], ALU.mult)
                k.inc(ins, "V")
            blk.vector(vh)

            def tht(te, ncI=ncI, te_done=te_done):
                k.w(te, "V")
                for tt in range(2):
                    ins = te.transpose(PB[2][:, tt * 128:(tt + 1) * 128],
                                       AFA[:, 256 + tt * 128:256 + (tt + 1) * 128],
                                       k.IDfs[:, :])
                k.inc(ins, "T")
                te_done[ncI] = k.c["T"]
            blk.tensor(tht)

            def vht(ve, ncI=ncI):
                k.w(ve, "T")
                for tt in range(2):
                    ins = ve.tensor_copy(
                        HT[:, tt * 2816 + ncI * 128: tt * 2816 + (ncI + 1) * 128],
                        PB[2][:, tt * 128:(tt + 1) * 128])
                k.inc(ins, "V")
            blk.vector(vht)

        for tt in range(2):
            quant_sweep(k, blk, HT[:, tt * 2816:(tt + 1) * 2816], ABSB,
                        TB_, QB_, SCL[:, 0:176], MJ[:, 0:176],
                        AFA[:, 0:2816], 2816, "hq")
            for jc in range(22):
                def thq(te, jc=jc):
                    k.w(te, "V")
                    ins = te.transpose(PB[4 + jc % 2][:, 0:128],
                                       AFA[:, jc * 128:(jc + 1) * 128],
                                       k.IDfs[:, :])
                    k.inc(ins, "T")
                blk.tensor(thq)

                def vhq(ve, jc=jc, tt=tt):
                    k.w(ve, "T")
                    ins = ve.tensor_copy(
                        HQF[:, jc * 256 + tt * 128: jc * 256 + tt * 128 + 128],
                        PB[4 + jc % 2][:, 0:128])
                    k.inc(ins, "V")
                blk.vector(vhq)

        td_done = {}
        for kc in range(22):
            def ywd(sy, kc=kc, td_done=td_done):
                if kc >= 2:
                    sy.wait_ge(k.sem["T"], td_done[kc - 2])
                else:
                    k.w(sy, "V")
                b = kc % 2
                ins = sy.dma_start(out=WS[b][:, 0:1024],
                                   in_=k.wqd_t.ap()[kc * 128:(kc + 1) * 128, :])
                k.inc(ins, "DY", 16)
            blk.sync(ywd)
            dyw = k.c["DY"]

            def tdn(te, kc=kc, dyw=dyw, td_done=td_done):
                te.wait_ge(k.sem["DY"], dyw)
                if kc == 0:
                    k.w(te, "V")
                b = kc % 2
                for dch in range(8):
                    ins = te.matmul(
                        PB[3 + dch // 2][:, (dch % 2) * 256:(dch % 2) * 256 + 256],
                        WS[b][:, dch * 128:(dch + 1) * 128],
                        HQF[:, kc * 256:(kc + 1) * 256],
                        start=(kc == 0 and dch % 2 == 0),
                        stop=(kc == 21 and dch % 2 == 1))
                k.inc(ins, "T")
                td_done[kc] = k.c["T"]
            blk.tensor(tdn)

        def vdn(ve, t0=t0):
            k.w(ve, "T")
            for dch in range(8):
                xsl = slice(dch * 2048 + t0, dch * 2048 + t0 + 256)
                ins = ve.scalar_tensor_tensor(
                    XT[:, xsl],
                    PB[3 + dch // 2][:, (dch % 2) * 256:(dch % 2) * 256 + 256],
                    0.5, XT[:, xsl], ALU.mult, ALU.add)
            k.inc(ins, "V")
        blk.vector(vdn)


def build(core_id):
    nc = bass.Bass("TRN2", target_bir_lowering=False, debug=False)
    k = Ctx(nc)
    k.pid = nc.partition_id()

    x_in = nc.dram_tensor("x", [TC, D], F32, kind="ExternalInput").ap()
    out = nc.dram_tensor("out", [TC, D], F32, kind="ExternalOutput").ap()
    EI = "ExternalInput"
    k.wr_d = nc.dram_tensor("l0f_wr", [D, 64], F32, kind=EI)
    k.br_d = nc.dram_tensor("l0f_br", [1, 64], F32, kind=EI)
    k.wup_d = nc.dram_tensor("l0f_wup", [64, D, 256], F32, kind=EI)
    k.bup_d = nc.dram_tensor("l0f_bup", [64, 256], F32, kind=EI)
    k.wgt_d = nc.dram_tensor("l0f_wgt", [64, D, 256], F32, kind=EI)
    k.bgt_d = nc.dram_tensor("l0f_bgt", [64, 256], F32, kind=EI)
    k.wdn_d = nc.dram_tensor("l0f_wdn", [64, 256, D], F32, kind=EI)
    k.bdn_d = nc.dram_tensor("l0f_bdn", [64, D], F32, kind=EI)
    k.l0a_wq = nc.dram_tensor("l0a_wq", [D, D], F32, kind=EI)
    k.l0a_bq = nc.dram_tensor("l0a_bq", [8, 128], F32, kind=EI)
    k.l0a_wl = nc.dram_tensor("l0a_wl", [D, 256], F32, kind=EI)
    k.l0a_bl = nc.dram_tensor("l0a_bl", [2, 128], F32, kind=EI)
    k.l0a_wk = nc.dram_tensor("l0a_wk", [256, D], F32, kind=EI)
    k.l0a_bk = nc.dram_tensor("l0a_bk", [8, 128], F32, kind=EI)
    k.l0a_wv = nc.dram_tensor("l0a_wv", [256, D], F32, kind=EI)
    k.l0a_bv = nc.dram_tensor("l0a_bv", [16, 64], F32, kind=EI)
    k.l0a_wo = nc.dram_tensor("l0a_wo", [D, D], F32, kind=EI)
    k.l0a_bo = nc.dram_tensor("l0a_bo", [8, 128], F32, kind=EI)
    k.l1a_wq = nc.dram_tensor("l1a_wq", [D, D], F32, kind=EI)
    k.l1a_wk = nc.dram_tensor("l1a_wk", [D, D], F32, kind=EI)
    k.l1a_wv = nc.dram_tensor("l1a_wv", [D, D], F32, kind=EI)
    k.l1a_wo = nc.dram_tensor("l1a_wo", [D, D], F32, kind=EI)
    k.l1a_bo = nc.dram_tensor("l1a_bo", [8, 128], F32, kind=EI)
    k.cinfo = nc.dram_tensor("cinfo", [1, 1], F32, kind=EI)
    k.l1f_wg = nc.dram_tensor("l1f_wg", [2816, D], F32, kind=EI)
    k.l1f_wu = nc.dram_tensor("l1f_wu", [2816, D], F32, kind=EI)
    k.l1f_wd = nc.dram_tensor("l1f_wd", [D, 2816], F32, kind=EI)
    k.wqg_t = nc.dram_tensor("wqg_t", [D, 2816], BF16)
    k.wqu_t = nc.dram_tensor("wqu_t", [D, 2816], BF16)
    k.wqd_t = nc.dram_tensor("wqd_t", [2816, D], BF16)
    k.lat_in = nc.dram_tensor("lat_in", [128, 4096], BF16)
    k.lat_out = nc.dram_tensor("lat_out", [256, 4096], BF16)
    k.xn_in = nc.dram_tensor("xn_in", [128, NCH * TC], BF16)
    k.xn_out = nc.dram_tensor("xn_out", [256, NCH * TC], BF16)
    k.sc_in = nc.dram_tensor("sc_in", [64, TC], F32)
    k.sc_my = nc.dram_tensor("sc_my", [128, 1024], F32)
    k.wup_b = nc.dram_tensor("wup_b", [64, 128, 2048], BF16)
    k.wgt_b = nc.dram_tensor("wgt_b", [64, 128, 2048], BF16)
    k.wdn_b = nc.dram_tensor("wdn_b", [64, 128, 2048], BF16)
    k.sc_out = nc.dram_tensor("sc_out", [512, TC], F32)
    k.gg_in = nc.dram_tensor("gg_in", [8, 512], F32)
    k.gg_out = nc.dram_tensor("gg_out", [64, 512], F32)
    if _DBG:
        k.dbg_d = nc.dram_tensor("dbg", [128, 8192], F32, kind="ExternalOutput")
        k.dbg2_d = nc.dram_tensor("dbg2", [128, NCH * TC], mybir.dt.bfloat16,
                                  kind="ExternalOutput")

    k.IDbf = nc.inline_tensor(np.eye(128, dtype=ml_dtypes.bfloat16), "idbf")
    k.IDf = nc.inline_tensor(np.eye(128, dtype=np.float32), "idf")
    onesf = np.ones((128, 128), dtype=np.float32)
    k.ONESf_d = nc.inline_tensor(onesf, "onesf")
    k.ONESbf_d = nc.inline_tensor(onesf.astype(ml_dtypes.bfloat16), "onesbf")
    conarr = np.tile(np.asarray(CONST_VALS, np.float32)[None, :], (128, 1))
    k.CON_d = nc.inline_tensor(np.ascontiguousarray(conarr), "conal")

    XT = nc.alloc_sbuf_tensor("XT", [128, NCH * TC], F32)
    XN = nc.alloc_sbuf_tensor("XN", [128, NCH * TC], BF16)
    AB = nc.alloc_sbuf_tensor("AB", [128, 28672], BF16)
    AFA = nc.alloc_sbuf_tensor("AFA", [128, 6144], F32)
    k.XT, k.XN, k.AB, k.AFA = XT, XN, AB, AFA
    k.REP = AB[0:64, 15872:24064]
    IDbf = nc.alloc_sbuf_tensor("IDbf_s", [128, 128], BF16)
    IDf = nc.alloc_sbuf_tensor("IDf_s", [128, 128], F32)
    ONESf = nc.alloc_sbuf_tensor("ONESf_s", [128, 128], F32)
    ONESbf = nc.alloc_sbuf_tensor("ONESbf_s", [128, 128], BF16)
    k.ONESf, k.ONESbf, k.IDfs, k.IDbfs = ONESf, ONESbf, IDf, IDbf
    CONS = nc.alloc_sbuf_tensor("CONS", [128, len(CONST_VALS)], F32)
    k.CONS = CONS
    k.cc = lambda v: CONS[:, CONST_VALS.index(v):CONST_VALS.index(v) + 1]
    k.WR = nc.alloc_sbuf_tensor("WRs", [128, 512], BF16)
    k.BRB = nc.alloc_sbuf_tensor("BRBs", [128, 64], F32)
    k.BR1 = nc.alloc_sbuf_tensor("BR1s", [1, 64], F32)
    k.BUP = nc.alloc_sbuf_tensor("BUPs", [128, 128], F32)
    k.BGT = nc.alloc_sbuf_tensor("BGTs", [128, 128], F32)
    k.BUPR = nc.alloc_sbuf_tensor("BUPRs", [64, 256], F32)
    k.BGTR = nc.alloc_sbuf_tensor("BGTRs", [64, 256], F32)
    k.BDN = nc.alloc_sbuf_tensor("BDNs", [64, 1024], BF16)
    k.GSseg = nc.alloc_sbuf_tensor("GSsegs", [128, 16], F32)
    k.GMseg = nc.alloc_sbuf_tensor("GMsegs", [128, 16], F32)
    k.GSA = nc.alloc_sbuf_tensor("GSAs", [64, 256], F32)
    k.GMA = nc.alloc_sbuf_tensor("GMAs", [64, 256], F32)
    k.RS = nc.alloc_sbuf_tensor("RSs", [64, 20], F32)
    k.THB = nc.alloc_sbuf_tensor("THBs", [128, 64], F32)
    k.FSTT = nc.alloc_sbuf_tensor("FSTTs", [64, 2048], BF16)
    NRND = 44
    ckl = np.zeros((64, NRND), np.float32)
    lr = math.log(64.0) - math.log(1e-38)
    for r in range(NRND):
        ckl[:, r] = math.exp(lr * (2.0 ** -(r + 1)))
    k.CKL_d = nc.inline_tensor(np.ascontiguousarray(ckl), "ckl")
    k.CKL = nc.alloc_sbuf_tensor("CKLs", [64, NRND], F32)
    k.EIDX_d = nc.inline_tensor(
        np.ascontiguousarray(np.arange(64, dtype=np.float32)[:, None]), "eidx")
    k.EIDX = nc.alloc_sbuf_tensor("EIDXs", [64, 1], F32)
    k.IOTA_d = nc.inline_tensor(np.ascontiguousarray(
        np.tile(np.arange(256, dtype=np.float32)[None, :], (64, 1))), "iota256")
    k.IOTA256 = nc.alloc_sbuf_tensor("IOTAs", [64, 256], F32)
    k.EYE64_d = nc.inline_tensor(np.eye(64, dtype=ml_dtypes.bfloat16), "eye64")
    k.EYE64 = nc.alloc_sbuf_tensor("EYE64s", [64, 64], BF16)
    rep = np.repeat(np.eye(64, dtype=ml_dtypes.bfloat16), 128, axis=1)
    k.REP_d = nc.inline_tensor(np.ascontiguousarray(rep), "rep64")
    k.PIO_d = nc.inline_tensor(
        np.ascontiguousarray(np.arange(128, dtype=np.float32)[:, None]), "pio")
    k.DIF_d = nc.inline_tensor(np.ascontiguousarray(
        (np.arange(128, dtype=np.float32)[:, None]
         - np.arange(256, dtype=np.float32)[None, :])), "dif")
    k.PIOTA = nc.alloc_sbuf_tensor("PIOTAs", [128, 1], F32)
    k.DIF = nc.alloc_sbuf_tensor("DIFs", [128, 256], F32)
    k.BQ0 = nc.alloc_sbuf_tensor("BQ0s", [128, 8], F32)
    k.BK0 = nc.alloc_sbuf_tensor("BK0s", [128, 8], F32)
    k.BVH0 = nc.alloc_sbuf_tensor("BVH0s", [64, 16], F32)
    k.BO0 = nc.alloc_sbuf_tensor("BO0s", [128, 8], F32)
    k.BL0 = nc.alloc_sbuf_tensor("BL0s", [128, 2], F32)
    k.BO1 = nc.alloc_sbuf_tensor("BO1s", [128, 8], F32)
    k.HFS = nc.alloc_sbuf_tensor("HFSs", [1, 1], F32)
    k.HFB = nc.alloc_sbuf_tensor("HFBs", [128, 1], F32)
    k.HFROW = nc.alloc_sbuf_tensor("HFROWs", [1, 128], F32)
    k.M0 = nc.alloc_sbuf_tensor("M0s", [128, 256], F32)
    k.M1 = nc.alloc_sbuf_tensor("M1s", [128, 256], F32)
    k.M2 = nc.alloc_sbuf_tensor("M2s", [128, 256], F32)
    k.M3 = nc.alloc_sbuf_tensor("M3s", [128, 256], F32)
    k.MC0 = nc.alloc_sbuf_tensor("MC0s", [128, 1], F32)
    arows = np.zeros((1, 8 * 92), np.float32)
    brows = np.zeros((1, 8 * 92), np.float32)
    for hp_ in range(8):
        for hd_ in range(2):
            sl8_ = 0.7071067811865476 ** (2 * hp_ + hd_ + 1) / 8.0
            for mi in range(46):
                m = mi - 14
                pen0 = 10000.0 if m >= 2 else 0.0
                pen1 = 10000.0 if m >= 18 else 0.0
                arows[0, hp_ * 92 + hd_ * 46 + mi] = sl8_ * 128 * m - 25.0 - pen0
                brows[0, hp_ * 92 + hd_ * 46 + mi] = pen0 - pen1 - 2048.0 * sl8_
    k.AROWS_d = nc.inline_tensor(np.ascontiguousarray(arows), "arows")
    k.BROWS_d = nc.inline_tensor(np.ascontiguousarray(brows), "brows")
    k.AROWS = nc.alloc_sbuf_tensor("AROWSs", [1, 8 * 92], F32)
    k.BROWS = nc.alloc_sbuf_tensor("BROWSs", [1, 8 * 92], F32)

    PB = [nc.alloc_psum_tensor(f"pb{i}", [128, 512], F32) for i in range(7)]
    PH = nc.alloc_psum_tensor("ph0", [128, 1024], BF16)
    k.PB, k.PH = PB, PH

    with nc.Block() as blk:
        @blk.sync
        def _(sy):
            ins = sy.dma_start(out=IDf[:, :], in_=k.IDf.ap())
            k.inc(ins, "DY", 16)
            ins = sy.dma_start(out=IDbf[:, :], in_=k.IDbf.ap())
            k.inc(ins, "DY", 16)
            ins = sy.dma_start(out=ONESf[:, :], in_=k.ONESf_d.ap())
            k.inc(ins, "DY", 16)
            ins = sy.dma_start(out=ONESbf[:, :], in_=k.ONESbf_d.ap())
            k.inc(ins, "DY", 16)
            ins = sy.dma_start(out=CONS[:, :], in_=k.CON_d.ap())
            k.inc(ins, "DY", 16)
            ins = sy.dma_start(out=k.CKL[:, :], in_=k.CKL_d.ap())
            k.inc(ins, "DY", 16)
            ins = sy.dma_start(out=k.EIDX[:, :], in_=k.EIDX_d.ap())
            k.inc(ins, "DY", 16)
            ins = sy.dma_start(out=k.IOTA256[:, :], in_=k.IOTA_d.ap())
            k.inc(ins, "DY", 16)
            ins = sy.dma_start(out=k.EYE64[:, :], in_=k.EYE64_d.ap())
            k.inc(ins, "DY", 16)
            ins = sy.dma_start(out=k.BR1[:, :], in_=k.br_d.ap())
            k.inc(ins, "DY", 16)
            ins = sy.dma_start(out=k.BUPR[:, :], in_=k.bup_d.ap())
            k.inc(ins, "DY", 16)
            ins = sy.dma_start(out=k.BGTR[:, :], in_=k.bgt_d.ap())
            k.inc(ins, "DY", 16)
            ins = sy.dma_start(out=k.PIOTA[:, :], in_=k.PIO_d.ap())
            k.inc(ins, "DY", 16)
            ins = sy.dma_start(out=k.DIF[:, :], in_=k.DIF_d.ap())
            k.inc(ins, "DY", 16)
            for dst, srcd in ((k.BQ0, k.l0a_bq), (k.BK0, k.l0a_bk),
                              (k.BO0, k.l0a_bo), (k.BO1, k.l1a_bo),
                              (k.BVH0, k.l0a_bv), (k.BL0, k.l0a_bl)):
                ins = sy.dma_start(out=dst[:, :],
                                   in_=srcd.ap().rearrange("c p -> p c"),
                                   allow_slow_non_contiguous=True)
                k.inc(ins, "DY", 16)
            ins = sy.dma_start(out=k.HFS[:, :], in_=k.cinfo.ap())
            k.inc(ins, "DY", 16)
            ins = sy.dma_start(out=k.AROWS[:, :], in_=k.AROWS_d.ap())
            k.inc(ins, "DY", 16)
            ins = sy.dma_start(out=k.BROWS[:, :], in_=k.BROWS_d.ap())
            k.inc(ins, "DY", 16)

        @blk.gpsimd
        def _(gp):
            ins = gp.dma_start(out=k.WR[:, :].rearrange("p (kc e) -> p kc e", e=64),
                               in_=k.wr_d.ap().rearrange("(kc p) e -> p kc e", p=128))
            k.inc(ins, "DG", 16)
            ins = gp.dma_start(out=k.BDN[:, :], in_=k.bdn_d.ap())
            k.inc(ins, "DG", 16)

        @blk.tensor
        def _(te):
            k.w(te, "DY")
            te.matmul(PB[5][:, 0:64], ONESf[0:1, :], k.BR1[:, :],
                      start=True, stop=True)
            te.matmul(PB[3][:, 0:1], ONESf[0:1, :], k.HFS[:, :],
                      start=True, stop=True)
            te.transpose(PB[6][:, 0:64], k.BUPR[:, 0:128], IDf[0:64, 0:64])
            te.transpose(PB[6][:, 64:128], k.BUPR[:, 128:256], IDf[0:64, 0:64])
            te.transpose(PB[4][:, 0:64], k.BGTR[:, 0:128], IDf[0:64, 0:64])
            ins = te.transpose(PB[4][:, 64:128], k.BGTR[:, 128:256], IDf[0:64, 0:64])
            k.inc(ins, "T")

        @blk.vector
        def _(ve):
            k.w(ve, "T")
            ve.tensor_copy(k.HFB[:, :], PB[3][:, 0:1])
            ve.tensor_copy(k.BRB[:, :], PB[5][:, 0:64])
            ve.tensor_copy(k.BUP[:, :], PB[6][:, 0:128])
            ins = ve.tensor_copy(k.BGT[:, :], PB[4][:, 0:128])
            k.inc(ins, "V")
        for tt in range(16):
            @blk.sync
            def _(sy, tt=tt):
                k.w(sy, "V")
                ins = sy.dma_start(out=AFA[:, 0:1024],
                                   in_=x_in[tt * 128:(tt + 1) * 128, :])
                k.inc(ins, "DY", 16)

            for half in range(2):
                @blk.tensor
                def _(te, tt=tt, half=half):
                    if half == 0:
                        k.w(te, "DY")
                    else:
                        k.w(te, "V")
                    for q in range(4):
                        fc = half * 4 + q
                        ins = te.transpose(PB[q][:, 0:128],
                                           AFA[:, fc * 128:(fc + 1) * 128],
                                           IDf[:, :])
                    k.inc(ins, "T")

                @blk.vector
                def _(ve, tt=tt, half=half):
                    k.w(ve, "T")
                    for q in range(4):
                        fc = half * 4 + q
                        ins = ve.tensor_copy(
                            XT[:, fc * 2048 + tt * 128: fc * 2048 + (tt + 1) * 128],
                            PB[q][:, 0:128])
                    k.inc(ins, "V")

        subs = [s for s in (_os.environ.get("K_SUBS") or
                            "f0,a0,f0,F1,A1,F1").split(",") if s]
        if _STOP == "load":
            subs = []
        nf0 = 0
        if any(s == "f0" for s in subs):
            precast_emit(k, blk)
        for s in subs:
            if s == "f0":
                l0ffn_emit(k, blk, nf0)
                nf0 += 1
            elif s == "a0":
                attn_emit(k, blk, 0)
            elif s == "A1":
                attn_emit(k, blk, 1)
            elif s == "F1":
                l1ffn_emit(k, blk, 0)
            else:
                raise ValueError(f"unknown sublayer {s}")
        if _DBG:
            def dbgf(sy):
                k.w(sy, "V")
                k.w(sy, "T")
                k.w(sy, "S")
                dd = k.dbg_d.ap()
                ins = sy.dma_start(out=dd[0:64, 0:256], in_=k.GSA[:, :])
                k.inc(ins, "DY", 16)
                ins = sy.dma_start(out=dd[0:64, 256:512], in_=k.GMA[:, :])
                k.inc(ins, "DY", 16)
                ins = sy.dma_start(out=dd[0:64, 512:532], in_=k.RS[:, :])
                k.inc(ins, "DY", 16)
                ins = sy.dma_start(out=dd[0:128, 1024:2048], in_=AFA[:, 3072:4096])
                k.inc(ins, "DY", 16)
                ins = sy.dma_start(out=dd[0:128, 2048:3072], in_=AFA[:, 2048:3072])
                k.inc(ins, "DY", 16)
                ins = sy.dma_start(out=dd[0:128, 3072:3136], in_=k.THB[:, :])
                k.inc(ins, "DY", 16)
                ins = sy.dma_start(out=k.dbg2_d.ap(), in_=k.AB[:, 0:16384])
                k.inc(ins, "DY", 16)
                ins = sy.dma_start(out=dd[0:128, 4096:4227], in_=AFA[:, 5120:5251])
                k.inc(ins, "DY", 16)
            blk.sync(dbgf)

        for tt in range(16):
            for half in range(2):
                @blk.tensor
                def _(te, tt=tt, half=half):
                    k.w(te, "V")
                    for q in range(4):
                        fc = half * 4 + q
                        ins = te.transpose(PB[q][:, 0:128],
                                           XT[:, fc * 2048 + tt * 128: fc * 2048 + (tt + 1) * 128],
                                           IDf[:, :])
                    k.inc(ins, "T")

                @blk.vector
                def _(ve, tt=tt, half=half):
                    k.w(ve, "T")
                    k.w(ve, "DY")
                    for q in range(4):
                        fc = half * 4 + q
                        ins = ve.tensor_copy(AFA[:, fc * 128:(fc + 1) * 128],
                                             PB[q][:, 0:128])
                    k.inc(ins, "V")

            @blk.sync
            def _(sy, tt=tt):
                k.w(sy, "V")
                ins = sy.dma_start(out=out[tt * 128:(tt + 1) * 128, :],
                                   in_=AFA[:, 0:1024])
                k.inc(ins, "DY", 16)

        @blk.sync
        def _(sy):
            k.w(sy, "DY")

    return nc


def _f32c(a):
    return np.ascontiguousarray(np.asarray(a, dtype=np.float32))


def kernel(**inputs):
    x = np.asarray(inputs["L0_attn_x"], dtype=np.float32)
    x_shards = np.ascontiguousarray(x.reshape(NCORE, TC, D))

    def gi(name, shape):
        a = inputs.get(name)
        if a is None:
            return np.zeros(shape, np.float32)
        return _f32c(a).reshape(shape)

    nc = build(0)
    shared = {
        "l0f_wr": gi("L0_ffn_Wr", (D, 64)),
        "l0f_br": gi("L0_ffn_br", (1, 64)),
        "l0f_wup": gi("L0_ffn_Wup", (64, D, 256)),
        "l0f_bup": gi("L0_ffn_bup", (64, 256)),
        "l0f_wgt": gi("L0_ffn_Wgate", (64, D, 256)),
        "l0f_bgt": gi("L0_ffn_bgate", (64, 256)),
        "l0f_wdn": gi("L0_ffn_Wdown", (64, 256, D)),
        "l0f_bdn": gi("L0_ffn_bdown", (64, D)),
        "l0a_wq": gi("L0_attn_Wq", (D, D)),
        "l0a_bq": gi("L0_attn_bq", (8, 128)),
        "l0a_wl": gi("L0_attn_Wl", (D, 256)),
        "l0a_bl": gi("L0_attn_bl", (2, 128)),
        "l0a_wk": gi("L0_attn_Wk", (256, D)),
        "l0a_bk": gi("L0_attn_bk", (8, 128)),
        "l0a_wv": gi("L0_attn_Wv", (256, D)),
        "l0a_bv": gi("L0_attn_bv", (16, 64)),
        "l0a_wo": gi("L0_attn_Wo", (D, D)),
        "l0a_bo": gi("L0_attn_bo", (8, 128)),
        "l1a_wq": gi("L1_attn_Wq", (D, D)),
        "l1a_wk": gi("L1_attn_Wk", (D, D)),
        "l1a_wv": gi("L1_attn_Wv", (D, D)),
        "l1a_wo": gi("L1_attn_Wo", (D, D)),
        "l1a_bo": gi("L1_attn_bo", (8, 128)),
        "l1f_wg": gi("L1_ffn_w_gate", (2816, D)),
        "l1f_wu": gi("L1_ffn_w_up", (2816, D)),
        "l1f_wd": gi("L1_ffn_w_down", (D, 2816)),
    }
    in_maps = []
    for c in range(NCORE):
        m = dict(shared)
        m["x"] = x_shards[c]
        m["cinfo"] = np.asarray([[float(c % 2)]], np.float32)
        in_maps.append(m)
    res = run_bass_kernel_spmd(nc, in_maps, core_ids=list(range(NCORE)))
    out = np.concatenate([r["out"] for r in res.results], axis=0)
    if _DBG:
        kernel._dbg = [r["dbg"] for r in res.results]
        kernel._dbg2 = [r["dbg2"] for r in res.results]
    return out.reshape(4, 4096, 1024)


if __name__ == "__main__":
    rng = np.random.default_rng(0)
    x = rng.standard_normal((4, 4096, 1024)).astype(np.float32)
    y = kernel(L0_attn_x=x)
    print("identity check:", np.abs(y - x).max())


# version 08 (final)
# speedup vs baseline: 1.1714x; 1.0065x over previous
import math
import numpy as np
import ml_dtypes
import concourse.bass as bass
import concourse.mybir as mybir
from concourse.bass_utils import run_bass_kernel_spmd

F32 = mybir.dt.float32
BF16 = mybir.dt.bfloat16
ALU = mybir.AluOpType
AF = mybir.ActivationFunctionType
AX = mybir.AxisListType

NCORE = 8
TC = 2048
D = 1024
NCH = D // 128

import os as _os
_STOP = _os.environ.get("K_STOP") or None
_DBG = False

CONST_VALS = [1e-5, -0.5, 0.5, 1e-6, 2.0, -1.0, -15.0]


class Ctx:

    def __init__(self, nc):
        self.nc = nc
        self.c = {}
        self.sem = {}
        for name in ["T", "V", "S", "G", "Y", "DY", "DS", "DG", "CC"]:
            self.sem[name] = nc.alloc_semaphore(name)
            self.c[name] = 0

    def bump(self, name, n=1):
        self.c[name] += n
        return self.c[name]

    def w(self, eng, name, val=None):
        eng.wait_ge(self.sem[name], self.c[name] if val is None else val)

    def inc(self, instr, name, n=1):
        instr.then_inc(self.sem[name], n)
        self.bump(name, n)
        return instr

    def vb(self, ve, ins):
        self.inc(ins, "V")
        self.w(ve, "V")


def ln_emit(k, blk, XT, XN, PB, AFA, tag):
    nc = k.nc
    ONES = k.ONESbf
    ONESf = k.ONESf
    for tc in range(4):
        t0 = tc * 512

        @blk.scalar
        def _(sc, tc=tc, t0=t0):
            k.w(sc, "V")
            k.w(sc, "T")
            for fc in range(8):
                ins = sc.activation(out=AFA[:, fc * 512:(fc + 1) * 512],
                                    in_=XT[:, fc * 2048 + t0: fc * 2048 + t0 + 512],
                                    func=AF.Square)
                if fc == 7:
                    k.inc(ins, "S")

        @blk.tensor
        def _(te, tc=tc, t0=t0):
            k.w(te, "S")
            for fc in range(8):
                te.matmul(PB[0][:, :], ONESf[:, :],
                          XT[:, fc * 2048 + t0: fc * 2048 + t0 + 512],
                          start=(fc == 0), stop=(fc == 7))
            for fc in range(8):
                ins = te.matmul(PB[1][:, :], ONESf[:, :],
                                AFA[:, fc * 512:(fc + 1) * 512],
                                start=(fc == 0), stop=(fc == 7))
            k.inc(ins, "T")

        @blk.vector
        def _(ve, tc=tc, t0=t0):
            k.w(ve, "T")
            MB = AFA[:, 4096:4608]
            VR = AFA[:, 4608:5120]
            ve.tensor_scalar(MB, PB[0][:, :], 1.0 / 1024.0, None, ALU.mult)
            ve.tensor_tensor(VR, MB, MB, ALU.mult)
            ins = ve.scalar_tensor_tensor(VR, PB[1][:, :], 1.0 / 1024.0, VR,
                                          ALU.mult, ALU.subtract)
            k.inc(ins, "V")

        @blk.scalar
        def _(sc, tc=tc, t0=t0):
            k.w(sc, "V")
            VR = AFA[:, 4608:5120]
            sc.activation(out=VR, in_=VR, func=AF.Ln, bias=k.cc(1e-5), scale=1.0)
            ins = sc.activation(out=VR, in_=VR, func=AF.Exp, scale=k.cc(-0.5))
            k.inc(ins, "S")

        @blk.vector
        def _(ve, tc=tc, t0=t0):
            k.w(ve, "S")
            MB = AFA[:, 4096:4608]
            VR = AFA[:, 4608:5120]
            TMP = AFA[:, 5120:5632]
            for fc in range(8):
                sl = slice(fc * 2048 + t0, fc * 2048 + t0 + 512)
                ve.tensor_tensor(TMP, XT[:, sl], MB, ALU.subtract)
                ins = ve.tensor_tensor(XN[:, sl], TMP, VR, ALU.mult)
            k.inc(ins, "V")





def sort_network_ops():
    ops = []

    def cmp(pb, pc, a_off, b_off, dims, a_rev=False, b_rev=False):
        ops.append(("cmp", dict(pb=pb, pc=pc, a_off=a_off, b_off=b_off,
                                dims=tuple(dims), a_rev=a_rev, b_rev=b_rev)))

    def cmp2(a_pb, b_pb, pc, b_rev=False):
        ops.append(("cmp2", dict(a_pb=a_pb, b_pb=b_pb, pc=pc, a_rev=False,
                                 b_rev=b_rev)))

    def apply_perm_groups(gsrc):
        lane_map = [0] * 128
        for g in range(16):
            for c in range(8):
                lane_map[g * 8 + c] = gsrc[g] * 8 + c
        ops.append(("perm", tuple(lane_map)))

    n = 1024
    for kk in [2 ** i for i in range(10)]:
        cmp(0, 128, 0, kk, [(2 * kk, n // (2 * kk)), (1, kk)], b_rev=True)
        d = kk // 2
        while d >= 1:
            cmp(0, 128, 0, d, [(2 * d, n // (2 * d)), (1, d)])
            d //= 2
    apply_perm_groups([2 * m for m in range(8)] + [2 * m + 1 for m in range(8)])
    cmp2(0, 64, 64, b_rev=True)
    for d in [512, 256, 128, 64, 32, 16, 8, 4, 2, 1]:
        cmp(0, 128, 0, d, [(2 * d, 1024 // (2 * d)), (1, d)])
    gsrc = [0] * 16
    for m in range(4):
        for u in range(2):
            gsrc[u * 4 + m] = u * 8 + 2 * m
            gsrc[8 + u * 4 + m] = (1 - u) * 8 + 2 * m + 1
    apply_perm_groups(gsrc)
    cmp2(0, 64, 64, b_rev=True)
    cmp2(0, 64, 64)
    cmp2(0, 32, 32)
    cmp2(64, 96, 32)
    for d in [512, 256, 128, 64, 32, 16, 8, 4, 2, 1]:
        cmp(0, 128, 0, d, [(2 * d, 1024 // (2 * d)), (1, d)])
    gsrc = [0] * 16
    for a_i, r in ((0, 0), (1, 2)):
        for q in range(4):
            gsrc[q * 2 + a_i] = (q % 2) * 4 + r + (q // 2) * 8
    for a_i, r in ((0, 1), (1, 3)):
        for w in range(4):
            gsrc[8 + (3 - w) * 2 + a_i] = (w % 2) * 4 + r + (w // 2) * 8
    apply_perm_groups(gsrc)
    cmp2(0, 64, 64, b_rev=True)
    cmp2(0, 64, 64)
    cmp2(0, 32, 32)
    cmp2(64, 96, 32)
    gsrc = [0] * 16
    for a_i in range(2):
        for q in range(4):
            gsrc[(q % 2) * 4 + (q // 2) * 2 + a_i] = q * 2 + a_i
        for j in range(4):
            gsrc[8 + (j % 2) * 4 + (j // 2) * 2 + a_i] = 8 + j * 2 + a_i
    apply_perm_groups(gsrc)
    cmp2(0, 32, 32)
    cmp2(64, 96, 32)
    for d in [512, 256, 128, 64, 32, 16, 8, 4, 2, 1]:
        cmp(0, 128, 0, d, [(2 * d, 1024 // (2 * d)), (1, d)])
    gsrc = [0] * 16
    for q in range(8):
        g = ((q % 2) * 4 + ((q // 2) % 2) * 2 + 0) + (8 if q >= 4 else 0)
        gsrc[q] = g
    for w in range(8):
        g = ((w % 2) * 4 + ((w // 2) % 2) * 2 + 1) + (8 if w >= 4 else 0)
        gsrc[8 + (7 - w)] = g
    apply_perm_groups(gsrc)
    cmp2(0, 64, 64, b_rev=True)
    cmp2(0, 64, 64)
    cmp2(0, 32, 32)
    cmp2(64, 96, 32)
    gsrc = [0] * 16
    for q in range(16):
        gsrc[(q % 4) * 4 + q // 4] = q
    apply_perm_groups(gsrc)
    cmp2(0, 64, 64)
    cmp2(0, 32, 32)
    cmp2(64, 96, 32)
    for d in [512, 256, 128, 64, 32, 16, 8, 4, 2, 1]:
        cmp(0, 128, 0, d, [(2 * d, 1024 // (2 * d)), (1, d)])
    fmap = [(q % 4) * 4 + q // 4 for q in range(16)]
    return ops, fmap


def emit_sort(k, blk, SA, SB_, SCR):
    ops, fmap = sort_network_ops()
    cur, oth = SA, SB_

    def vsec(f):
        blk.vector(f)

    def ysec(f):
        blk.sync(f)

    i = 0
    while i < len(ops):
        op = ops[i]
        if op[0] == "perm":
            lane_map = list(op[1])
            _c, _o = cur, oth

            def yf(sy, lane_map=lane_map, _c=_c, _o=_o):
                k.w(sy, "V")
                for g in range(16):
                    s0 = lane_map[g * 8]
                    ins = sy.dma_start(out=_o[g * 8:(g + 1) * 8, :],
                                       in_=_c[s0:s0 + 8, :])
                    k.bump("DY", 16)
                    ins.then_inc(k.sem["DY"], 16)
            ysec(yf)

            def vf(ve):
                k.w(ve, "DY")
                ins = ve.nop()
                k.inc(ins, "V")
            vsec(vf)
            cur, oth = oth, cur
            i += 1
            continue
        if op[0] == "cmp":
            d = op[1]
            _c, _o = cur, oth

            def vf(ve, d=d, _c=_c, _o=_o):
                (so, co), (si, ci) = d["dims"]
                va = _c.rearrange("p (b x) -> p b x", x=so)
                vo = _o.rearrange("p (b x) -> p b x", x=so)

                def sl(v, off, rev, ci=ci):
                    if rev:
                        if off == 0:
                            return v[:, :, ci - 1::-1]
                        return v[:, :, off + ci - 1:off - 1:-1]
                    return v[:, :, off:off + ci]
                a_off, b_off, brev = d["a_off"], d["b_off"], d["b_rev"]
                ve.tensor_tensor(sl(vo, a_off, False), sl(va, a_off, False),
                                 sl(va, b_off, brev), ALU.min)
                ins = ve.tensor_tensor(sl(vo, b_off, False), sl(va, a_off, brev),
                                       sl(va, b_off, False), ALU.max)
                k.inc(ins, "V")
            vsec(vf)
            cur, oth = oth, cur
            i += 1
            continue
        d = op[1]
        group = [d]
        if d["pc"] == 32:
            d2 = ops[i + 1][1]
            assert ops[i + 1][0] == "cmp2" and d2["pc"] == 32
            group.append(d2)
            i += 2
        else:
            i += 1
        _c, _o = cur, oth

        def vf(ve, group=group, _c=_c, _o=_o):
            for d in group:
                apb, bpb, pc, brev = d["a_pb"], d["b_pb"], d["pc"], d["b_rev"]
                scr = SCR[apb:apb + pc, :]
                if brev:
                    ve.tensor_copy(scr, _c[bpb:bpb + pc, 1023::-1])
                else:
                    ve.tensor_copy(scr, _c[bpb:bpb + pc, :])
                ve.tensor_tensor(_o[apb:apb + pc, :], _c[apb:apb + pc, :],
                                 scr, ALU.min)
                if brev:
                    ins = ve.tensor_tensor(_o[bpb:bpb + pc, :],
                                           _c[apb:apb + pc, 1023::-1],
                                           SCR[apb:apb + pc, 1023::-1], ALU.max)
                else:
                    ins = ve.tensor_tensor(_o[bpb:bpb + pc, :],
                                           _c[apb:apb + pc, :], scr, ALU.max)
            k.inc(ins, "V")
        vsec(vf)
        cur, oth = oth, cur
    return cur, fmap



def l0ffn_emit(k, blk, ci):
    nc = k.nc
    XT, XN, AB, AFA = k.XT, k.XN, k.AB, k.AFA
    PB = k.PB
    ln_emit(k, blk, XT, XN, PB, AFA, f"l0ffn{ci}_ln")

    STOKf = AFA[:, 2048:3072]
    for tt in range(16):
        def tf(te, tt=tt):
            k.w(te, "V")
            for fc in range(8):
                ins = te.matmul(PB[0][:, 0:64], XN[:, fc * 2048 + tt * 128: fc * 2048 + (tt + 1) * 128],
                                k.WR[:, fc * 64:(fc + 1) * 64],
                                start=(fc == 0), stop=(fc == 7))
            k.inc(ins, "T")
        blk.tensor(tf)

        def vf(ve, tt=tt):
            k.w(ve, "T")
            SM = AFA[:, 5120:5184]
            NM = AFA[:, 5184:5185]
            ve.tensor_tensor(SM, PB[0][:, 0:64], k.BRB[:, :], ALU.add)
            ins = ve.tensor_reduce(NM, SM, AX.X, ALU.max, negate=True)
            k.inc(ins, "V")
        blk.vector(vf)

        def sf(sc, tt=tt):
            k.w(sc, "V")
            SM = AFA[:, 5120:5184]
            SE = AFA[:, 5185:5249]
            NM = AFA[:, 5184:5185]
            SSUM = AFA[:, 5249:5250]
            ins = sc.activation(out=SE, in_=SM, func=AF.Exp, bias=NM,
                                scale=1.0)
            k.inc(ins, "S")
        blk.scalar(sf)

        def vf2(ve, tt=tt):
            k.w(ve, "S")
            SE = AFA[:, 5185:5249]
            SSUM = AFA[:, 5249:5250]
            RC = AFA[:, 5250:5251]
            k.vb(ve, ve.tensor_reduce(SSUM, SE, AX.X, ALU.add))
            k.vb(ve, ve.reciprocal(RC, SSUM))
            ins = ve.tensor_tensor(STOKf[:, tt * 64:(tt + 1) * 64], SE,
                                   RC.to_broadcast([128, 64]), ALU.mult)
            k.inc(ins, "V")
        blk.vector(vf2)

        def tf2(te, tt=tt):
            k.w(te, "V")
            k.w(te, "DY")
            ins = te.transpose(PB[1][0:64, 0:128],
                               STOKf[:, tt * 64:(tt + 1) * 64], k.IDfs[:, :])
            k.inc(ins, "T")
        blk.tensor(tf2)

        def vf3(ve, tt=tt):
            k.w(ve, "T")
            ins = ve.tensor_copy(AFA[0:64, 5251:5379], PB[1][0:64, 0:128])
            k.inc(ins, "V")
        blk.vector(vf3)

        def yf(sy, tt=tt):
            k.w(sy, "V")
            ins = sy.dma_start(out=k.sc_in.ap()[:, tt * 128:(tt + 1) * 128],
                               in_=AFA[0:64, 5251:5379])
            k.inc(ins, "DY", 16)
        blk.sync(yf)

    if _STOP == "router":
        return
    def gf(gp):
        k.w(gp, "DY")
        ins = gp.collective_compute(
            "AllGather", ALU.bypass, replica_groups=[list(range(8))],
            ins=[k.sc_in.ap().opt()], outs=[k.sc_out.ap().opt()])
        k.inc(ins, "CC")
    blk.gpsimd(gf)

    def yf2(sy):
        k.w(sy, "CC")
        k.w(sy, "V")
        for jr in range(2):
            src3 = k.sc_out.ap()[:, jr * 1024:(jr + 1) * 1024].rearrange(
                "(j2 e) c -> j2 e c", j2=8)[:, bass.ds(k.pid * 8, 8)]
            dst3 = k.sc_my.ap().rearrange("(j2 r) c -> j2 r c",
                                          r=16)[:, jr * 8:(jr + 1) * 8, :]
            ins = sy.dma_start(out=dst3, in_=src3)
            k.inc(ins, "DY", 16)
        sy.wait_ge(k.sem["DY"], k.c["DY"])
        ins = sy.dma_start(out=AFA[:, 0:1024], in_=k.sc_my.ap())
        k.inc(ins, "DY", 16)
    blk.sync(yf2)

    def vf4(ve):
        k.w(ve, "DY")
        ins = ve.nop()
        k.inc(ins, "V")
    blk.vector(vf4)

    SRT, fmap = emit_sort(k, blk, AFA[:, 0:1024], AFA[:, 1024:2048],
                          AFA[:, 5120:6144])

    if _STOP == "sort":
        return
    def vf5(ve):
        seg = SRT.rearrange("p (s x) -> p s x", x=64)
        ve.tensor_reduce(k.GSseg[:, :], seg, AX.X, ALU.add)
        ins = ve.tensor_reduce(k.GMseg[:, :], seg, AX.X, ALU.min)
        k.inc(ins, "V")
    blk.vector(vf5)

    def yf3(sy):
        k.w(sy, "V")
        for off, seg_t in ((0, k.GSseg), (256, k.GMseg)):
            for gh in range(4):
                o4 = k.gg_in.ap()[:, off:off + 256].rearrange(
                    "c (gl g2 s) -> gl g2 c s", gl=4, g2=4)[:, gh]
                ins = sy.dma_start(out=o4, in_=seg_t[gh * 32:(gh + 1) * 32, :])
                k.inc(ins, "DY", 16)
    blk.sync(yf3)

    def gf2(gp):
        k.w(gp, "DY")
        ins = gp.collective_compute(
            "AllGather", ALU.bypass, replica_groups=[list(range(8))],
            ins=[k.gg_in.ap().opt()], outs=[k.gg_out.ap().opt()])
        k.inc(ins, "CC")
    blk.gpsimd(gf2)

    def yf4(sy):
        k.w(sy, "CC")
        ins = sy.dma_start(out=k.GSA[:, :], in_=k.gg_out.ap()[:, 0:256])
        k.inc(ins, "DY", 16)
        ins = sy.dma_start(out=k.GMA[:, :], in_=k.gg_out.ap()[:, 256:512])
        k.inc(ins, "DY", 16)
    blk.sync(yf4)

    if _STOP == "gs":
        return
    RS = k.RS
    NR = 44

    def vf6(ve):
        k.w(ve, "DY")
        ve.memset(RS[:, 0:1], 1e-38)
        ins = ve.memset(RS[:, 1:2], 64.0)
        k.inc(ins, "V")
    blk.vector(vf6)

    for r in range(NR):
        def vf7(ve, r=r):
            k.w(ve, "T")
            k.vb(ve, ve.tensor_tensor(RS[:, 2:3], RS[:, 0:1],
                                       k.CKL[:, r:r + 1], ALU.mult))
            ve.tensor_tensor(AFA[0:64, 5120:5376], k.GSA[:, :],
                             RS[:, 2:3].to_broadcast([64, 256]), ALU.is_gt)
            ins = ve.tensor_reduce(RS[:, 16:17], AFA[0:64, 5120:5376],
                                   AX.X, ALU.add)
            k.inc(ins, "V")
        blk.vector(vf7)

        def tf3(te, r=r):
            k.w(te, "V")
            ins = te.matmul(PB[0][0:64, 0:1], k.ONESf[0:64, 0:64],
                            RS[:, 16:17], start=True, stop=True)
            k.inc(ins, "T")
        blk.tensor(tf3)

        def vf8(ve, r=r):
            k.w(ve, "T")
            k.vb(ve, ve.tensor_scalar(RS[:, 3:4], PB[0][0:64, 0:1], 4096.0,
                                       None, ALU.is_ge))
            k.vb(ve, ve.tensor_tensor(RS[:, 4:5], RS[:, 2:3], RS[:, 0:1], ALU.subtract))
            k.vb(ve, ve.tensor_tensor(RS[:, 4:5], RS[:, 4:5], RS[:, 3:4], ALU.mult))
            k.vb(ve, ve.tensor_tensor(RS[:, 0:1], RS[:, 0:1], RS[:, 4:5], ALU.add))
            k.vb(ve, ve.tensor_tensor(RS[:, 4:5], RS[:, 1:2], RS[:, 2:3], ALU.subtract))
            k.vb(ve, ve.tensor_tensor(RS[:, 4:5], RS[:, 4:5], RS[:, 3:4], ALU.mult))
            ins = ve.tensor_tensor(RS[:, 1:2], RS[:, 2:3], RS[:, 4:5], ALU.add)
            k.inc(ins, "V")
        blk.vector(vf8)

    def vf9(ve):
        k.w(ve, "V")
        TMP = AFA[0:64, 5120:5376]
        ve.tensor_tensor(TMP, k.GSA[:, :], RS[:, 1:2].to_broadcast([64, 256]),
                         ALU.is_le)
        ve.tensor_tensor(TMP, TMP, k.GSA[:, :], ALU.mult)
        ins = ve.tensor_reduce(RS[:, 15:16], TMP, AX.X, ALU.max)
        k.inc(ins, "V")
    blk.vector(vf9)

    def tf4(te):
        k.w(te, "V")
        ins = te.transpose(PB[0][0:1, 0:64], RS[:, 15:16], k.IDfs[0:64, 0:64])
        k.inc(ins, "T")
    blk.tensor(tf4)

    def vf10(ve):
        k.w(ve, "T")
        ins = ve.tensor_reduce(AFA[0:1, 5120:5121], PB[0][0:1, 0:64], AX.X, ALU.max)
        k.inc(ins, "V")
    blk.vector(vf10)

    def tf5(te):
        k.w(te, "V")
        ins = te.matmul(PB[1][0:64, 0:1], k.ONESf[0:1, 0:64],
                        AFA[0:1, 5120:5121], start=True, stop=True)
        k.inc(ins, "T")
    blk.tensor(tf5)

    def vf11(ve):
        k.w(ve, "T")
        TMP = AFA[0:64, 5120:5376]
        ins = ve.tensor_copy(RS[:, 5:6], PB[1][0:64, 0:1])
        k.inc(ins, "V")
        k.w(ve, "V")
        ve.tensor_tensor(TMP, k.GSA[:, :], RS[:, 5:6].to_broadcast([64, 256]),
                         ALU.is_gt)
        ve.tensor_reduce(RS[:, 6:7], TMP, AX.X, ALU.add)
        ve.tensor_tensor(TMP, k.GSA[:, :], RS[:, 5:6].to_broadcast([64, 256]),
                         ALU.is_equal)
        ins = ve.tensor_reduce(RS[:, 7:8], TMP, AX.X, ALU.add)
        k.inc(ins, "V")
    blk.vector(vf11)

    def tf6(te):
        k.w(te, "V")
        ins = te.matmul(PB[0][0:64, 0:1], k.ONESf[0:64, 0:64],
                        RS[:, 6:7], start=True, stop=True)
        k.inc(ins, "T")
    blk.tensor(tf6)

    def vf12(ve):
        k.w(ve, "T")
        k.w(ve, "V")
        k.vb(ve, ve.tensor_scalar(RS[:, 8:9], PB[0][0:64, 0:1], -1.0, 4096.0,
                                  ALU.mult, ALU.add))
        k.vb(ve, ve.tensor_scalar(RS[:, 9:10], RS[:, 6:7], 64.0, None, ALU.mult))
        k.vb(ve, ve.tensor_tensor(RS[:, 9:10], RS[:, 9:10], k.EIDX[:, :], ALU.add))
        ins = ve.tensor_scalar(RS[:, 10:11], RS[:, 7:8], 1.0, None, ALU.is_ge)
        k.inc(ins, "V")
    blk.vector(vf12)

    def tf7(te):
        k.w(te, "V")
        te.transpose(PB[0][0:1, 0:64], RS[:, 9:10], k.IDfs[0:64, 0:64])
        ins = te.transpose(PB[0][0:1, 64:128], RS[:, 10:11], k.IDfs[0:64, 0:64])
        k.inc(ins, "T")
    blk.tensor(tf7)

    def vf13(ve):
        k.w(ve, "T")
        ins = ve.tensor_copy(AFA[0:1, 5120:5248], PB[0][0:1, 0:128])
        k.inc(ins, "V")
    blk.vector(vf13)

    def tf8(te):
        k.w(te, "V")
        ins = te.matmul(PB[1][0:64, 0:128], k.ONESf[0:1, 0:64],
                        AFA[0:1, 5120:5248], start=True, stop=True)
        k.inc(ins, "T")
    blk.tensor(tf8)

    def vf14(ve):
        k.w(ve, "T")
        k.w(ve, "V")
        CMP = AFA[0:64, 5376:5440]
        ve.tensor_tensor(CMP, PB[1][0:64, 0:64], RS[:, 9:10].to_broadcast([64, 64]),
                         ALU.is_lt)
        ve.tensor_tensor(CMP, CMP, PB[1][0:64, 64:128], ALU.mult)
        k.vb(ve, ve.tensor_reduce(RS[:, 11:12], CMP, AX.X, ALU.add))
        k.vb(ve, ve.tensor_tensor(RS[:, 12:13], RS[:, 11:12], RS[:, 8:9], ALU.is_lt))
        k.vb(ve, ve.tensor_tensor(RS[:, 12:13], RS[:, 12:13], RS[:, 10:11], ALU.mult))
        k.vb(ve, ve.tensor_tensor(RS[:, 13:14], RS[:, 6:7], RS[:, 12:13], ALU.add))
        k.vb(ve, ve.tensor_scalar(RS[:, 14:15], RS[:, 13:14], -1.0, 256.0,
                                  ALU.mult, ALU.add))
        MSK = AFA[0:64, 5376:5632]
        ve.tensor_tensor(MSK, k.IOTA256[:, :], RS[:, 14:15].to_broadcast([64, 256]),
                         ALU.is_lt)
        ve.tensor_scalar(MSK, MSK, 1e30, None, ALU.mult)
        ve.tensor_tensor(MSK, MSK, k.GMA[:, :], ALU.add)
        ins = ve.tensor_reduce(RS[:, 5:6], MSK, AX.X, ALU.min)
        k.inc(ins, "V")
    blk.vector(vf14)

    def tf9(te):
        k.w(te, "V")
        ins = te.transpose(PB[0][0:1, 0:64], RS[:, 5:6], k.IDfs[0:64, 0:64])
        k.inc(ins, "T")
    blk.tensor(tf9)

    def vf15(ve):
        k.w(ve, "T")
        ins = ve.tensor_copy(AFA[0:1, 5120:5184], PB[0][0:1, 0:64])
        k.inc(ins, "V")
    blk.vector(vf15)

    def tf10(te):
        k.w(te, "V")
        ins = te.matmul(PB[1][:, 0:64], k.ONESf[0:1, :],
                        AFA[0:1, 5120:5184], start=True, stop=True)
        k.inc(ins, "T")
    blk.tensor(tf10)

    def vf16(ve):
        k.w(ve, "T")
        ins = ve.tensor_copy(k.THB[:, :], PB[1][:, 0:64])
        k.inc(ins, "V")
    blk.vector(vf16)

    FSTf = AFA[:, 3072:4096]
    for tt in range(16):
        def vf17(ve, tt=tt):
            if tt == 0:
                k.w(ve, "V")
            MS = AFA[:, 5120:5184]
            SUM = AFA[:, 5184:5185]
            RC = AFA[:, 5185:5186]
            sl = slice(tt * 64, (tt + 1) * 64)
            ve.tensor_tensor(MS, STOKf[:, sl], k.THB[:, :], ALU.is_ge)
            ve.tensor_tensor(FSTf[:, sl], MS, STOKf[:, sl], ALU.mult)
            k.vb(ve, ve.tensor_reduce(SUM, FSTf[:, sl], AX.X, ALU.add))
            k.vb(ve, ve.tensor_scalar(SUM, SUM, 1e-6, None, ALU.add))
            k.vb(ve, ve.reciprocal(RC, SUM))
            ins = ve.tensor_tensor(FSTf[:, sl], FSTf[:, sl],
                                   RC.to_broadcast([128, 64]), ALU.mult)
            k.inc(ins, "V")
        blk.vector(vf17)

        def tf11(te, tt=tt):
            k.w(te, "V")
            ins = te.transpose(PB[1][0:64, 0:128], FSTf[:, tt * 64:(tt + 1) * 64],
                               k.IDfs[:, :])
            k.inc(ins, "T")
        blk.tensor(tf11)

        def vf18(ve, tt=tt):
            k.w(ve, "T")
            ins = ve.tensor_copy(k.FSTT[:, tt * 128:(tt + 1) * 128],
                                 PB[1][0:64, 0:128])
            k.inc(ins, "V")
        blk.vector(vf18)

    if _STOP == "fs":
        return
    moe_emit(k, blk, ci)


def precast_emit(k, blk):
    def gpc(gp):
        for e in range(64):
            ins = gp.dma_start(
                out=k.wup_b.ap()[e].rearrange("p (kc h) -> p kc h", h=256),
                in_=k.wup_d.ap()[e].rearrange("(kc p) h -> p kc h", p=128))
            k.inc(ins, "G", 16)
            ins = gp.dma_start(
                out=k.wgt_b.ap()[e].rearrange("p (kc h) -> p kc h", h=256),
                in_=k.wgt_d.ap()[e].rearrange("(kc p) h -> p kc h", p=128))
            k.inc(ins, "G", 16)
            ins = gp.dma_start(
                out=k.wdn_b.ap()[e].rearrange("p (c d) -> p c d", d=1024),
                in_=k.wdn_d.ap()[e].rearrange("(c p) d -> p c d", p=128))
            k.inc(ins, "G", 16)
    blk.gpsimd(gpc)


def moe_emit(k, blk, ci):
    nc = k.nc
    XT, XN, AB, AFA = k.XT, k.XN, k.AB, k.AFA
    PB = k.PB
    WUP = [AB[:, 0:2048], AB[:, 2048:4096]]
    WGT = [AB[:, 4096:6144], AB[:, 6144:8192]]
    WDN = [AB[:, 8192:10240], AB[:, 10240:12288]]
    SU0 = AFA[:, 0:256]
    SU1 = AFA[:, 512:768]
    GF0 = AFA[:, 256:512]
    GF1 = AFA[:, 768:1024]
    HP = [AB[:, 14848:15360], AB[:, 15360:15872]]

    def wload(sy, it, e):
        b = it % 2
        sy.wait_ge(k.sem["G"], min(e + 1, 64) * 48)
        for buf, srcd in ((WUP[b], k.wup_b), (WGT[b], k.wgt_b),
                          (WDN[b], k.wdn_b)):
            ins = sy.dma_start(out=buf, in_=srcd.ap()[e])
            k.inc(ins, "DS", 16)

    def grep(gp):
        ins = gp.dma_start(out=k.REP, in_=k.REP_d.ap())
        k.inc(ins, "DG", 16)
    blk.gpsimd(grep)

    dg0 = k.c["DS"]
    rep_dg = k.c["DG"]
    NP = int(_os.environ.get("K_NP", "8"))
    EN = int(_os.environ.get("K_NE", "64"))
    if EN == 0:
        return

    def gpre(sy):
        wload(sy, 0, 0)
        wload(sy, 1, 1)
    blk.sync(gpre)

    if EN < 0:
        def vw(ve):
            ve.wait_ge(k.sem["DS"], dg0 + 96)
            ins = ve.nop()
            k.inc(ins, "V")
        blk.vector(vw)
        return

    for P in range(NP):
        t0 = P * 256
        for e in range(EN):
            it = P * EN + e
            b = it % 2
            T0, S0, V0 = k.c["T"], k.c["S"], k.c["V"]
            fsb_vwait = k.c["V"] if e == 0 else V0 - 2
            dg_wait = dg0 + 48 * (it + 1)

            def tf(te, P=P, e=e, b=b, t0=t0, T0=T0, S0=S0, V0=V0,
                   fsb_vwait=fsb_vwait, dg_wait=dg_wait):
                te.wait_ge(k.sem["DS"], dg_wait)
                te.wait_ge(k.sem["DG"], rep_dg)
                te.wait_ge(k.sem["V"], fsb_vwait)
                te.matmul(PB[2][:, 0:256],
                          k.REP[:, e * 128:(e + 1) * 128],
                          k.FSTT[:, t0:t0 + 256], start=True, stop=True)
                te.wait_ge(k.sem["S"], S0)
                for kc in range(8):
                    ins = te.matmul(PB[0][:, 0:256],
                              WUP[b][:, kc * 256: kc * 256 + 128],
                              XN[:, kc * 2048 + t0: kc * 2048 + t0 + 256],
                              start=(kc == 0), stop=(kc == 7))
                k.inc(ins, "T")
                for kc in range(8):
                    ins = te.matmul(PB[0][:, 256:512],
                              WUP[b][:, kc * 256 + 128: kc * 256 + 256],
                              XN[:, kc * 2048 + t0: kc * 2048 + t0 + 256],
                              start=(kc == 0), stop=(kc == 7))
                k.inc(ins, "T")
                te.wait_ge(k.sem["V"], V0)
                for kc in range(8):
                    ins = te.matmul(PB[1][:, 0:256],
                              WGT[b][:, kc * 256: kc * 256 + 128],
                              XN[:, kc * 2048 + t0: kc * 2048 + t0 + 256],
                              start=(kc == 0), stop=(kc == 7))
                k.inc(ins, "T")
                for kc in range(8):
                    ins = te.matmul(PB[1][:, 256:512],
                              WGT[b][:, kc * 256 + 128: kc * 256 + 256],
                              XN[:, kc * 2048 + t0: kc * 2048 + t0 + 256],
                              start=(kc == 0), stop=(kc == 7))
                k.inc(ins, "T")
                if e > 0:
                    ep = it - 1
                    hb, wb = HP[ep % 2], WDN[ep % 2]
                    for ehc in range(2):
                        for ds in range(8):
                            ins = te.matmul(PB[3 + ds // 2][:, (ds % 2) * 256:(ds % 2) * 256 + 256],
                                      wb[:, ehc * 1024 + ds * 128: ehc * 1024 + (ds + 1) * 128],
                                      hb[:, ehc * 256:(ehc + 1) * 256],
                                      start=False, stop=False)
                else:
                    for ds in range(8):
                        ins = te.matmul(PB[3 + ds // 2][:, (ds % 2) * 256:(ds % 2) * 256 + 256],
                                  k.BDN[:, ds * 128:(ds + 1) * 128],
                                  k.FSTT[:, t0:t0 + 256],
                                  start=(ds % 2 == 0), stop=False)
                k.inc(ins, "T")
            blk.tensor(tf)

            def sf(sc, e=e, T0=T0, V0=V0):
                sc.wait_ge(k.sem["T"], T0 + 2)
                sc.wait_ge(k.sem["V"], max(V0 - 3, 0))
                ins = sc.activation(out=SU0, in_=PB[0][:, 0:256],
                                    func=AF.Silu, bias=k.BUP[:, e:e + 1],
                                    scale=1.0)
                k.inc(ins, "S")
                sc.wait_ge(k.sem["V"], max(V0 - 2, 0))
                ins = sc.activation(out=SU1, in_=PB[0][:, 256:512],
                                    func=AF.Silu, bias=k.BUP[:, 64 + e:64 + e + 1],
                                    scale=1.0)
                k.inc(ins, "S")
            blk.scalar(sf)

            def vf(ve, e=e, it=it, T0=T0, S0=S0, V0=V0):
                ve.wait_ge(k.sem["S"], S0 + 1)
                ins = ve.tensor_tensor(GF0, SU0, PB[2][:, 0:256], ALU.mult)
                k.inc(ins, "V")
                ve.wait_ge(k.sem["S"], S0 + 2)
                ins = ve.tensor_tensor(GF1, SU1, PB[2][:, 0:256], ALU.mult)
                k.inc(ins, "V")
                ve.wait_ge(k.sem["T"], T0 + 4)
                ins = ve.scalar_tensor_tensor(HP[it % 2][:, 0:256], PB[1][:, 0:256],
                                              k.BGT[:, e:e + 1], GF0,
                                              ALU.add, ALU.mult)
                k.inc(ins, "V")
                ins = ve.scalar_tensor_tensor(HP[it % 2][:, 256:512], PB[1][:, 256:512],
                                              k.BGT[:, 64 + e:64 + e + 1], GF1,
                                              ALU.add, ALU.mult)
                k.inc(ins, "V")
            blk.vector(vf)

            if it + 2 < NP * EN:
                nit = it + 2
                dwait = T0 + 10 if e < EN - 1 else T0 + 6
                def gf3(sy, nit=nit, dwait=dwait):
                    sy.wait_ge(k.sem["T"], dwait)
                    wload(sy, nit, nit % EN)
                blk.sync(gf3)

        def tf12(te, P=P, t0=t0, itl=P * EN + EN - 1):
            k.w(te, "V")
            hb, wb = HP[itl % 2], WDN[itl % 2]
            for ehc in range(2):
                for ds in range(8):
                    ins = te.matmul(PB[3 + ds // 2][:, (ds % 2) * 256:(ds % 2) * 256 + 256],
                              wb[:, ehc * 1024 + ds * 128: ehc * 1024 + (ds + 1) * 128],
                              hb[:, ehc * 256:(ehc + 1) * 256],
                              start=False, stop=(ehc == 1 and ds % 2 == 1))
            k.inc(ins, "T")
        blk.tensor(tf12)

        def vfa(ve, P=P, t0=t0):
            k.w(ve, "T")
            for ds in range(8):
                xsl = slice(ds * 2048 + t0, ds * 2048 + t0 + 256)
                ins = ve.scalar_tensor_tensor(
                    XT[:, xsl], PB[3 + ds // 2][:, (ds % 2) * 256:(ds % 2) * 256 + 256],
                    0.5, XT[:, xsl], ALU.mult, ALU.add)
            k.inc(ins, "V")
        blk.vector(vfa)



def attn_emit(k, blk, layer):
    nc = k.nc
    XT, XN, AB, AFA, PB = k.XT, k.XN, k.AB, k.AFA, k.PB
    ln_emit(k, blk, XT, XN, PB, AFA, f"l{layer}a_ln")

    XOT = AB[:, 0:4096]
    LATF = AB[:, 0:8192]
    KP = AB[:, 8192:12288]
    LATO = AB[:, 8192:12288]
    VT = AB[:, 12288:16448]
    QP = AB[:, 16448:18496]
    EB = AB[:, 18496:19520]
    CTX = AB[:, 19520:19776]
    CTSTG = AB[0:64, 19776:20032]
    WQP = AB[:, 20032:21056]
    WOP = AB[:, 21056:22080]
    WKP = AB[:, 22080:23104]
    WVP = AB[:, 23104:24128]
    WLP = AB[:, 24128:26176]
    DCOPY = AFA[0:1, 0:256]
    RD = AFA[0:1, 256:512]
    RDB = AFA[0:64, 1024:1280]
    TBL = AFA[:, 1280:1372]
    NKT = 32
    SL8 = [0.7071067811865476 ** (h + 1) / 8.0 for h in range(16)]
    nkv = 2 if layer == 0 else 8

    if layer == 0:
        def gw0(gp):
            ins = gp.dma_start(out=WLP.rearrange("p (kc m) -> p kc m", m=256),
                               in_=k.l0a_wl.ap().rearrange("(kc p) m -> p kc m", p=128))
            k.inc(ins, "DG", 16)
        blk.gpsimd(gw0)
        for lc in range(2):
            for tb in range(4):
                def tf(te, lc=lc, tb=tb):
                    k.w(te, "DG")
                    k.w(te, "V")
                    k.w(te, "S")
                    for kc in range(8):
                        ins = te.matmul(
                            PB[4][:, :],
                            WLP[:, kc * 256 + lc * 128: kc * 256 + lc * 128 + 128],
                            XN[:, kc * 2048 + tb * 512: kc * 2048 + tb * 512 + 512],
                            start=(kc == 0), stop=(kc == 7))
                    k.inc(ins, "T")
                blk.tensor(tf)

                def vf(ve, lc=lc, tb=tb):
                    k.w(ve, "T")
                    o = lc * 2048 + tb * 512
                    ins = ve.tensor_scalar(LATO[:, o:o + 512], PB[4][:, :],
                                           k.BL0[:, lc:lc + 1], None, ALU.add)
                    k.inc(ins, "V")
                blk.vector(vf)

        def yl(sy):
            k.w(sy, "V")
            ins = sy.dma_start(out=k.lat_in.ap(), in_=LATO)
            k.inc(ins, "DY", 16)
        blk.sync(yl)

        def gl(gp):
            k.w(gp, "DY")
            ins = gp.collective_compute(
                "AllGather", ALU.bypass,
                replica_groups=[[0, 1], [2, 3], [4, 5], [6, 7]],
                ins=[k.lat_in.ap().opt()], outs=[k.lat_out.ap().opt()])
            k.inc(ins, "CC")
        blk.gpsimd(gl)

        def yl2(sy):
            k.w(sy, "CC")
            for lc in range(2):
                for h in range(2):
                    ins = sy.dma_start(
                        out=LATF[:, lc * 4096 + h * 2048: lc * 4096 + (h + 1) * 2048],
                        in_=k.lat_out.ap()[h * 128:(h + 1) * 128,
                                           lc * 2048:(lc + 1) * 2048])
                    k.inc(ins, "DY", 16)
        blk.sync(yl2)

        def vsig(ve):
            k.w(ve, "DY")
            k.inc(ve.nop(), "V")
        blk.vector(vsig)
    else:
        def yx(sy):
            k.w(sy, "V")
            ins = sy.dma_start(out=k.xn_in.ap(), in_=XN[:, :])
            k.inc(ins, "DY", 16)
        blk.sync(yx)

        def gx(gp):
            k.w(gp, "DY")
            ins = gp.collective_compute(
                "AllGather", ALU.bypass,
                replica_groups=[[0, 1], [2, 3], [4, 5], [6, 7]],
                ins=[k.xn_in.ap().opt()], outs=[k.xn_out.ap().opt()])
            k.inc(ins, "CC")
        blk.gpsimd(gx)

        def vsig2(ve):
            ve.wait_ge(k.sem["CC"], k.c["CC"])
            k.inc(ve.nop(), "V")
        blk.vector(vsig2)

    def vones(ve):
        for t in range(NKT):
            ve.memset(VT[:, t * 130 + 64: t * 130 + 65], 1.0)
            ins = ve.memset(VT[:, t * 130 + 129: t * 130 + 130], 1.0)
        k.inc(ins, "V")
        if layer == 1:
            DIFH = AFA[:, 512:768]
            ins = ve.tensor_scalar(k.MC0[:, :], k.HFB[:, :], 2048.0, None,
                                   ALU.mult)
            k.vb(ve, ins)
            ve.tensor_scalar(DIFH, k.DIF[:, :], k.MC0[:, :], None,
                             ALU.subtract)
            for mt, off in ((k.M0, 0.0), (k.M1, -128.0), (k.M2, -2048.0),
                            (k.M3, -2176.0)):
                ins = ve.tensor_scalar(mt[:, :], DIFH, off, None, ALU.is_le)
            k.inc(ins, "V")
            ins = ve.tensor_scalar(k.HFROW[:, :], k.ONESf[0:1, :],
                                   k.HFS[0:1, 0:1], None, ALU.mult)
            k.inc(ins, "V")
    blk.vector(vones)

    for hp in range(8):
        wq_d = k.l0a_wq if layer == 0 else k.l1a_wq
        wk_d = k.l0a_wk if layer == 0 else k.l1a_wk
        wv_d = k.l0a_wv if layer == 0 else k.l1a_wv
        wo_d = k.l0a_wo if layer == 0 else k.l1a_wo

        def gwp(gp, hp=hp, wq_d=wq_d, wk_d=wk_d, wv_d=wv_d, wo_d=wo_d):
            k.w(gp, "T")
            ins = gp.dma_start(
                out=WQP.rearrange("p (kc m) -> p kc m", m=128),
                in_=wq_d.ap()[:, hp * 128:(hp + 1) * 128].rearrange(
                    "(kc p) m -> p kc m", p=128))
            k.inc(ins, "DG", 16)
            ins = gp.dma_start(
                out=WKP[:, 0:nkv * 128].rearrange("p (kc m) -> p kc m", m=128),
                in_=wk_d.ap()[:, hp * 128:(hp + 1) * 128].rearrange(
                    "(kc p) m -> p kc m", p=128))
            k.inc(ins, "DG", 16)
            ins = gp.dma_start(
                out=WVP[:, 0:nkv * 128].rearrange("p (kc m) -> p kc m", m=128),
                in_=wv_d.ap()[:, hp * 128:(hp + 1) * 128].rearrange(
                    "(kc p) m -> p kc m", p=128))
            k.inc(ins, "DG", 16)
            ins = gp.dma_start(out=WOP,
                               in_=wo_d.ap()[hp * 128:(hp + 1) * 128, :])
            k.inc(ins, "DG", 16)
        blk.gpsimd(gwp)

        for tb in range(4):
            def tq(te, hp=hp, tb=tb):
                k.w(te, "DG")
                k.w(te, "V")
                pb = PB[4 + tb % 2]
                for kc in range(8):
                    ins = te.matmul(
                        pb[:, :], WQP[:, kc * 128:(kc + 1) * 128],
                        XN[:, kc * 2048 + tb * 512: kc * 2048 + tb * 512 + 512],
                        start=(kc == 0), stop=(kc == 7))
                k.inc(ins, "T")
            blk.tensor(tq)

            def vq(ve, hp=hp, tb=tb, layer=layer):
                k.w(ve, "T")
                pb = PB[4 + tb % 2]
                bias = k.BQ0[:, hp:hp + 1] if layer == 0 else 0.0
                ins = ve.tensor_scalar(QP[:, tb * 512:(tb + 1) * 512],
                                       pb[:, :], bias, None, ALU.add)
                k.inc(ins, "V")
            blk.vector(vq)

        for kb in range(8):
            if layer == 1:
                def yxo(sy, kb=kb):
                    k.w(sy, "T")
                    k.w(sy, "CC")
                    h = kb // 4
                    ltk = (kb % 4) * 512
                    src = k.xn_out.ap()[h * 128:(h + 1) * 128, :].rearrange(
                        "p (c n) -> p c n", c=8)[:, :, ltk:ltk + 512]
                    ins = sy.dma_start(
                        out=XOT.rearrange("p (c n) -> p c n", n=512), in_=src)
                    k.inc(ins, "DY", 16)
                blk.sync(yxo)

            def tk(te, hp=hp, kb=kb, layer=layer):
                if layer == 1:
                    k.w(te, "DY")
                k.w(te, "V")
                pb = PB[4 + kb % 2]
                for kc in range(nkv):
                    if layer == 0:
                        rhs = LATF[:, kc * 4096 + kb * 512: kc * 4096 + kb * 512 + 512]
                    else:
                        rhs = XOT[:, kc * 512:(kc + 1) * 512]
                    ins = te.matmul(pb[:, :], WKP[:, kc * 128:(kc + 1) * 128],
                                    rhs, start=(kc == 0), stop=(kc == nkv - 1))
                k.inc(ins, "T")
            blk.tensor(tk)

            def vk(ve, hp=hp, kb=kb, layer=layer):
                k.w(ve, "T")
                pb = PB[4 + kb % 2]
                bias = k.BK0[:, hp:hp + 1] if layer == 0 else 0.0
                ins = ve.tensor_scalar(KP[:, kb * 512:(kb + 1) * 512],
                                       pb[:, :], bias, None, ALU.add)
                k.inc(ins, "V")
            blk.vector(vk)

            for ti in range(4):
                t = kb * 4 + ti
                def tv(te, hp=hp, kb=kb, ti=ti, t=t, layer=layer):
                    k.w(te, "V")
                    pb = PB[6][:, (ti % 2) * 128:(ti % 2) * 128 + 128]
                    for kc in range(nkv):
                        if layer == 0:
                            lhsT = LATF[:, kc * 4096 + t * 128: kc * 4096 + t * 128 + 128]
                        else:
                            lhsT = XOT[:, kc * 512 + ti * 128: kc * 512 + ti * 128 + 128]
                        ins = te.matmul(pb, lhsT, WVP[:, kc * 128:(kc + 1) * 128],
                                        start=(kc == 0), stop=(kc == nkv - 1))
                    k.inc(ins, "T")
                blk.tensor(tv)

                def vv(ve, t=t, ti=ti):
                    k.w(ve, "T")
                    pb = PB[6][:, (ti % 2) * 128:(ti % 2) * 128 + 128]
                    ve.tensor_copy(VT[:, t * 130: t * 130 + 64], pb[:, 0:64])
                    ins = ve.tensor_copy(VT[:, t * 130 + 65: t * 130 + 129],
                                         pb[:, 64:128])
                    k.inc(ins, "V")
                blk.vector(vv)

        k.vt_ready = k.c["V"]

        if layer == 1:
            def ttb(te, hp=hp):
                k.w(te, "V")
                te.matmul(PB[5][:, 0:92], k.ONESf[0:1, :],
                          k.AROWS[0:1, hp * 92:(hp + 1) * 92],
                          start=True, stop=False)
                ins = te.matmul(PB[5][:, 0:92], k.HFROW[:, :],
                                k.BROWS[0:1, hp * 92:(hp + 1) * 92],
                                start=False, stop=True)
                k.inc(ins, "T")
            blk.tensor(ttb)

            def vtb(ve, hp=hp):
                k.w(ve, "T")
                PCOL = AFA[:, 801:802]
                for head in range(2):
                    sl8 = SL8[2 * hp + head]
                    ins = ve.tensor_scalar(PCOL, k.PIOTA[:, :], sl8, None,
                                           ALU.mult)
                    k.vb(ve, ins)
                    ins = ve.tensor_scalar(
                        TBL[:, head * 46:(head + 1) * 46],
                        PB[5][:, head * 46:(head + 1) * 46], PCOL, None,
                        ALU.add)
                k.inc(ins, "V")
            blk.vector(vtb)
        tbl_done = k.c["V"]

        for qb in range(8):
            for head in range(2):
                gh = 2 * hp + head
                diag = sorted({2 * qb, 2 * qb + 1, 16 + 2 * qb, 17 + 2 * qb}
                              ) if layer == 1 else []
                T0, S0 = k.c["T"], k.c["S"]
                LAG = int(_os.environ.get("K_LAG", "2"))
                NR3 = [0, 1, 3 - head] if LAG >= 2 else [0, 1]
                NRN = len(NR3)
                V_pre = k.c["V"]
                sexp = [S0 + t + 1 for t in range(NKT)]
                tsc = [0] * NKT
                tctx = [0] * NKT
                for t in range(LAG):
                    tsc[t] = T0 + t + 1
                for t in range(LAG, NKT):
                    tsc[t] = T0 + 2 * t - LAG + 1
                for t in range(NKT - LAG):
                    tctx[t] = T0 + 2 * t + LAG + 2
                for i, t in enumerate(range(NKT - LAG, NKT)):
                    tctx[t] = T0 + 2 * (NKT - LAG) + LAG + i + 1

                vmask = {}
                if layer == 1:
                    V0m = k.c["V"]
                    for i, t in enumerate(diag):
                        vmask[t] = V0m + i + 1

                    def vmk(ve, qb=qb, diag=diag, sexp=sexp):
                        for t in diag:
                            ve.wait_ge(k.sem["S"], sexp[t])
                            u = (t - 2 * qb) % 2
                            lo = t < 16
                            mm = [[k.M2, k.M3], [k.M0, k.M1]][lo][u][:, :]
                            ins = ve.tensor_tensor(
                                EB[:, (t % 4) * 256:(t % 4 + 1) * 256],
                                EB[:, (t % 4) * 256:(t % 4 + 1) * 256],
                                mm, ALU.mult)
                            k.inc(ins, "V")
                    blk.vector(vmk)

                V_vt = k.vt_ready

                def tat(te, qb=qb, head=head, layer=layer, NR3=NR3, NRN=NRN,
                        vmask=vmask, sexp=sexp, diag=diag, S0=S0, V_vt=V_vt,
                        V_pre=V_pre):
                    te.wait_ge(k.sem["V"], max(V_vt, V_pre))

                    def score(t):
                        if t >= NRN:
                            te.wait_ge(k.sem["S"], sexp[t - NRN])
                        else:
                            te.wait_ge(k.sem["S"], S0)
                        ins = te.matmul(PB[NR3[t % NRN]][:, 0:256],
                                        KP[head * 64:(head + 1) * 64,
                                           t * 128:(t + 1) * 128],
                                        QP[head * 64:(head + 1) * 64,
                                           qb * 256:(qb + 1) * 256],
                                        start=True, stop=True)
                        k.inc(ins, "T")

                    def ctx(t):
                        if layer == 1 and t in diag:
                            te.wait_ge(k.sem["V"], vmask[t])
                        else:
                            te.wait_ge(k.sem["S"], sexp[t])
                        ins = te.matmul(
                            PB[2 + head][0:65, 0:256],
                            VT[:, t * 130 + head * 65: t * 130 + head * 65 + 65],
                            EB[:, (t % 4) * 256:(t % 4 + 1) * 256],
                            start=(t == 0), stop=(t == NKT - 1))
                        k.inc(ins, "T")
                    LAG = int(_os.environ.get("K_LAG", "3"))
                    for t in range(LAG):
                        score(t)
                    for t in range(LAG, NKT):
                        score(t)
                        ctx(t - LAG)
                    for t in range(NKT - LAG, NKT):
                        ctx(t)
                blk.tensor(tat)
                assert k.c["T"] == tctx[NKT - 1]

                def ssf(sc, qb=qb, head=head, layer=layer, NR3=NR3, NRN=NRN,
                        tsc=tsc, tctx=tctx, tbl_done=tbl_done):
                    if layer == 1:
                        sc.wait_ge(k.sem["V"], tbl_done)
                    for t in range(NKT):
                        sc.wait_ge(k.sem["T"], tsc[t])
                        if t >= 4:
                            sc.wait_ge(k.sem["T"], tctx[t - 4])
                        if layer == 1:
                            m = t - 2 * qb + 14
                            bias = TBL[:, head * 46 + m: head * 46 + m + 1]
                        else:
                            bias = k.cc(-15.0)
                        ins = sc.activation(
                            out=EB[:, (t % 4) * 256:(t % 4 + 1) * 256],
                            in_=PB[NR3[t % NRN]][:, 0:256], func=AF.Exp,
                            bias=bias, scale=0.125)
                        k.inc(ins, "S")
                blk.scalar(ssf)
                assert k.c["S"] == sexp[NKT - 1]

                def vdiv(ve, head=head, tc31=tctx[NKT - 1]):
                    ve.wait_ge(k.sem["T"], tc31)
                    ins = ve.tensor_copy(
                        DCOPY, PB[2 + head][64:65, 0:256])
                    k.inc(ins, "V")
                    k.w(ve, "V")
                    ins = ve.reciprocal(RD, DCOPY)
                    k.inc(ins, "V")
                blk.vector(vdiv)

                def trd(te, head=head):
                    k.w(te, "V")
                    ins = te.matmul(PB[4 + head][0:64, 0:256],
                                    k.ONESf[0:1, 0:64], RD,
                                    start=True, stop=True)
                    k.inc(ins, "T")
                blk.tensor(trd)

                def vev(ve, hp=hp, head=head, layer=layer):
                    k.w(ve, "T")
                    dst = CTX[0:64, :] if head == 0 else CTSTG
                    ve.tensor_copy(RDB, PB[4 + head][0:64, 0:256])
                    ins = ve.tensor_tensor(dst, PB[2 + head][0:64, 0:256],
                                           RDB, ALU.mult)
                    k.inc(ins, "V")
                    if layer == 0:
                        c = 2 * hp + head
                        ins = ve.tensor_scalar(dst, dst,
                                               k.BVH0[0:64, c:c + 1], None,
                                               ALU.add)
                        k.inc(ins, "V")
                    if head == 1:
                        ins = ve.tensor_copy(CTX[64:128, :], CTSTG)
                        k.inc(ins, "V")
                blk.vector(vev)

            def top(te, hp=hp, qb=qb):
                k.w(te, "V")
                for dch in range(8):
                    ins = te.matmul(
                        PB[3 + dch // 2][:, (dch % 2) * 256:(dch % 2) * 256 + 256],
                        WOP[:, dch * 128:(dch + 1) * 128], CTX,
                        start=(dch % 2 == 0), stop=(dch % 2 == 1))
                k.inc(ins, "T")
            blk.tensor(top)

            def vop(ve, qb=qb):
                k.w(ve, "T")
                for dch in range(8):
                    sl = slice(dch * 2048 + qb * 256, dch * 2048 + qb * 256 + 256)
                    ins = ve.tensor_tensor(
                        XT[:, sl], XT[:, sl],
                        PB[3 + dch // 2][:, (dch % 2) * 256:(dch % 2) * 256 + 256],
                        ALU.add)
                k.inc(ins, "V")
            blk.vector(vop)

    def vbo(ve, layer=layer):
        BO = k.BO0 if layer == 0 else k.BO1
        for dch in range(8):
            ins = ve.tensor_scalar(XT[:, dch * 2048:(dch + 1) * 2048],
                                   XT[:, dch * 2048:(dch + 1) * 2048],
                                   BO[:, dch:dch + 1], None, ALU.add)
        k.inc(ins, "V")
    blk.vector(vbo)



FP4_MID = [0.25, 0.75, 1.25, 1.75, 2.5, 3.5, 5.0]
FP4_D = [0.5, 0.5, 0.5, 0.5, 1.0, 1.0, 2.0]


def quant_sweep(k, blk, X, ABSX, T, Q, SCL, MJ, QF32, N, tag):
    nc = k.nc
    NB = N // 16

    def sab(sc):
        k.w(sc, "V")
        ins = sc.activation(out=ABSX, in_=X, func=AF.Abs)
        k.inc(ins, "S")
    blk.scalar(sab)

    def vsw(ve):
        k.w(ve, "S")
        va = ABSX.rearrange("p (b s) -> p b s", s=16)
        ve.tensor_reduce(SCL, va, AX.X, ALU.max)
        ve.tensor_scalar(SCL, SCL, 1.0 / 6.0, None, ALU.mult)
        ve.tensor_scalar(SCL, SCL, 1e-12, None, ALU.max)
        first = True
        for j in range(7):
            ins = ve.tensor_scalar(MJ, SCL, FP4_MID[j], None, ALU.mult)
            k.vb(ve, ins)
            mjb = MJ.rearrange("p (b u) -> p b u", u=1).to_broadcast([128, NB, 16])
            ve.tensor_tensor(T.rearrange("p (b s) -> p b s", s=16), va, mjb,
                             ALU.is_gt)
            if first:
                ve.tensor_scalar(Q, T, FP4_D[j], None, ALU.mult)
                first = False
            else:
                ve.scalar_tensor_tensor(Q, T, FP4_D[j], Q, ALU.mult, ALU.add)
        ve.tensor_scalar(T, X, 0.0, None, ALU.is_gt)
        ins = ve.tensor_scalar(T, T, 2.0, -1.0, ALU.mult, ALU.add)
        k.vb(ve, ins)
        sclb = SCL.rearrange("p (b u) -> p b u", u=1).to_broadcast([128, NB, 16])
        ve.tensor_tensor(QF32.rearrange("p (b s) -> p b s", s=16),
                         Q.rearrange("p (b s) -> p b s", s=16), sclb, ALU.mult)
        ins = ve.tensor_tensor(QF32, QF32, T, ALU.mult)
        k.inc(ins, "V")
    blk.vector(vsw)


def l1w_quant_emit(k, blk):
    nc = k.nc
    AB, AFA, PB = k.AB, k.AFA, k.PB
    SCL = AFA[:, 5632:5820]
    MJ = AFA[:, 5820:6008]

    def wtile(src_d, dst_d, r, ncols):
        X = AFA[:, 0:ncols]
        nh = 1408 if ncols > 2048 else ncols
        STG = AB[:, 0:ncols]

        def yld(sy, src_d=src_d, r=r, ncols=ncols):
            k.w(sy, "V")
            k.w(sy, "T")
            ins = sy.dma_start(out=X, in_=src_d.ap()[r * 128:(r + 1) * 128, :])
            k.inc(ins, "DY", 16)
        blk.sync(yld)

        def vw(ve):
            k.w(ve, "DY")
            k.inc(ve.nop(), "V")
        blk.vector(vw)

        for h0 in range(0, ncols, nh):
            quant_sweep(k, blk, AFA[:, h0:h0 + nh],
                        AFA[:, 2816:2816 + nh],
                        AFA[:, 4224:4224 + nh],
                        AB[:, 4096:4096 + nh],
                        SCL[:, 0:nh // 16], MJ[:, 0:nh // 16],
                        AFA[:, h0:h0 + nh],
                        nh, "wq")

        nj = ncols // 128
        for jc in range(nj):
            def tt_(te, jc=jc, ncols=ncols):
                if jc == 0:
                    k.w(te, "V")
                else:
                    k.w(te, "V")
                ins = te.transpose(PB[4 + jc % 2][:, 0:128],
                                   AFA[:, jc * 128:(jc + 1) * 128],
                                   k.IDfs[:, :])
                k.inc(ins, "T")
            blk.tensor(tt_)

            def vc_(ve, jc=jc, ncols=ncols):
                k.w(ve, "T")
                if jc == 0:
                    k.w(ve, "DY")
                ins = ve.tensor_copy(STG[:, jc * 128:(jc + 1) * 128],
                                     PB[4 + jc % 2][:, 0:128])
                k.inc(ins, "V")
            blk.vector(vc_)

        def yst(sy, dst_d=dst_d, r=r, ncols=ncols, nj=nj):
            k.w(sy, "V")
            for jc in range(nj):
                ins = sy.dma_start(
                    out=dst_d.ap()[jc * 128:(jc + 1) * 128,
                                   r * 128:(r + 1) * 128],
                    in_=STG[:, jc * 128:(jc + 1) * 128])
                k.inc(ins, "DY", 16)
        blk.sync(yst)

    for r in range(22):
        wtile(k.l1f_wg, k.wqg_t, r, 1024)
        wtile(k.l1f_wu, k.wqu_t, r, 1024)
    for r in range(8):
        wtile(k.l1f_wd, k.wqd_t, r, 2816)

    def vfin(ve):
        k.w(ve, "DY")
        k.inc(ve.nop(), "V")
    blk.vector(vfin)


def l1ffn_emit(k, blk, ci):
    nc = k.nc
    AB, AFA, PB, XT = k.AB, k.AFA, k.PB, k.XT
    if not getattr(k, "l1w_done", False):
        l1w_quant_emit(k, blk)
        k.l1w_done = True
    XQF = AB[:, 0:2048]
    HT = AB[:, 2048:7680]
    TB_ = AB[:, 7680:10496]
    QB_ = AB[:, 10496:13312]
    HQF = AB[:, 13312:18944]
    WS = [AB[:, 18944:21504], AB[:, 21504:24064]]
    ABSB = AB[:, 24064:26880]
    SCL = AFA[:, 5632:5820]
    MJ = AFA[:, 5820:6008]
    MU = AFA[:, 6008:6009]
    VS = AFA[:, 6010:6011]
    RS_ = AFA[:, 6012:6013]

    for blkI in range(8):
        t0 = blkI * 256
        for tt in range(2):
            def txp(te, t0=t0, tt=tt):
                k.w(te, "V")
                k.w(te, "S")
                for fc in range(8):
                    ins = te.transpose(
                        PB[4 + fc // 4][:, (fc % 4) * 128:(fc % 4) * 128 + 128],
                        XT[:, fc * 2048 + t0 + tt * 128:
                           fc * 2048 + t0 + tt * 128 + 128],
                        k.IDfs[:, :])
                k.inc(ins, "T")
            blk.tensor(txp)

            def vcp(ve):
                k.w(ve, "T")
                ve.tensor_copy(AFA[:, 0:512], PB[4][:, :])
                ins = ve.tensor_copy(AFA[:, 512:1024], PB[5][:, :])
                k.inc(ins, "V")
            blk.vector(vcp)

            def vln(ve):
                XTM = AFA[:, 0:1024]
                XC = AFA[:, 1024:2048]
                k.w(ve, "V")
                k.vb(ve, ve.tensor_reduce(MU, XTM, AX.X, ALU.add))
                k.vb(ve, ve.tensor_scalar(MU, MU, 1.0 / 1024.0, None,
                                          ALU.mult))
                ins = ve.tensor_scalar(XC, XTM, MU, None, ALU.subtract)
                k.inc(ins, "V")
            blk.vector(vln)

            def sln(sc):
                XC = AFA[:, 1024:2048]
                k.w(sc, "S")
                k.w(sc, "V")
                ins = sc.activation(out=AFA[:, 2048:3072], in_=XC,
                                    func=AF.Square, accum_out=VS)
                k.inc(ins, "S")
                k.w(sc, "S")
                ins = sc.activation(out=RS_, in_=VS, func=AF.Ln,
                                    bias=k.cc(1e-5), scale=1.0 / 1024.0)
                k.inc(ins, "S")
                k.w(sc, "S")
                ins = sc.activation(out=RS_, in_=RS_, func=AF.Exp,
                                    scale=k.cc(-0.5))
                k.inc(ins, "S")
            blk.scalar(sln)

            def vn2(ve):
                XTM = AFA[:, 0:1024]
                XC = AFA[:, 1024:2048]
                k.w(ve, "S")
                ins = ve.tensor_scalar(XTM, XC, RS_, None, ALU.mult)
                k.inc(ins, "V")
            blk.vector(vn2)

            quant_sweep(k, blk, AFA[:, 0:1024], AFA[:, 2048:3072],
                        AFA[:, 3072:4096], QB_[:, 0:1024],
                        SCL[:, 0:64], MJ[:, 0:64], AFA[:, 0:1024], 1024, "xq")

            def txq(te, tt=tt):
                k.w(te, "V")
                for fc in range(8):
                    ins = te.transpose(
                        PB[4 + fc // 4][:, (fc % 4) * 128:(fc % 4) * 128 + 128],
                        AFA[:, fc * 128:(fc + 1) * 128], k.IDfs[:, :])
                k.inc(ins, "T")
            blk.tensor(txq)

            def vxq(ve, tt=tt):
                k.w(ve, "T")
                for fc in range(8):
                    ins = ve.tensor_copy(
                        XQF[:, fc * 256 + tt * 128: fc * 256 + tt * 128 + 128],
                        PB[4 + fc // 4][:, (fc % 4) * 128:(fc % 4) * 128 + 128])
                k.inc(ins, "V")
            blk.vector(vxq)

        te_done = {}
        gu_done = {}
        su_done = {}
        ht_done = {}
        vht_done = {}
        for ncI in range(22):
            def ywg(sy, ncI=ncI, te_done=te_done):
                if ncI >= 2:
                    sy.wait_ge(k.sem["T"], te_done[ncI - 2])
                else:
                    k.w(sy, "V")
                b = ncI % 2
                ins = sy.dma_start(
                    out=WS[b][:, 0:1024].rearrange("p (kc m) -> p kc m", m=128),
                    in_=k.wqg_t.ap()[:, ncI * 128:(ncI + 1) * 128].rearrange(
                        "(kc p) m -> p kc m", p=128))
                k.inc(ins, "DY", 16)
                ins = sy.dma_start(
                    out=WS[b][:, 1024:2048].rearrange("p (kc m) -> p kc m", m=128),
                    in_=k.wqu_t.ap()[:, ncI * 128:(ncI + 1) * 128].rearrange(
                        "(kc p) m -> p kc m", p=128))
                k.inc(ins, "DY", 16)
            blk.sync(ywg)
            dyw = k.c["DY"]

            def tgu(te, ncI=ncI, dyw=dyw, gvw=gu_done.get(ncI - 1),
                    gsw=su_done.get(ncI - 1)):
                te.wait_ge(k.sem["DY"], dyw)
                if gvw is not None:
                    te.wait_ge(k.sem["V"], gvw)
                    te.wait_ge(k.sem["S"], gsw)
                else:
                    k.w(te, "V")
                    k.w(te, "S")
                b = ncI % 2
                for kc in range(8):
                    te.matmul(PB[0][:, 0:256], WS[b][:, kc * 128:(kc + 1) * 128],
                              XQF[:, kc * 256:(kc + 1) * 256],
                              start=(kc == 0), stop=(kc == 7))
                for kc in range(8):
                    ins = te.matmul(PB[1][:, 0:256],
                                    WS[b][:, 1024 + kc * 128:1024 + (kc + 1) * 128],
                                    XQF[:, kc * 256:(kc + 1) * 256],
                                    start=(kc == 0), stop=(kc == 7))
                k.inc(ins, "T")
            blk.tensor(tgu)

            sg_sl = slice(1024 + (ncI % 2) * 256, 1024 + (ncI % 2) * 256 + 256)
            hh_sl = slice(1536 + (ncI % 2) * 256, 1536 + (ncI % 2) * 256 + 256)

            def sgu(sc, ncI=ncI, sg_sl=sg_sl, hvw=gu_done.get(ncI - 2)):
                k.w(sc, "T")
                if hvw is not None:
                    sc.wait_ge(k.sem["V"], hvw)
                ins = sc.activation(out=AFA[:, sg_sl], in_=PB[0][:, 0:256],
                                    func=AF.Silu)
                k.inc(ins, "S")
                su_done[ncI] = k.c["S"]
            blk.scalar(sgu)

            def vh(ve, ncI=ncI, sg_sl=sg_sl, hh_sl=hh_sl,
                   htw=ht_done.get(ncI - 2)):
                k.w(ve, "S")
                if htw is not None:
                    ve.wait_ge(k.sem["T"], htw)
                ins = ve.tensor_tensor(AFA[:, hh_sl], AFA[:, sg_sl],
                                       PB[1][:, 0:256], ALU.mult)
                k.inc(ins, "V")
                gu_done[ncI] = k.c["V"]
            blk.vector(vh)

            def tht(te, ncI=ncI, hh_sl=hh_sl, te_done=te_done,
                    vhw=vht_done.get(ncI - 2)):
                te.wait_ge(k.sem["V"], gu_done[ncI])
                if vhw is not None:
                    te.wait_ge(k.sem["V"], vhw)
                for tt in range(2):
                    ins = te.transpose(
                        PB[2][:, (ncI % 2) * 256 + tt * 128:
                              (ncI % 2) * 256 + (tt + 1) * 128],
                        AFA[:, hh_sl.start + tt * 128: hh_sl.start + (tt + 1) * 128],
                        k.IDfs[:, :])
                k.inc(ins, "T")
                te_done[ncI] = k.c["T"]
                ht_done[ncI] = k.c["T"]
            blk.tensor(tht)

            def vht(ve, ncI=ncI):
                ve.wait_ge(k.sem["T"], ht_done[ncI])
                for tt in range(2):
                    ins = ve.tensor_copy(
                        HT[:, tt * 2816 + ncI * 128: tt * 2816 + (ncI + 1) * 128],
                        PB[2][:, (ncI % 2) * 256 + tt * 128:
                               (ncI % 2) * 256 + (tt + 1) * 128])
                k.inc(ins, "V")
                vht_done[ncI] = k.c["V"]
            blk.vector(vht)

        for tt in range(2):
            quant_sweep(k, blk, HT[:, tt * 2816:(tt + 1) * 2816], ABSB,
                        TB_, QB_, SCL[:, 0:176], MJ[:, 0:176],
                        AFA[:, 0:2816], 2816, "hq")
            for jc in range(22):
                def thq(te, jc=jc):
                    k.w(te, "V")
                    ins = te.transpose(PB[4 + jc % 2][:, 0:128],
                                       AFA[:, jc * 128:(jc + 1) * 128],
                                       k.IDfs[:, :])
                    k.inc(ins, "T")
                blk.tensor(thq)

                def vhq(ve, jc=jc, tt=tt):
                    k.w(ve, "T")
                    ins = ve.tensor_copy(
                        HQF[:, jc * 256 + tt * 128: jc * 256 + tt * 128 + 128],
                        PB[4 + jc % 2][:, 0:128])
                    k.inc(ins, "V")
                blk.vector(vhq)

        td_done = {}
        for kc in range(22):
            def ywd(sy, kc=kc, td_done=td_done):
                if kc >= 2:
                    sy.wait_ge(k.sem["T"], td_done[kc - 2])
                else:
                    k.w(sy, "V")
                b = kc % 2
                ins = sy.dma_start(out=WS[b][:, 0:1024],
                                   in_=k.wqd_t.ap()[kc * 128:(kc + 1) * 128, :])
                k.inc(ins, "DY", 16)
            blk.sync(ywd)
            dyw = k.c["DY"]

            def tdn(te, kc=kc, dyw=dyw, td_done=td_done):
                te.wait_ge(k.sem["DY"], dyw)
                if kc == 0:
                    k.w(te, "V")
                b = kc % 2
                for dch in range(8):
                    ins = te.matmul(
                        PB[3 + dch // 2][:, (dch % 2) * 256:(dch % 2) * 256 + 256],
                        WS[b][:, dch * 128:(dch + 1) * 128],
                        HQF[:, kc * 256:(kc + 1) * 256],
                        start=(kc == 0 and dch % 2 == 0),
                        stop=(kc == 21 and dch % 2 == 1))
                k.inc(ins, "T")
                td_done[kc] = k.c["T"]
            blk.tensor(tdn)

        def vdn(ve, t0=t0):
            k.w(ve, "T")
            for dch in range(8):
                xsl = slice(dch * 2048 + t0, dch * 2048 + t0 + 256)
                ins = ve.scalar_tensor_tensor(
                    XT[:, xsl],
                    PB[3 + dch // 2][:, (dch % 2) * 256:(dch % 2) * 256 + 256],
                    0.5, XT[:, xsl], ALU.mult, ALU.add)
            k.inc(ins, "V")
        blk.vector(vdn)


def build(core_id):
    nc = bass.Bass("TRN2", target_bir_lowering=False, debug=False)
    k = Ctx(nc)
    k.pid = nc.partition_id()

    x_in = nc.dram_tensor("x", [TC, D], F32, kind="ExternalInput").ap()
    out = nc.dram_tensor("out", [TC, D], F32, kind="ExternalOutput").ap()
    EI = "ExternalInput"
    k.wr_d = nc.dram_tensor("l0f_wr", [D, 64], F32, kind=EI)
    k.br_d = nc.dram_tensor("l0f_br", [1, 64], F32, kind=EI)
    k.wup_d = nc.dram_tensor("l0f_wup", [64, D, 256], F32, kind=EI)
    k.bup_d = nc.dram_tensor("l0f_bup", [64, 256], F32, kind=EI)
    k.wgt_d = nc.dram_tensor("l0f_wgt", [64, D, 256], F32, kind=EI)
    k.bgt_d = nc.dram_tensor("l0f_bgt", [64, 256], F32, kind=EI)
    k.wdn_d = nc.dram_tensor("l0f_wdn", [64, 256, D], F32, kind=EI)
    k.bdn_d = nc.dram_tensor("l0f_bdn", [64, D], F32, kind=EI)
    k.l0a_wq = nc.dram_tensor("l0a_wq", [D, D], F32, kind=EI)
    k.l0a_bq = nc.dram_tensor("l0a_bq", [8, 128], F32, kind=EI)
    k.l0a_wl = nc.dram_tensor("l0a_wl", [D, 256], F32, kind=EI)
    k.l0a_bl = nc.dram_tensor("l0a_bl", [2, 128], F32, kind=EI)
    k.l0a_wk = nc.dram_tensor("l0a_wk", [256, D], F32, kind=EI)
    k.l0a_bk = nc.dram_tensor("l0a_bk", [8, 128], F32, kind=EI)
    k.l0a_wv = nc.dram_tensor("l0a_wv", [256, D], F32, kind=EI)
    k.l0a_bv = nc.dram_tensor("l0a_bv", [16, 64], F32, kind=EI)
    k.l0a_wo = nc.dram_tensor("l0a_wo", [D, D], F32, kind=EI)
    k.l0a_bo = nc.dram_tensor("l0a_bo", [8, 128], F32, kind=EI)
    k.l1a_wq = nc.dram_tensor("l1a_wq", [D, D], F32, kind=EI)
    k.l1a_wk = nc.dram_tensor("l1a_wk", [D, D], F32, kind=EI)
    k.l1a_wv = nc.dram_tensor("l1a_wv", [D, D], F32, kind=EI)
    k.l1a_wo = nc.dram_tensor("l1a_wo", [D, D], F32, kind=EI)
    k.l1a_bo = nc.dram_tensor("l1a_bo", [8, 128], F32, kind=EI)
    k.cinfo = nc.dram_tensor("cinfo", [1, 1], F32, kind=EI)
    k.l1f_wg = nc.dram_tensor("l1f_wg", [2816, D], F32, kind=EI)
    k.l1f_wu = nc.dram_tensor("l1f_wu", [2816, D], F32, kind=EI)
    k.l1f_wd = nc.dram_tensor("l1f_wd", [D, 2816], F32, kind=EI)
    k.wqg_t = nc.dram_tensor("wqg_t", [D, 2816], BF16)
    k.wqu_t = nc.dram_tensor("wqu_t", [D, 2816], BF16)
    k.wqd_t = nc.dram_tensor("wqd_t", [2816, D], BF16)
    k.lat_in = nc.dram_tensor("lat_in", [128, 4096], BF16)
    k.lat_out = nc.dram_tensor("lat_out", [256, 4096], BF16)
    k.xn_in = nc.dram_tensor("xn_in", [128, NCH * TC], BF16)
    k.xn_out = nc.dram_tensor("xn_out", [256, NCH * TC], BF16)
    k.sc_in = nc.dram_tensor("sc_in", [64, TC], F32)
    k.sc_my = nc.dram_tensor("sc_my", [128, 1024], F32)
    k.wup_b = nc.dram_tensor("wup_b", [64, 128, 2048], BF16)
    k.wgt_b = nc.dram_tensor("wgt_b", [64, 128, 2048], BF16)
    k.wdn_b = nc.dram_tensor("wdn_b", [64, 128, 2048], BF16)
    k.sc_out = nc.dram_tensor("sc_out", [512, TC], F32)
    k.gg_in = nc.dram_tensor("gg_in", [8, 512], F32)
    k.gg_out = nc.dram_tensor("gg_out", [64, 512], F32)
    if _DBG:
        k.dbg_d = nc.dram_tensor("dbg", [128, 8192], F32, kind="ExternalOutput")
        k.dbg2_d = nc.dram_tensor("dbg2", [128, NCH * TC], mybir.dt.bfloat16,
                                  kind="ExternalOutput")

    k.IDbf = nc.inline_tensor(np.eye(128, dtype=ml_dtypes.bfloat16), "idbf")
    k.IDf = nc.inline_tensor(np.eye(128, dtype=np.float32), "idf")
    onesf = np.ones((128, 128), dtype=np.float32)
    k.ONESf_d = nc.inline_tensor(onesf, "onesf")
    k.ONESbf_d = nc.inline_tensor(onesf.astype(ml_dtypes.bfloat16), "onesbf")
    conarr = np.tile(np.asarray(CONST_VALS, np.float32)[None, :], (128, 1))
    k.CON_d = nc.inline_tensor(np.ascontiguousarray(conarr), "conal")

    XT = nc.alloc_sbuf_tensor("XT", [128, NCH * TC], F32)
    XN = nc.alloc_sbuf_tensor("XN", [128, NCH * TC], BF16)
    AB = nc.alloc_sbuf_tensor("AB", [128, 28672], BF16)
    AFA = nc.alloc_sbuf_tensor("AFA", [128, 6144], F32)
    k.XT, k.XN, k.AB, k.AFA = XT, XN, AB, AFA
    k.REP = AB[0:64, 15872:24064]
    IDbf = nc.alloc_sbuf_tensor("IDbf_s", [128, 128], BF16)
    IDf = nc.alloc_sbuf_tensor("IDf_s", [128, 128], F32)
    ONESf = nc.alloc_sbuf_tensor("ONESf_s", [128, 128], F32)
    ONESbf = nc.alloc_sbuf_tensor("ONESbf_s", [128, 128], BF16)
    k.ONESf, k.ONESbf, k.IDfs, k.IDbfs = ONESf, ONESbf, IDf, IDbf
    CONS = nc.alloc_sbuf_tensor("CONS", [128, len(CONST_VALS)], F32)
    k.CONS = CONS
    k.cc = lambda v: CONS[:, CONST_VALS.index(v):CONST_VALS.index(v) + 1]
    k.WR = nc.alloc_sbuf_tensor("WRs", [128, 512], BF16)
    k.BRB = nc.alloc_sbuf_tensor("BRBs", [128, 64], F32)
    k.BR1 = nc.alloc_sbuf_tensor("BR1s", [1, 64], F32)
    k.BUP = nc.alloc_sbuf_tensor("BUPs", [128, 128], F32)
    k.BGT = nc.alloc_sbuf_tensor("BGTs", [128, 128], F32)
    k.BUPR = nc.alloc_sbuf_tensor("BUPRs", [64, 256], F32)
    k.BGTR = nc.alloc_sbuf_tensor("BGTRs", [64, 256], F32)
    k.BDN = nc.alloc_sbuf_tensor("BDNs", [64, 1024], BF16)
    k.GSseg = nc.alloc_sbuf_tensor("GSsegs", [128, 16], F32)
    k.GMseg = nc.alloc_sbuf_tensor("GMsegs", [128, 16], F32)
    k.GSA = nc.alloc_sbuf_tensor("GSAs", [64, 256], F32)
    k.GMA = nc.alloc_sbuf_tensor("GMAs", [64, 256], F32)
    k.RS = nc.alloc_sbuf_tensor("RSs", [64, 20], F32)
    k.THB = nc.alloc_sbuf_tensor("THBs", [128, 64], F32)
    k.FSTT = nc.alloc_sbuf_tensor("FSTTs", [64, 2048], BF16)
    NRND = 44
    ckl = np.zeros((64, NRND), np.float32)
    lr = math.log(64.0) - math.log(1e-38)
    for r in range(NRND):
        ckl[:, r] = math.exp(lr * (2.0 ** -(r + 1)))
    k.CKL_d = nc.inline_tensor(np.ascontiguousarray(ckl), "ckl")
    k.CKL = nc.alloc_sbuf_tensor("CKLs", [64, NRND], F32)
    k.EIDX_d = nc.inline_tensor(
        np.ascontiguousarray(np.arange(64, dtype=np.float32)[:, None]), "eidx")
    k.EIDX = nc.alloc_sbuf_tensor("EIDXs", [64, 1], F32)
    k.IOTA_d = nc.inline_tensor(np.ascontiguousarray(
        np.tile(np.arange(256, dtype=np.float32)[None, :], (64, 1))), "iota256")
    k.IOTA256 = nc.alloc_sbuf_tensor("IOTAs", [64, 256], F32)
    k.EYE64_d = nc.inline_tensor(np.eye(64, dtype=ml_dtypes.bfloat16), "eye64")
    k.EYE64 = nc.alloc_sbuf_tensor("EYE64s", [64, 64], BF16)
    rep = np.repeat(np.eye(64, dtype=ml_dtypes.bfloat16), 128, axis=1)
    k.REP_d = nc.inline_tensor(np.ascontiguousarray(rep), "rep64")
    k.PIO_d = nc.inline_tensor(
        np.ascontiguousarray(np.arange(128, dtype=np.float32)[:, None]), "pio")
    k.DIF_d = nc.inline_tensor(np.ascontiguousarray(
        (np.arange(128, dtype=np.float32)[:, None]
         - np.arange(256, dtype=np.float32)[None, :])), "dif")
    k.PIOTA = nc.alloc_sbuf_tensor("PIOTAs", [128, 1], F32)
    k.DIF = nc.alloc_sbuf_tensor("DIFs", [128, 256], F32)
    k.BQ0 = nc.alloc_sbuf_tensor("BQ0s", [128, 8], F32)
    k.BK0 = nc.alloc_sbuf_tensor("BK0s", [128, 8], F32)
    k.BVH0 = nc.alloc_sbuf_tensor("BVH0s", [64, 16], F32)
    k.BO0 = nc.alloc_sbuf_tensor("BO0s", [128, 8], F32)
    k.BL0 = nc.alloc_sbuf_tensor("BL0s", [128, 2], F32)
    k.BO1 = nc.alloc_sbuf_tensor("BO1s", [128, 8], F32)
    k.HFS = nc.alloc_sbuf_tensor("HFSs", [1, 1], F32)
    k.HFB = nc.alloc_sbuf_tensor("HFBs", [128, 1], F32)
    k.HFROW = nc.alloc_sbuf_tensor("HFROWs", [1, 128], F32)
    k.M0 = nc.alloc_sbuf_tensor("M0s", [128, 256], F32)
    k.M1 = nc.alloc_sbuf_tensor("M1s", [128, 256], F32)
    k.M2 = nc.alloc_sbuf_tensor("M2s", [128, 256], F32)
    k.M3 = nc.alloc_sbuf_tensor("M3s", [128, 256], F32)
    k.MC0 = nc.alloc_sbuf_tensor("MC0s", [128, 1], F32)
    arows = np.zeros((1, 8 * 92), np.float32)
    brows = np.zeros((1, 8 * 92), np.float32)
    for hp_ in range(8):
        for hd_ in range(2):
            sl8_ = 0.7071067811865476 ** (2 * hp_ + hd_ + 1) / 8.0
            for mi in range(46):
                m = mi - 14
                pen0 = 10000.0 if m >= 2 else 0.0
                pen1 = 10000.0 if m >= 18 else 0.0
                arows[0, hp_ * 92 + hd_ * 46 + mi] = sl8_ * 128 * m - 25.0 - pen0
                brows[0, hp_ * 92 + hd_ * 46 + mi] = pen0 - pen1 - 2048.0 * sl8_
    k.AROWS_d = nc.inline_tensor(np.ascontiguousarray(arows), "arows")
    k.BROWS_d = nc.inline_tensor(np.ascontiguousarray(brows), "brows")
    k.AROWS = nc.alloc_sbuf_tensor("AROWSs", [1, 8 * 92], F32)
    k.BROWS = nc.alloc_sbuf_tensor("BROWSs", [1, 8 * 92], F32)

    PB = [nc.alloc_psum_tensor(f"pb{i}", [128, 512], F32) for i in range(7)]
    PH = nc.alloc_psum_tensor("ph0", [128, 1024], BF16)
    k.PB, k.PH = PB, PH

    with nc.Block() as blk:
        @blk.sync
        def _(sy):
            ins = sy.dma_start(out=IDf[:, :], in_=k.IDf.ap())
            k.inc(ins, "DY", 16)
            ins = sy.dma_start(out=IDbf[:, :], in_=k.IDbf.ap())
            k.inc(ins, "DY", 16)
            ins = sy.dma_start(out=ONESf[:, :], in_=k.ONESf_d.ap())
            k.inc(ins, "DY", 16)
            ins = sy.dma_start(out=ONESbf[:, :], in_=k.ONESbf_d.ap())
            k.inc(ins, "DY", 16)
            ins = sy.dma_start(out=CONS[:, :], in_=k.CON_d.ap())
            k.inc(ins, "DY", 16)
            ins = sy.dma_start(out=k.CKL[:, :], in_=k.CKL_d.ap())
            k.inc(ins, "DY", 16)
            ins = sy.dma_start(out=k.EIDX[:, :], in_=k.EIDX_d.ap())
            k.inc(ins, "DY", 16)
            ins = sy.dma_start(out=k.IOTA256[:, :], in_=k.IOTA_d.ap())
            k.inc(ins, "DY", 16)
            ins = sy.dma_start(out=k.EYE64[:, :], in_=k.EYE64_d.ap())
            k.inc(ins, "DY", 16)
            ins = sy.dma_start(out=k.BR1[:, :], in_=k.br_d.ap())
            k.inc(ins, "DY", 16)
            ins = sy.dma_start(out=k.BUPR[:, :], in_=k.bup_d.ap())
            k.inc(ins, "DY", 16)
            ins = sy.dma_start(out=k.BGTR[:, :], in_=k.bgt_d.ap())
            k.inc(ins, "DY", 16)
            ins = sy.dma_start(out=k.PIOTA[:, :], in_=k.PIO_d.ap())
            k.inc(ins, "DY", 16)
            ins = sy.dma_start(out=k.DIF[:, :], in_=k.DIF_d.ap())
            k.inc(ins, "DY", 16)
            for dst, srcd in ((k.BQ0, k.l0a_bq), (k.BK0, k.l0a_bk),
                              (k.BO0, k.l0a_bo), (k.BO1, k.l1a_bo),
                              (k.BVH0, k.l0a_bv), (k.BL0, k.l0a_bl)):
                ins = sy.dma_start(out=dst[:, :],
                                   in_=srcd.ap().rearrange("c p -> p c"),
                                   allow_slow_non_contiguous=True)
                k.inc(ins, "DY", 16)
            ins = sy.dma_start(out=k.HFS[:, :], in_=k.cinfo.ap())
            k.inc(ins, "DY", 16)
            ins = sy.dma_start(out=k.AROWS[:, :], in_=k.AROWS_d.ap())
            k.inc(ins, "DY", 16)
            ins = sy.dma_start(out=k.BROWS[:, :], in_=k.BROWS_d.ap())
            k.inc(ins, "DY", 16)

        @blk.gpsimd
        def _(gp):
            ins = gp.dma_start(out=k.WR[:, :].rearrange("p (kc e) -> p kc e", e=64),
                               in_=k.wr_d.ap().rearrange("(kc p) e -> p kc e", p=128))
            k.inc(ins, "DG", 16)
            ins = gp.dma_start(out=k.BDN[:, :], in_=k.bdn_d.ap())
            k.inc(ins, "DG", 16)

        @blk.tensor
        def _(te):
            k.w(te, "DY")
            te.matmul(PB[5][:, 0:64], ONESf[0:1, :], k.BR1[:, :],
                      start=True, stop=True)
            te.matmul(PB[3][:, 0:1], ONESf[0:1, :], k.HFS[:, :],
                      start=True, stop=True)
            te.transpose(PB[6][:, 0:64], k.BUPR[:, 0:128], IDf[0:64, 0:64])
            te.transpose(PB[6][:, 64:128], k.BUPR[:, 128:256], IDf[0:64, 0:64])
            te.transpose(PB[4][:, 0:64], k.BGTR[:, 0:128], IDf[0:64, 0:64])
            ins = te.transpose(PB[4][:, 64:128], k.BGTR[:, 128:256], IDf[0:64, 0:64])
            k.inc(ins, "T")

        @blk.vector
        def _(ve):
            k.w(ve, "T")
            ve.tensor_copy(k.HFB[:, :], PB[3][:, 0:1])
            ve.tensor_copy(k.BRB[:, :], PB[5][:, 0:64])
            ve.tensor_copy(k.BUP[:, :], PB[6][:, 0:128])
            ins = ve.tensor_copy(k.BGT[:, :], PB[4][:, 0:128])
            k.inc(ins, "V")
        for tt in range(16):
            @blk.sync
            def _(sy, tt=tt):
                k.w(sy, "V")
                ins = sy.dma_start(out=AFA[:, 0:1024],
                                   in_=x_in[tt * 128:(tt + 1) * 128, :])
                k.inc(ins, "DY", 16)

            for half in range(2):
                @blk.tensor
                def _(te, tt=tt, half=half):
                    if half == 0:
                        k.w(te, "DY")
                    else:
                        k.w(te, "V")
                    for q in range(4):
                        fc = half * 4 + q
                        ins = te.transpose(PB[q][:, 0:128],
                                           AFA[:, fc * 128:(fc + 1) * 128],
                                           IDf[:, :])
                    k.inc(ins, "T")

                @blk.vector
                def _(ve, tt=tt, half=half):
                    k.w(ve, "T")
                    for q in range(4):
                        fc = half * 4 + q
                        ins = ve.tensor_copy(
                            XT[:, fc * 2048 + tt * 128: fc * 2048 + (tt + 1) * 128],
                            PB[q][:, 0:128])
                    k.inc(ins, "V")

        subs = [s for s in (_os.environ.get("K_SUBS") or
                            "f0,a0,f0,F1,A1,F1").split(",") if s]
        if _STOP == "load":
            subs = []
        nf0 = 0
        if any(s == "f0" for s in subs):
            precast_emit(k, blk)
        for s in subs:
            if s == "f0":
                l0ffn_emit(k, blk, nf0)
                nf0 += 1
            elif s == "a0":
                attn_emit(k, blk, 0)
            elif s == "A1":
                attn_emit(k, blk, 1)
            elif s == "F1":
                l1ffn_emit(k, blk, 0)
            else:
                raise ValueError(f"unknown sublayer {s}")
        if _DBG:
            def dbgf(sy):
                k.w(sy, "V")
                k.w(sy, "T")
                k.w(sy, "S")
                dd = k.dbg_d.ap()
                ins = sy.dma_start(out=dd[0:64, 0:256], in_=k.GSA[:, :])
                k.inc(ins, "DY", 16)
                ins = sy.dma_start(out=dd[0:64, 256:512], in_=k.GMA[:, :])
                k.inc(ins, "DY", 16)
                ins = sy.dma_start(out=dd[0:64, 512:532], in_=k.RS[:, :])
                k.inc(ins, "DY", 16)
                ins = sy.dma_start(out=dd[0:128, 1024:2048], in_=AFA[:, 3072:4096])
                k.inc(ins, "DY", 16)
                ins = sy.dma_start(out=dd[0:128, 2048:3072], in_=AFA[:, 2048:3072])
                k.inc(ins, "DY", 16)
                ins = sy.dma_start(out=dd[0:128, 3072:3136], in_=k.THB[:, :])
                k.inc(ins, "DY", 16)
                ins = sy.dma_start(out=k.dbg2_d.ap(), in_=k.AB[:, 0:16384])
                k.inc(ins, "DY", 16)
                ins = sy.dma_start(out=dd[0:128, 4096:4227], in_=AFA[:, 5120:5251])
                k.inc(ins, "DY", 16)
            blk.sync(dbgf)

        for tt in range(16):
            for half in range(2):
                @blk.tensor
                def _(te, tt=tt, half=half):
                    k.w(te, "V")
                    for q in range(4):
                        fc = half * 4 + q
                        ins = te.transpose(PB[q][:, 0:128],
                                           XT[:, fc * 2048 + tt * 128: fc * 2048 + (tt + 1) * 128],
                                           IDf[:, :])
                    k.inc(ins, "T")

                @blk.vector
                def _(ve, tt=tt, half=half):
                    k.w(ve, "T")
                    k.w(ve, "DY")
                    for q in range(4):
                        fc = half * 4 + q
                        ins = ve.tensor_copy(AFA[:, fc * 128:(fc + 1) * 128],
                                             PB[q][:, 0:128])
                    k.inc(ins, "V")

            @blk.sync
            def _(sy, tt=tt):
                k.w(sy, "V")
                ins = sy.dma_start(out=out[tt * 128:(tt + 1) * 128, :],
                                   in_=AFA[:, 0:1024])
                k.inc(ins, "DY", 16)

        @blk.sync
        def _(sy):
            k.w(sy, "DY")

    return nc


def _f32c(a):
    return np.ascontiguousarray(np.asarray(a, dtype=np.float32))


def kernel(**inputs):
    x = np.asarray(inputs["L0_attn_x"], dtype=np.float32)
    x_shards = np.ascontiguousarray(x.reshape(NCORE, TC, D))

    def gi(name, shape):
        a = inputs.get(name)
        if a is None:
            return np.zeros(shape, np.float32)
        return _f32c(a).reshape(shape)

    nc = build(0)
    shared = {
        "l0f_wr": gi("L0_ffn_Wr", (D, 64)),
        "l0f_br": gi("L0_ffn_br", (1, 64)),
        "l0f_wup": gi("L0_ffn_Wup", (64, D, 256)),
        "l0f_bup": gi("L0_ffn_bup", (64, 256)),
        "l0f_wgt": gi("L0_ffn_Wgate", (64, D, 256)),
        "l0f_bgt": gi("L0_ffn_bgate", (64, 256)),
        "l0f_wdn": gi("L0_ffn_Wdown", (64, 256, D)),
        "l0f_bdn": gi("L0_ffn_bdown", (64, D)),
        "l0a_wq": gi("L0_attn_Wq", (D, D)),
        "l0a_bq": gi("L0_attn_bq", (8, 128)),
        "l0a_wl": gi("L0_attn_Wl", (D, 256)),
        "l0a_bl": gi("L0_attn_bl", (2, 128)),
        "l0a_wk": gi("L0_attn_Wk", (256, D)),
        "l0a_bk": gi("L0_attn_bk", (8, 128)),
        "l0a_wv": gi("L0_attn_Wv", (256, D)),
        "l0a_bv": gi("L0_attn_bv", (16, 64)),
        "l0a_wo": gi("L0_attn_Wo", (D, D)),
        "l0a_bo": gi("L0_attn_bo", (8, 128)),
        "l1a_wq": gi("L1_attn_Wq", (D, D)),
        "l1a_wk": gi("L1_attn_Wk", (D, D)),
        "l1a_wv": gi("L1_attn_Wv", (D, D)),
        "l1a_wo": gi("L1_attn_Wo", (D, D)),
        "l1a_bo": gi("L1_attn_bo", (8, 128)),
        "l1f_wg": gi("L1_ffn_w_gate", (2816, D)),
        "l1f_wu": gi("L1_ffn_w_up", (2816, D)),
        "l1f_wd": gi("L1_ffn_w_down", (D, 2816)),
    }
    in_maps = []
    for c in range(NCORE):
        m = dict(shared)
        m["x"] = x_shards[c]
        m["cinfo"] = np.asarray([[float(c % 2)]], np.float32)
        in_maps.append(m)
    res = run_bass_kernel_spmd(nc, in_maps, core_ids=list(range(NCORE)))
    out = np.concatenate([r["out"] for r in res.results], axis=0)
    if _DBG:
        kernel._dbg = [r["dbg"] for r in res.results]
        kernel._dbg2 = [r["dbg2"] for r in res.results]
    return out.reshape(4, 4096, 1024)


if __name__ == "__main__":
    rng = np.random.default_rng(0)
    x = rng.standard_normal((4, 4096, 1024)).astype(np.float32)
    y = kernel(L0_attn_x=x)
    print("identity check:", np.abs(y - x).max())
